# version 18
# baseline (speedup 1.0000x reference)
import numpy as np
import ml_dtypes
from contextlib import ExitStack

import concourse.bass as bass
import concourse.mybir as mybir
from concourse.alu_op_type import AluOpType as OP
from concourse.bass_utils import run_bass_kernel_spmd

F32 = mybir.dt.float32
BF16 = mybir.dt.bfloat16
AF = mybir.ActivationFunctionType
AX = mybir.AxisListType
BF = ml_dtypes.bfloat16

T = 4096
D = 1024
NT = T // 128
NW = T // 512
NC = D // 128

_DBG = {"upto": None, "dump": False}

ENGINES = ("pe", "act", "ve", "sync", "gp")
_ENG_ATTR = {"pe": "tensor", "act": "scalar", "ve": "vector", "sync": "sync", "gp": "gpsimd"}


class Bld:

    def __init__(self, nc):
        self.nc = nc
        self.q = {e: [] for e in ENGINES}
        self.ctx = ExitStack()
        self.sem = {}
        names = ["pe", "act", "ve", "gp"]
        names += [f"ld{i}" for i in range(8)] + [f"st{i}" for i in range(4)]
        for name in names:
            self.sem[name] = self.ctx.enter_context(nc.semaphore("sem_" + name))
        self.cnt = {k: 0 for k in self.sem}
        self.maxwait = {k: 0 for k in self.sem}

    AUTOSYNC = ("act", "ve", "gp")

    def emit(self, eng, fn, waits=(), inc=None, n=1, dma=False):
        waits = list(waits)
        if eng in self.AUTOSYNC and dma:
            waits.append((eng, self.cnt[eng]))
        elif eng in self.AUTOSYNC:
            sname = eng
            waits.append((sname, self.cnt[sname]))
            assert inc in (None, sname)
            inc, n = sname, 1
        elif eng == "pe" and inc == "pe":
            waits.append(("pe", self.cnt["pe"]))
        for s, v in waits:
            self.maxwait[s] = max(self.maxwait[s], v)
        if inc is not None:
            self.cnt[inc] += n
            target = self.cnt[inc]
        else:
            target = None
        incsem = self.sem[inc] if inc is not None else None
        w = [(self.sem[s], v) for s, v in waits]

        def run(e):
            for sh, v in w:
                e.wait_ge(sh, v)
            r = fn(e)
            if incsem is not None:
                r.then_inc(incsem, n)

        self.q[eng].append(run)
        return target

    def pe(self, fn, waits=(), inc=None):
        return self.emit("pe", fn, waits, inc)

    def act(self, fn, waits=(), inc=None):
        return self.emit("act", fn, waits, inc)

    def ve(self, fn, waits=(), inc=None):
        return self.emit("ve", fn, waits, inc)

    def ld(self, fn, waits=(), slot=0):
        s = f"ld{slot % 8}"
        return s, self.emit("sync", fn, waits, s, 16)

    def st(self, fn, waits=(), slot=0):
        s = f"st{slot % 4}"
        return s, self.emit("gp", fn, waits, s, 16, dma=True)

    def gp(self, fn, waits=(), inc=None):
        return self.emit("gp", fn, waits, inc if inc else None)

    def barrier(self):
        snap = {}
        for e, s in (("pe", "pe"), ("act", "act"), ("ve", "ve"), ("gp", "gp")):
            snap[s] = self.emit(e, lambda en: en.nop(), inc=s if e == "pe" else None)
        for i in range(8):
            snap[f"ld{i}"] = self.cnt[f"ld{i}"]
        for i in range(4):
            snap[f"st{i}"] = self.cnt[f"st{i}"]
        for e in ("pe", "act", "ve", "gp", "sync"):
            waits = [(s, snap[s]) for s in snap]

            def mk(waits=waits):
                def run(en):
                    for sname, v in waits:
                        self.maxwait[sname] = max(self.maxwait[sname], v)
                        en.wait_ge(self.sem[sname], v)
                return run
            self.q[e].append(mk())

    def replay(self, blk):
        for e in ENGINES:
            lst = self.q[e]
            if not lst:
                continue
            getattr(blk, _ENG_ATTR[e])(lambda eng, lst=lst: [f(eng) for f in lst])

    def validate_counts(self):
        for s in self.sem:
            assert self.maxwait[s] <= self.cnt[s], (
                f"wait on {s} for {self.maxwait[s]} but final count {self.cnt[s]}")

    def sb(self, name, shape, dt, es=None, side=None):
        kw = {"side": side} if side else {}
        return (es or self.ctx).enter_context(self.nc.sbuf_tensor(name, shape, dt, **kw))

    def ps(self, name, shape, dt, es=None):
        return (es or self.ctx).enter_context(self.nc.psum_tensor(name, shape, dt))

    def scope(self):
        b = self

        class _S(ExitStack):
            def __exit__(self, *a):
                r = ExitStack.__exit__(self, *a)
                b.barrier()
                return r
        return _S()

    def fconst(self, value):
        key = ("fc", value)
        if not hasattr(self, "_fc"):
            self._fc = {}
        if key not in self._fc:
            t = self.sb(f"fc_{len(self._fc)}", [128, 1], F32)
            self.ve(lambda e: e.memset(t[:], float(value)), inc="ve")
            self._fc[key] = t
        return self._fc[key]



def emit_ln_xnT(b, consts, x_dram, stats_sb, xnT, ext=False, stats_dram=None,
                xnT_all=None):
    nc = b.nc
    off = 1 if ext else 0
    es = b.scope()
    tag = x_dram.tensor.name
    xs = [b.sb(f"ln_x{i}_{tag}", [128, D], F32, es) for i in range(4)]
    st6 = b.sb(f"ln_st6_{tag}", [128, 12], F32, es)
    mvs = [b.sb(f"ln_mv{i}_{tag}", [128, 2], F32, es) for i in range(2)]
    lnv = b.sb(f"ln_lnv_{tag}", [128, 1], F32, es)
    xnb = [b.sb(f"ln_xnb{i}_{tag}", [128, D], BF16, es) for i in range(2)]
    ptr = [b.ps(f"ln_pt{i}_{tag}", [128, 1024], BF16, es) for i in range(2)]
    ident = consts["ident"]

    if ext:
        for c in range(NC):
            b.ve(lambda e, c=c: e.memset(xnT[c][:, 0:1], 0.0), inc="ve")

    pe_done = b.cnt["pe"]
    evac_cnt = [b.cnt["ve"], b.cnt["ve"]]
    scal_hist = [b.cnt["ve"]] * 4
    eps5 = consts["eps5"]
    ars = {}
    for tl in range(NT + 1):
        if tl < NT:
            t, buf, nb = tl, tl % 4, tl % 2
            ld_t = b.ld(lambda e, t=t, buf=buf: e.dma_start(
                out=xs[buf][:], in_=x_dram[128 * t:128 * t + 128, :]),
                waits=[("ve", scal_hist[buf])] if t >= 4 else (), slot=buf)
            s1 = b.ve(lambda e, buf=buf: e.bn_stats(st6[:, 0:6], xs[buf][:, 0:512]),
                      waits=[ld_t], inc="ve")
            s2 = b.ve(lambda e, buf=buf: e.bn_stats(st6[:, 6:12], xs[buf][:, 512:1024]),
                      waits=[("ve", s1)], inc="ve")
            ve_mv = b.ve(lambda e, nb=nb: e.bn_aggr(mvs[nb][:], st6[:]),
                         waits=[("ve", s2)], inc="ve")
            b.ve(lambda e, t=t, nb=nb: e.tensor_copy(stats_sb[:, 2 * t:2 * t + 1],
                                                     mvs[nb][:, 0:1]),
                 waits=[("ve", ve_mv)])
            b.act(lambda e, nb=nb: e.activation(lnv[:], mvs[nb][:, 1:2], AF.Ln,
                                                bias=eps5[:], scale=1.0),
                  waits=[("ve", ve_mv)])
            ars[t] = b.act(lambda e, t=t: e.activation(
                stats_sb[:, 2 * t + 1:2 * t + 2], lnv[:], AF.Exp, bias=0.0, scale=-0.5),
                inc="act")
        if tl >= 1:
            t, buf, nb = tl - 1, (tl - 1) % 4, (tl - 1) % 2
            ve_done = b.ve(lambda e, t=t, buf=buf, nb=nb: e.tensor_scalar(
                xnb[nb][:], xs[buf][:], stats_sb[:, 2 * t:2 * t + 1],
                stats_sb[:, 2 * t + 1:2 * t + 2], OP.subtract, OP.mult),
                waits=[("act", ars.pop(t)), ("pe", pe_done)], inc="ve")
            scal_hist[buf] = ve_done
            pb = t % 2
            tp = None
            for c in range(NC):
                w2 = [("ve", ve_done), ("ve", evac_cnt[pb])] if c == 0 else ()
                tp = b.pe(lambda e, c=c, nb=nb, pb=pb: e.transpose(
                    ptr[pb][:, 128 * c:128 * c + 128],
                    xnb[nb][:, 128 * c:128 * c + 128], ident[:]),
                    waits=w2, inc="pe" if c == NC - 1 else None)
            if xnT_all is not None:
                ov = xnT_all.rearrange("p (c t) -> p c t", c=NC)[
                    :, :, off + 128 * t: off + 128 * t + 128]
                evac_cnt[pb] = b.ve(lambda e, ov=ov, pb=pb: e.tensor_copy(
                    ov, ptr[pb][:].rearrange("p (c t) -> p c t", c=NC)),
                    waits=[("pe", tp)])
            else:
                for c in range(NC):
                    evac_cnt[pb] = b.ve(lambda e, c=c, t=t, pb=pb: e.tensor_copy(
                        xnT[c][:, off + 128 * t: off + 128 * t + 128],
                        ptr[pb][:, 128 * c:128 * c + 128]),
                        waits=[("pe", tp)] if c == 0 else ())
            pe_done = tp
    if stats_dram is not None:
        b.st(lambda e: e.dma_start(out=stats_dram[:], in_=stats_sb[:]),
             waits=[("act", b.cnt["act"]), ("ve", b.cnt["ve"])], slot=3)
    es.close()



def _cast_load(b, es, name, dram_ap, rows, cols, stage, slot=6, scale=None,
               stage2=None, slot2=7, stages=None, slots=None):
    n = rows // 128
    out = b.sb(name + "_b", [128, n * cols], BF16, es)
    if stages is None:
        stages = [stage, stage2] if stage2 is not None else [stage]
        slots = [slot, slot2] if stage2 is not None else [slot]
    castc = [b.cnt["ve"]] * len(stages)
    for c in range(n):
        sb_ = c % len(stages)
        stg = stages[sb_]
        ld = b.ld(lambda e, c=c, stg=stg: e.dma_start(
            out=stg[:, 0:cols], in_=dram_ap[128 * c:128 * c + 128, :]),
            waits=[("ve", castc[sb_])], slot=slots[sb_])
        if scale is None:
            castc[sb_] = b.ve(lambda e, c=c, stg=stg: e.tensor_copy(
                out[:, cols * c:cols * c + cols], stg[:, 0:cols]),
                waits=[ld], inc="ve")
        else:
            castc[sb_] = b.ve(lambda e, c=c, stg=stg: e.tensor_scalar(
                out[:, cols * c:cols * c + cols], stg[:, 0:cols], scale, None,
                OP.mult),
                waits=[ld], inc="ve")
    return out


def _build_shift_bias(b, es, consts, Qm2, Km2, nrm_dram, cb, tag, extra_scale=1.0):
    nq = Qm2.shape[1]
    nk = Km2.shape[1]
    v = nrm_dram.rearrange("a (p c) -> (a p) c", p=128)
    st_q = b.st(lambda e: e.dma_start(out=v[:, 0:nq], in_=Qm2[:]),
                waits=[("ve", b.cnt["ve"])], slot=2)
    st_k = b.st(lambda e: e.dma_start(out=v[:, 16:16 + nk], in_=Km2[:]),
                waits=[("ve", b.cnt["ve"])], slot=1)
    flat = b.sb("nrmflat_" + tag, [1, 4096], F32, es)
    ld = b.ld(lambda e: e.dma_start(out=flat[:], in_=nrm_dram[:]),
              waits=[st_q, st_k], slot=6)
    mq = b.sb("nrm_mq_" + tag, [1, 4], F32, es)
    mrow = b.sb("nrm_mrow_" + tag, [1, 256], F32, es)
    fv = flat[:].rearrange("a (p c) -> a p c", c=32)
    b.ve(lambda e: e.reduce_max(mrow[:, 0:128], fv[:, :, 0:nq], axis=AX.X), waits=[ld])
    b.ve(lambda e: e.reduce_max(mq[:, 0:1], mrow[:, 0:128], axis=AX.X))
    b.ve(lambda e: e.reduce_max(mrow[:, 128:256], fv[:, :, 16:16 + nk], axis=AX.X))
    b.ve(lambda e: e.reduce_max(mq[:, 1:2], mrow[:, 128:256], axis=AX.X))
    ve0 = b.ve(lambda e: e.tensor_tensor(mq[:, 2:3], mq[:, 0:1], mq[:, 1:2], OP.mult),
               inc="ve")
    b.act(lambda e: e.activation(mq[:, 3:4], mq[:, 2:3], AF.Ln, bias=0.0, scale=1.0),
          waits=[("ve", ve0)])
    a1 = b.act(lambda e: e.activation(mq[:, 2:3], mq[:, 3:4], AF.Exp, bias=0.0, scale=0.5),
               inc="act")
    ve1 = b.ve(lambda e: e.tensor_scalar(mq[:, 3:4], mq[:, 2:3],
                                         -1.01 * extra_scale, None, OP.mult),
               waits=[("act", a1)], inc="ve")
    pbias = b.ps("pb_" + tag, [128, 8], F32, es)
    p0 = b.pe(lambda e: e.matmul(pbias[:, 0:1], consts["onesrow_f"][:], mq[:, 3:4],
                                 start=True, stop=True), waits=[("ve", ve1)], inc="pe")
    b.ve(lambda e: e.tensor_copy(cb[:], pbias[:, 0:1]), waits=[("pe", p0)], inc="ve")


def emit_l0a(b, consts, caps, xnT, es_xnT, a0_dram, nrm_dram, dbg=None):
    nc = b.nc
    ident = consts["ident"]
    H = 16

    es_main = b.scope()
    es_qkv = b.scope()
    esQ = b.scope()
    qTs_all = b.sb("l0a_qT_all", [128, 4 * T], BF16, es_qkv)
    kTs_all = b.sb("l0a_kT_all", [128, 4 * T], BF16, es_qkv)
    qTs = [qTs_all[:, j * T:(j + 1) * T] for j in range(4)]
    kTs = [kTs_all[:, j * T:(j + 1) * T] for j in range(4)]
    V = b.sb("l0a_V", [128, NT * 528], BF16, es_qkv)
    Qm2 = b.sb("l0a_qm2", [128, 16], F32, es_qkv)
    Km2 = b.sb("l0a_km2", [128, 16], F32, es_qkv)
    kvT = b.sb("l0a_kvT", [128, T], BF16, es_qkv)
    cb = b.sb("l0a_cb", [128, 1], F32, es_qkv)
    b.ve(lambda e: e.memset(Qm2[:], 0.0), inc="ve")
    b.ve(lambda e: e.memset(Km2[:], 0.0), inc="ve")
    b.ve(lambda e: e.memset(V[:], 1.0), inc="ve")

    stage = b.sb("l0a_stage", [128, 512], F32, esQ)
    stageB = b.sb("l0a_stageB", [128, 512], F32, esQ)
    PQ = b.ps("l0a_PQ", [128, 1024], F32, esQ)
    PTx = [b.ps(f"l0a_PTx{i}", [128, 256], BF16, esQ) for i in range(2)]
    qf32s = [b.sb(f"l0a_qf32{i}", [128, 512], F32, esQ) for i in range(2)]
    sqs = [b.sb(f"l0a_sq{i}", [128, 512], F32, esQ) for i in range(2)]
    mss = [b.sb(f"l0a_ms{i}", [128, 8], F32, esQ) for i in range(2)]
    lns = b.sb("l0a_lns", [128, 8], F32, esQ)
    rinvs = [b.sb(f"l0a_rinv{i}", [128, 8], F32, esQ) for i in range(2)]
    qtb = b.sb("l0a_qtb", [128, 256], BF16, esQ)
    eps6 = consts["eps6"]

    ptx_evac = [b.cnt["ve"], b.cnt["ve"]]
    for half in range(2):
        wq_b = _cast_load(b, esQ, f"l0a_wq{half}",
                          caps["wq"][:, 512 * half:512 * half + 512], 1024, 512, stage,
                          stage2=stageB)
        pq_free = [b.cnt["ve"], b.cnt["ve"]]
        ars = {}
        for tl in range(NT + 1):
            if tl < NT:
                t, bb, nb = tl, tl % 2, tl % 2
                pe_q = None
                for c in range(NC):
                    w_ = [("ve", pq_free[bb])] if c == 0 else ()
                    pe_q = b.pe(lambda e, t=t, c=c, bb=bb, wq_b=wq_b: e.matmul(
                        PQ[:, 512 * bb:512 * bb + 512],
                        xnT[c][:, 128 * t:128 * t + 128],
                        wq_b[:, 512 * c:512 * c + 512],
                        start=(c == 0), stop=(c == NC - 1)),
                        waits=w_, inc="pe" if c == NC - 1 else None)
                pq_free[bb] = b.ve(lambda e, bb=bb, nb=nb: e.tensor_copy(
                    qf32s[nb][:], PQ[:, 512 * bb:512 * bb + 512]),
                    waits=[("pe", pe_q)])
                b.ve(lambda e, nb=nb: e.tensor_tensor(sqs[nb][:], qf32s[nb][:],
                                                      qf32s[nb][:], OP.mult))
                ve_ms = b.ve(lambda e, nb=nb: e.tensor_reduce(
                    mss[nb][:], sqs[nb][:].rearrange("p (h d) -> p h d", d=64),
                    AX.X, OP.add))
                b.act(lambda e, nb=nb: e.activation(lns[:], mss[nb][:], AF.Ln,
                                                    bias=eps6[:], scale=1.0 / 64),
                      waits=[("ve", ve_ms)])
                ars[t] = b.act(lambda e, nb=nb: e.activation(
                    rinvs[nb][:], lns[:], AF.Exp, bias=0.0, scale=-0.5))
            if tl >= 1:
                t, nb = tl - 1, (tl - 1) % 2
                for hh in range(8):
                    b.ve(lambda e, hh=hh, nb=nb: e.tensor_scalar(
                        qtb[:, 32 * hh:32 * hh + 32],
                        qf32s[nb][:, 64 * hh:64 * hh + 32],
                        rinvs[nb][:, hh:hh + 1], None, OP.mult),
                        waits=[("act", ars.pop(t))] if hh == 0 else ())
                b.ve(lambda e, nb=nb: e.tensor_tensor(sqs[nb][:, 0:256], qtb[:],
                                                      qtb[:], OP.mult))
                b.ve(lambda e, nb=nb: e.tensor_reduce(
                    mss[nb][:], sqs[nb][:, 0:256].rearrange("p (h d) -> p h d", d=32),
                    AX.X, OP.add))
                ve_tail = b.ve(lambda e, half=half, nb=nb: e.tensor_tensor(
                    Qm2[:, 8 * half:8 * half + 8], Qm2[:, 8 * half:8 * half + 8],
                    mss[nb][:], OP.max))
                pb = t % 2
                tp = None
                for jj in range(2):
                    for a in range(4):
                        hh = 4 * jj + a
                        w_ = [("ve", ve_tail), ("ve", ptx_evac[pb])] \
                            if (jj == 0 and a == 0) else ()
                        tp = b.pe(lambda e, hh=hh, a=a, jj=jj, pb=pb: e.transpose(
                            PTx[pb][32 * a:32 * a + 32, 128 * jj:128 * jj + 128],
                            qtb[:, 32 * hh:32 * hh + 32],
                            ident[:], tile_position=(0, 32 * a)),
                            waits=w_, inc="pe" if (jj == 1 and a == 3) else None)
                ov = qTs_all.rearrange("p (j t) -> p j t", j=4)[
                    :, 2 * half:2 * half + 2, 128 * t:128 * t + 128]
                ptx_evac[pb] = b.ve(lambda e, ov=ov, pb=pb: e.tensor_copy(
                    ov, PTx[pb][:].rearrange("p (j t) -> p j t", j=2)),
                    waits=[("pe", tp)])

    wkv_b = _cast_load(b, esQ, "l0a_wkv", caps["wkv"], 1024, 128, stage,
                       stage2=stageB)
    ve_prev = [b.cnt["ve"], b.cnt["ve"]]
    for w in range(NW):
        bb = w % 2
        pe_kv = None
        for c in range(NC):
            pe_kv = b.pe(lambda e, w=w, c=c, bb=bb: e.matmul(
                PQ[:, 512 * bb:512 * bb + 512], wkv_b[:, 128 * c:128 * c + 128],
                xnT[c][:, 512 * w:512 * w + 512],
                start=(c == 0), stop=(c == NC - 1)),
                waits=[("ve", ve_prev[bb])] if c == 0 else (),
                inc="pe" if c == NC - 1 else None)
        ve_prev[bb] = b.ve(lambda e, w=w, bb=bb: e.tensor_copy(
            kvT[:, 512 * w:512 * w + 512], PQ[:, 512 * bb:512 * bb + 512]),
            waits=[("pe", pe_kv)], inc="ve")
    esQ.close()
    es_xnT.close()
    aoTs = [b.sb(f"l0a_aoT{s}", [128, T], BF16, es_main, side="right")
            for s in range(8)]
    for s in range(8):
        b.ve(lambda e, s=s: e.memset(aoTs[s][:], 0.0))

    esK = b.scope()
    stage2 = b.sb("l0a_stage2", [128, 512], F32, esK)
    wk_b = _cast_load(b, esK, "l0a_wk", caps["wk"], 128, 512, stage2)
    wv_b = _cast_load(b, esK, "l0a_wv", caps["wv"], 128, 512, stage2)
    qg = b.sb("l0a_qg", [1, 64], F32, esK)
    kg = b.sb("l0a_kg", [1, 32], F32, esK)
    g512 = b.sb("l0a_g512", [1, 512], F32, esK)
    gbc = b.sb("l0a_gbc", [128, 512], F32, esK)
    ldg = b.ld(lambda e: e.dma_start(out=qg[:], in_=caps["qgv"][:]), slot=5)
    ldk = b.ld(lambda e: e.dma_start(out=kg[:], in_=caps["kgv"][:]), slot=4)
    b.ve(lambda e: e.tensor_tensor(g512[:, 0:32], qg[:, 0:32], kg[:], OP.mult),
         waits=[ldg, ldk])
    b.ve(lambda e: e.tensor_scalar(g512[:, 0:32], g512[:, 0:32], 0.125, None, OP.mult))
    for h in range(1, H):
        b.ve(lambda e, h=h: e.tensor_copy(g512[:, 32 * h:32 * h + 32], g512[:, 0:32]))
    veg = b.ve(lambda e: e.nop(), inc="ve")
    PK = b.ps("l0a_PK", [128, 512], F32, esK)
    PVs = [b.ps(f"l0a_PV{i}", [128, 512], F32, esK) for i in range(2)]
    PTk = [b.ps(f"l0a_PTk{i}", [128, 512], BF16, esK) for i in range(2)]
    pes_v = {}
    pg = b.pe(lambda e: e.matmul(PK[:], consts["onesrow_f"][:], g512[:],
                                 start=True, stop=True), waits=[("ve", veg)], inc="pe")
    veg = b.ve(lambda e: e.tensor_copy(gbc[:], PK[:]), waits=[("pe", pg)], inc="ve")

    lnsk = b.sb("l0a_lnsk", [128, 16], F32, esK)
    kf32s = [b.sb(f"l0a_kf32{i}", [128, 512], F32, esK) for i in range(2)]
    ktmp = b.sb("l0a_ktmp", [128, 512], F32, esK)
    ktb = b.sb("l0a_ktb", [128, 512], BF16, esK)
    sqk = b.sb("l0a_sqk", [128, 512], F32, esK)
    msks = [b.sb(f"l0a_msk{i}", [128, 16], F32, esK) for i in range(2)]
    rinvks = [b.sb(f"l0a_rinvk{i}", [128, 16], F32, esK) for i in range(2)]

    ptx_evac = [b.cnt["ve"], b.cnt["ve"]]
    kcopy = [veg, veg]
    vevac = [veg, veg]
    ars = {}
    for tl in range(NT + 1):
        if tl < NT:
            t, nb = tl, tl % 2
            pe_k = b.pe(lambda e, t=t: e.matmul(PK[:], kvT[:, 128 * t:128 * t + 128],
                                                wk_b[:], start=True, stop=True),
                        waits=[("ve", kcopy[(t + 1) % 2])], inc="pe")
            pe_v = b.pe(lambda e, t=t, nb=nb: e.matmul(
                PVs[nb][:], kvT[:, 128 * t:128 * t + 128],
                wv_b[:], start=True, stop=True),
                waits=[("ve", vevac[nb])], inc="pe")
            pes_v[t] = pe_v
            kcopy[nb] = b.ve(lambda e, nb=nb: e.tensor_copy(kf32s[nb][:], PK[:]),
                             waits=[("pe", pe_k)])
            b.ve(lambda e, nb=nb: e.tensor_tensor(sqk[:], kf32s[nb][:], kf32s[nb][:],
                                                  OP.mult))
            ve_ms = b.ve(lambda e, nb=nb: e.tensor_reduce(
                msks[nb][:], sqk[:].rearrange("p (h d) -> p h d", d=32), AX.X, OP.add),
                inc="ve")
            b.act(lambda e, nb=nb: e.activation(lnsk[:], msks[nb][:], AF.Ln,
                                                bias=eps6[:], scale=1.0 / 32),
                  waits=[("ve", ve_ms)])
            ars[t] = b.act(lambda e, nb=nb: e.activation(
                rinvks[nb][:], lnsk[:], AF.Exp, bias=0.0, scale=-0.5), inc="act")
        if tl >= 1:
            t, nb = tl - 1, (tl - 1) % 2
            for h in range(H):
                b.ve(lambda e, h=h, nb=nb: e.tensor_scalar(
                    ktmp[:, 32 * h:32 * h + 32], kf32s[nb][:, 32 * h:32 * h + 32],
                    rinvks[nb][:, h:h + 1], None, OP.mult),
                    waits=[("act", ars.pop(t))] if h == 0 else ())
            b.ve(lambda e: e.tensor_tensor(ktb[:], ktmp[:], gbc[:], OP.mult))
            b.ve(lambda e: e.tensor_tensor(sqk[:], ktb[:], ktb[:], OP.mult))
            b.ve(lambda e, nb=nb: e.tensor_reduce(
                msks[nb][:], sqk[:].rearrange("p (h d) -> p h d", d=32), AX.X, OP.add))
            km_t = b.ve(lambda e, nb=nb: e.tensor_tensor(Km2[:], Km2[:], msks[nb][:],
                                                         OP.max), inc="ve")
            vevac[nb] = b.ve(lambda e, t=t, nb=nb: e.tensor_copy(
                V[:, 528 * t:528 * t + 528].rearrange("p (h c) -> p h c", c=33)[:, :, 0:32],
                PVs[nb][:].rearrange("p (h c) -> p h c", c=32)),
                waits=[("pe", pes_v.pop(t))], inc="ve")
            pb = t % 2
            tp = None
            for j in range(4):
                for a in range(4):
                    h = 4 * j + a
                    w = [("ve", km_t), ("ve", ptx_evac[pb])] if (j == 0 and a == 0) else ()
                    tp = b.pe(lambda e, h=h, a=a, j=j, pb=pb: e.transpose(
                        PTk[pb][32 * a:32 * a + 32, 128 * j:128 * j + 128],
                        ktb[:, 32 * h:32 * h + 32],
                        ident[:], tile_position=(0, 32 * a)),
                        waits=w, inc="pe" if (j == 3 and a == 3) else None)
            ov = kTs_all.rearrange("p (j t) -> p j t", j=4)[:, :, 128 * t:128 * t + 128]
            ptx_evac[pb] = b.ve(lambda e, ov=ov, pb=pb: e.tensor_copy(
                ov, PTk[pb][:].rearrange("p (j t) -> p j t", j=4)),
                waits=[("pe", tp)])
    _build_shift_bias(b, esK, consts, Qm2, Km2, nrm_dram, cb, "l0a")
    esK.close()
    if _DBG.get("upto") == "B1":
        es_main.close()
        return

    esA = b.scope()
    PS = b.ps("l0a_PS", [128, 2048], F32, esA)
    PAV = [b.ps(f"l0a_PAV{p}", [128, 512], F32, esA) for p in range(2)]
    PB2 = b.ps("l0a_PB2", [128, 512], F32, esA)
    PT = b.sb("l0a_PTexp", [128, 2048], BF16, esA)
    pavc = b.sb("l0a_pavc", [128, 512], F32, esA)
    bcast_sb = b.sb("l0a_bc", [128, 512], F32, esA)
    recipb = b.sb("l0a_recip", [128, 512], F32, esA)
    Epair = b.sb("cs2_Epair", [128, 128], F32, esA)
    lE = b.ld(lambda e: e.dma_start(out=Epair[:], in_=caps["Epair"][:]),
              waits=[("ve", b.cnt["ve"])], slot=7)
    masks2 = []
    for d in range(4):
        mt = b.sb(f"cs2_mask2_{d}", [128, 1024], BF16, esA)
        b.ld(lambda e, mt=mt, d=d: e.dma_start(out=mt[:], in_=caps[f"mask2_{d}"][:]),
             waits=[lE] if d == 0 else (), slot=7)
        masks2.append(mt)
    b.emit("ve", lambda e: e.nop(), waits=[("ld7", b.cnt["ld7"])])

    ps_free = [b.cnt["act"], b.cnt["act"]]
    pt_free = [b.cnt["pe"], b.cnt["pe"]]
    pav_free = [b.cnt["ve"], b.cnt["ve"]]
    pb2_free = b.cnt["ve"]
    norm_done = b.cnt["ve"]
    for j in range(4):
        for w in range(NW):
            for p in range(2):
                ktn = 4 * (w + 1)
                mz = b.ve(lambda e, p=p: e.memset(PAV[p][:], 0.0),
                          waits=[("ve", pav_free[p])])
                pending = None

                def mk_av(j, w, p, kt, bb, gate, ktn, mzw):
                    def emit_av():
                        for ai in range(2):
                            a = 2 * p + ai
                            h = 4 * j + a
                            wl = [gate] + mzw if ai == 0 else ()
                            r_ = b.pe(lambda e, kt=kt, h=h, ai=ai, bb=bb, p=p, ktn=ktn:
                                      e.matmul(
                                PAV[p][64 * ai:64 * ai + 33, :],
                                V[:, 528 * kt + 33 * h:528 * kt + 33 * h + 33],
                                PT[:, 1024 * bb + 512 * ai:1024 * bb + 512 * ai + 512],
                                start=(kt == 0 and ai == 0), stop=(kt == ktn - 1),
                                tile_position=(0, 64 * ai), skip_group_check=True),
                                waits=wl, inc="pe" if ai == 1 else None)
                            if ai == 1:
                                pt_free[bb] = r_
                    return emit_av

                for kt in range(ktn):
                    bb = kt % 2
                    diag = kt >= 4 * w
                    pe_s = None
                    for ai in range(2):
                        a = 2 * p + ai
                        wl = [("act", ps_free[bb])] if ai == 0 else ()
                        pe_s = b.pe(lambda e, j=j, w=w, kt=kt, a=a, ai=ai, bb=bb:
                                    e.matmul(
                            PS[:, 1024 * bb + 512 * ai:1024 * bb + 512 * ai + 512],
                            kTs[j][32 * a:32 * a + 32, 128 * kt:128 * kt + 128],
                            qTs[j][32 * a:32 * a + 32, 512 * w:512 * w + 512],
                            start=True, stop=True, tile_position=(32 * a, 0)),
                            waits=wl, inc="pe" if ai == 1 else None)
                    exp_t = b.act(lambda e, bb=bb: e.activation(
                        PT[:, 1024 * bb:1024 * bb + 1024],
                        PS[:, 1024 * bb:1024 * bb + 1024], AF.Exp, bias=cb[:],
                        scale=1.0),
                        waits=[("pe", pe_s), ("pe", pt_free[bb])])
                    ps_free[bb] = exp_t
                    gate = ("act", exp_t)
                    if diag:
                        d = kt - 4 * w
                        mk = b.ve(lambda e, d=d, bb=bb: e.tensor_tensor(
                            PT[:, 1024 * bb:1024 * bb + 1024],
                            PT[:, 1024 * bb:1024 * bb + 1024], masks2[d][:], OP.mult),
                            waits=[("act", exp_t)])
                        gate = ("ve", mk)
                    if pending is not None:
                        pending()
                    pending = mk_av(j, w, p, kt, bb, gate, ktn,
                                    [("ve", mz)] if kt == 0 else [])
                pending()
                pv = b.ve(lambda e, p=p: e.tensor_copy(pavc[:], PAV[p][:]),
                          waits=[("pe", pt_free[(ktn - 1) % 2])])
                pav_free[p] = pv
                pb_ = b.pe(lambda e: e.matmul(PB2[:], Epair[:], pavc[:],
                                              start=True, stop=True),
                           waits=[("ve", pv), ("ve", pb2_free)], inc="pe")
                b.ve(lambda e: e.tensor_copy(bcast_sb[:], PB2[:]), waits=[("pe", pb_)])
                pb2_free = b.ve(lambda e: e.reciprocal(recipb[:], bcast_sb[:]))
                b.ve(lambda e, j=j, w=w, p=p: e.tensor_tensor(
                    aoTs[2 * j + p][0:32, 512 * w:512 * w + 512], pavc[0:32, :],
                    recipb[0:32, :], OP.mult))
                b.ve(lambda e, j=j, w=w, p=p: e.tensor_tensor(
                    aoTs[2 * j + p][64:96, 512 * w:512 * w + 512], pavc[64:96, :],
                    recipb[64:96, :], OP.mult))
    esA.close()
    es_qkv.close()

    esO = b.scope()
    stg3 = [b.sb(f"l0a_stage3{i}", [128, 1024], F32, esO) for i in range(4)]
    wo_b = _cast_load(b, esO, "l0a_wo", caps["wosel"], 1024, 1024, None,
                      stages=stg3, slots=[4, 5, 6, 7])
    PO = b.ps("l0a_PO", [128, 1024], F32, esO)
    PTf = [b.ps(f"l0a_PTf{i}", [128, 128], F32, esO) for i in range(2)]
    of32 = [b.sb(f"l0a_of32_{i}", [128, 512], F32, esO) for i in range(2)]
    obuf = b.sb("l0a_obuf", [128, 4096], F32, esO)

    ptf_evac = [b.cnt["ve"], b.cnt["ve"]]
    po_free = [b.cnt["ve"], b.cnt["ve"]]
    ve_o = b.cnt["ve"]
    st_o = ("st0", b.cnt["st0"])
    for w in range(NW):
        for do in range(NC):
            bb = do % 2
            pe_o = None
            for s in range(8):
                pe_o = b.pe(lambda e, w=w, do=do, s=s, bb=bb: e.matmul(
                    PO[:, 512 * bb:512 * bb + 512],
                    wo_b[:, 1024 * s + 128 * do:1024 * s + 128 * do + 128],
                    aoTs[s][:, 512 * w:512 * w + 512],
                    start=(s == 0), stop=(s == 7)),
                    waits=[("ve", po_free[bb])] if s == 0 else (),
                    inc="pe" if s == 7 else None)
            po_free[bb] = b.ve(lambda e, bb=bb: e.tensor_copy(
                of32[bb][:], PO[:, 512 * bb:512 * bb + 512]), waits=[("pe", pe_o)])
            for tt in range(4):
                pb = tt % 2
                tp = b.pe(lambda e, tt=tt, pb=pb, bb=bb: e.transpose(
                    PTf[pb][:], of32[bb][:, 128 * tt:128 * tt + 128],
                    consts["identf"][:]),
                    waits=[("ve", po_free[bb]), ("ve", ptf_evac[pb])], inc="pe")
                ptf_evac[pb] = b.ve(lambda e, tt=tt, do=do, pb=pb: e.tensor_copy(
                    obuf[:, 1024 * tt + 128 * do:1024 * tt + 128 * do + 128], PTf[pb][:]),
                    waits=[("pe", tp)])
        for tt in range(4):
            st_o = b.st(lambda e, w=w, tt=tt: e.dma_start(
                out=a0_dram[512 * w + 128 * tt:512 * w + 128 * tt + 128, :],
                in_=obuf[:, 1024 * tt:1024 * tt + 1024]),
                waits=[("ve", max(ptf_evac))], slot=tt)
        if w < NW - 1:
            nv = b.emit("ve", lambda e: e.nop(),
                        waits=[(f"st{i}", b.cnt[f"st{i}"]) for i in range(4)])
            ptf_evac = [nv, nv]
    esO.close()
    es_main.close()



def emit_xn2T(b, consts, caps, x_dram, stats_sb, xn2T, xn2T_all):
    es = b.scope()
    xs = [b.sb(f"x2_{i}", [128, D], F32, es) for i in range(4)]
    xnfs = [b.sb(f"x2_xnf{i}", [128, D], F32, es) for i in range(2)]
    st6 = b.sb("x2_st6", [128, 12], F32, es)
    mvs = [b.sb(f"x2_mv{i}", [128, 2], F32, es) for i in range(2)]
    lnv = b.sb("x2_lnv", [128, 1], F32, es)
    rsts = [b.sb(f"x2_rst2{i}", [128, 1], F32, es) for i in range(2)]
    xnb = [b.sb(f"x2_xnb{i}", [128, D], BF16, es) for i in range(2)]
    ptr = [b.ps(f"x2_pt{i}", [128, 1024], BF16, es) for i in range(2)]
    wbc = b.sb("x2_wbc", [128, D], F32, es)
    bbc = b.sb("x2_bbc", [128, D], F32, es)
    nwv = b.sb("x2_nw", [1, D], F32, es)
    nbv = b.sb("x2_nb", [1, D], F32, es)
    PB = b.ps("x2_PB", [128, 1024], F32, es)
    ident = consts["ident"]
    eps5 = consts["eps5"]

    ln1 = b.ld(lambda e: e.dma_start(out=nwv[:], in_=caps["nw"][:]), slot=5)
    ln2 = b.ld(lambda e: e.dma_start(out=nbv[:], in_=caps["nb"][:]), slot=4)
    vv = b.ve(lambda e: e.nop(), waits=[ln1, ln2])
    pv = None
    for hh in range(2):
        pv = b.pe(lambda e, hh=hh: e.matmul(
            PB[:, 512 * hh:512 * hh + 512], consts["onesrow_f"][:],
            nwv[:, 512 * hh:512 * hh + 512], start=True, stop=True),
            waits=[("ve", vv)], inc="pe")
    vv = b.ve(lambda e: e.tensor_copy(wbc[:], PB[:]), waits=[("pe", pv)])
    for hh in range(2):
        pv = b.pe(lambda e, hh=hh: e.matmul(
            PB[:, 512 * hh:512 * hh + 512], consts["onesrow_f"][:],
            nbv[:, 512 * hh:512 * hh + 512], start=True, stop=True),
            waits=[("ve", vv)], inc="pe")
    vv = b.ve(lambda e: e.tensor_copy(bbc[:], PB[:]), waits=[("pe", pv)])

    pe_done = b.cnt["pe"]
    evac_cnt = [b.cnt["ve"], b.cnt["ve"]]
    scal_hist = [b.cnt["ve"]] * 4
    ars = {}
    for tl in range(NT + 1):
        if tl < NT:
            t, buf, nb2 = tl, tl % 4, tl % 2
            ld_t = b.ld(lambda e, t=t, buf=buf: e.dma_start(
                out=xs[buf][:], in_=x_dram[128 * t:128 * t + 128, :]),
                waits=[("ve", scal_hist[buf])] if t >= 4 else (), slot=buf)
            scal_hist[buf] = b.ve(lambda e, t=t, buf=buf, nb2=nb2: e.tensor_scalar(
                xnfs[nb2][:], xs[buf][:], stats_sb[:, 2 * t:2 * t + 1],
                stats_sb[:, 2 * t + 1:2 * t + 2], OP.subtract, OP.mult),
                waits=[ld_t])
            s1 = b.ve(lambda e, nb2=nb2: e.bn_stats(st6[:, 0:6], xnfs[nb2][:, 0:512]))
            s2 = b.ve(lambda e, nb2=nb2: e.bn_stats(st6[:, 6:12], xnfs[nb2][:, 512:1024]),
                      waits=[("ve", s1)])
            ve_mv = b.ve(lambda e, nb2=nb2: e.bn_aggr(mvs[nb2][:], st6[:]),
                         waits=[("ve", s2)])
            b.act(lambda e, nb2=nb2: e.activation(lnv[:], mvs[nb2][:, 1:2], AF.Ln,
                                                  bias=eps5[:], scale=1.0),
                  waits=[("ve", ve_mv)])
            ars[t] = b.act(lambda e, nb2=nb2: e.activation(
                rsts[nb2][:], lnv[:], AF.Exp, bias=0.0, scale=-0.5))
        if tl >= 1:
            t, nb2 = tl - 1, (tl - 1) % 2
            b.ve(lambda e, nb2=nb2: e.tensor_scalar(
                xnfs[nb2][:], xnfs[nb2][:], mvs[nb2][:, 0:1], rsts[nb2][:],
                OP.subtract, OP.mult),
                waits=[("act", ars.pop(t))])
            b.ve(lambda e, nb2=nb2: e.tensor_tensor(xnfs[nb2][:], xnfs[nb2][:],
                                                    wbc[:], OP.mult))
            ve_done = b.ve(lambda e, nb2=nb2: e.tensor_tensor(
                xnb[nb2][:], xnfs[nb2][:], bbc[:], OP.add),
                waits=[("pe", pe_done)])
            pb = t % 2
            tp = None
            for c in range(NC):
                w2 = [("ve", ve_done), ("ve", evac_cnt[pb])] if c == 0 else ()
                tp = b.pe(lambda e, c=c, nb2=nb2, pb=pb: e.transpose(
                    ptr[pb][:, 128 * c:128 * c + 128],
                    xnb[nb2][:, 128 * c:128 * c + 128], ident[:]),
                    waits=w2, inc="pe" if c == NC - 1 else None)
            ov = xn2T_all.rearrange("p (c t) -> p c t", c=NC)[:, :, 128 * t:128 * t + 128]
            evac_cnt[pb] = b.ve(lambda e, ov=ov, pb=pb: e.tensor_copy(
                ov, ptr[pb][:].rearrange("p (c t) -> p c t", c=NC)),
                waits=[("pe", tp)])
            pe_done = tp
    es.close()


def emit_l0ffn(b, consts, caps, x_dram, stats_sb, f0_dram, qT0f, kT0f, aoT0f, dbg=None):
    H = 8
    es_main = b.scope()
    xn2T_all = b.sb("xn2T_all", [128, NC * T], BF16, es_main, side="right")
    xn2T = [xn2T_all[:, c * T:(c + 1) * T] for c in range(NC)]
    emit_xn2T(b, consts, caps, x_dram, stats_sb, xn2T, xn2T_all)

    V0 = b.sb("l0f_V", [128, NT * 1024], BF16, es_main)
    cb = b.sb("l0f_cb", [128, 1], F32, es_main)
    qm = b.sb("l0f_qm", [1, 2], F32, es_main)
    b.ve(lambda e: e.memset(qm[:], 0.0))

    def qk_pass(which, wT_dram, bias_dram, out_dram, scale, qmcol):
        esP = b.scope()
        stage = b.sb(f"l0f_stage_{which}", [128, 1024], F32, esP)
        stageB = b.sb(f"l0f_stageB_{which}", [128, 1024], F32, esP)
        wT_b = _cast_load(b, esP, f"l0f_{which}w", wT_dram, 1024, 1024, stage,
                          stage2=stageB)
        bia = b.sb(f"l0f_{which}b", [128, 8], F32, esP)
        lb = b.ld(lambda e: e.dma_start(out=bia[:], in_=bias_dram[:]), slot=5)
        qsl = [b.sb(f"l0f_{which}sl{i}", [128, 512], BF16, esP) for i in range(4)]
        sqs = b.sb(f"l0f_{which}sq", [128, 512], F32, esP)
        nrm1 = b.sb(f"l0f_{which}n1", [1, 512], F32, esP)
        PP = b.ps(f"l0f_PP_{which}", [128, 1024], F32, esP)
        PN = b.ps(f"l0f_PN_{which}", [128, 512], F32, esP)
        pp_free = [b.cnt["ve"], b.cnt["ve"]]
        sl_st = [(f"st{i}", 0) for i in range(4)]
        sidx = 0
        for e_ in range(4):
            mch = NC >> e_
            for do in range(NC):
                for ws in range(2):
                    w = 2 * e_ + ws
                    bb = sidx % 2
                    qb = sidx % 4
                    pe_q = None
                    for ci in range(mch):
                        pe_q = b.pe(lambda e, ci=ci, do=do, w=w, mch=mch, bb=bb:
                                    e.matmul(
                            PP[:, 512 * bb:512 * bb + 512],
                            wT_b[:, 1024 * ci + 128 * do:1024 * ci + 128 * do + 128],
                            xn2T[ci][:, 512 * w:512 * w + 512],
                            start=(ci == 0), stop=(ci == mch - 1)),
                            waits=[("ve", pp_free[bb]), lb] if ci == 0 else (),
                            inc="pe" if ci == mch - 1 else None)
                    if scale is not None:
                        vq = b.ve(lambda e, do=do, bb=bb, qb=qb: e.tensor_scalar(
                            qsl[qb][:], PP[:, 512 * bb:512 * bb + 512],
                            bia[:, do:do + 1], scale, OP.add, OP.mult),
                            waits=[("pe", pe_q), sl_st[qb]])
                    else:
                        vq = b.ve(lambda e, do=do, bb=bb, qb=qb: e.tensor_scalar(
                            qsl[qb][:], PP[:, 512 * bb:512 * bb + 512],
                            bia[:, do:do + 1], None, OP.add),
                            waits=[("pe", pe_q), sl_st[qb]])
                    pp_free[bb] = b.ve(lambda e, qb=qb: e.tensor_tensor(
                        sqs[:], qsl[qb][:], qsl[qb][:], OP.mult))
                    sl_st[qb] = b.st(lambda e, do=do, w=w, qb=qb: e.dma_start(
                        out=out_dram[128 * do:128 * do + 128, 512 * w:512 * w + 512],
                        in_=qsl[qb][:]), waits=[("ve", pp_free[bb])], slot=qb)
                    pn = b.pe(lambda e: e.matmul(PN[0:1, :], consts["onescol_f"][:],
                                                 sqs[:], start=True, stop=True),
                              waits=[("ve", pp_free[bb])], inc="pe")
                    b.ve(lambda e: e.reduce_max(nrm1[:, 0:1], PN[0:1, :], axis=AX.X),
                         waits=[("pe", pn)])
                    b.ve(lambda e, qmcol=qmcol: e.tensor_tensor(
                        qm[:, qmcol:qmcol + 1], qm[:, qmcol:qmcol + 1],
                        nrm1[:, 0:1], OP.max))
                    sidx += 1
        b.emit("ve", lambda e: e.nop(),
               waits=[(f"st{i}", b.cnt[f"st{i}"]) for i in range(4)])
        esP.close()

    qk_pass("q", caps["qwT"], caps["qb"], qT0f, 1.0 / 32, 0)
    qk_pass("k", caps["kwT"], caps["kb"], kT0f, None, 1)

    esV = b.scope()
    stageV = b.sb("l0f_stageV", [128, 1024], F32, esV)
    stageVB = b.sb("l0f_stageVB", [128, 1024], F32, esV)
    vwT_b = _cast_load(b, esV, "l0f_vw", caps["vwT"], 1024, 1024, stageV,
                       stage2=stageVB)
    vb1024 = b.sb("l0f_vb1024", [1, 1024], F32, esV)
    vbc = b.sb("l0f_vbc", [128, 1024], F32, esV)
    lb = b.ld(lambda e: e.dma_start(out=vb1024[:], in_=caps["vb"][:]), slot=5)
    vv = b.ve(lambda e: e.nop(), waits=[lb])
    PP = b.ps("l0f_PPv", [128, 1024], F32, esV)
    for hh in range(2):
        pv = b.pe(lambda e, hh=hh: e.matmul(
            PP[:, 0:512], consts["onesrow_f"][:], vb1024[:, 512 * hh:512 * hh + 512],
            start=True, stop=True), waits=[("ve", vv)], inc="pe")
        vv = b.ve(lambda e, hh=hh: e.tensor_copy(vbc[:, 512 * hh:512 * hh + 512],
                                                 PP[:, 0:512]),
                  waits=[("pe", pv)])
    pp_free = [vv, vv]
    sidx = 0
    for e_ in range(4):
        mch = NC >> e_
        for ts_ in range(8):
            t = 8 * e_ + ts_
            for hh in range(2):
                bb = sidx % 2
                pe_v = None
                for ci in range(mch):
                    pe_v = b.pe(lambda e, ci=ci, t=t, hh=hh, mch=mch, bb=bb: e.matmul(
                        PP[:, 512 * bb:512 * bb + 512],
                        xn2T[ci][:, 128 * t:128 * t + 128],
                        vwT_b[:, 1024 * ci + 512 * hh:1024 * ci + 512 * hh + 512],
                        start=(ci == 0), stop=(ci == mch - 1)),
                        waits=[("ve", pp_free[bb])] if ci == 0 else (),
                        inc="pe" if ci == mch - 1 else None)
                pp_free[bb] = b.ve(lambda e, t=t, hh=hh, bb=bb: e.tensor_tensor(
                    V0[:, 1024 * t + 512 * hh:1024 * t + 512 * hh + 512],
                    PP[:, 512 * bb:512 * bb + 512],
                    vbc[:, 512 * hh:512 * hh + 512], OP.add),
                    waits=[("pe", pe_v)])
                sidx += 1
    esV.close()

    esB = b.scope()
    PN2 = b.ps("l0f_PN2", [128, 8], F32, esB)
    ve0 = b.ve(lambda e: e.tensor_tensor(qm[:, 0:1], qm[:, 0:1], qm[:, 1:2], OP.mult))
    b.act(lambda e: e.activation(qm[:, 1:2], qm[:, 0:1], AF.Ln, bias=0.0, scale=1.0),
          waits=[("ve", ve0)])
    a1 = b.act(lambda e: e.activation(qm[:, 0:1], qm[:, 1:2], AF.Exp, bias=0.0, scale=0.5))
    ve1 = b.ve(lambda e: e.tensor_scalar(qm[:, 1:2], qm[:, 0:1], -1.01, None, OP.mult),
               waits=[("act", a1)])
    p0 = b.pe(lambda e: e.matmul(PN2[:, 0:1], consts["onesrow_f"][:], qm[:, 1:2],
                                 start=True, stop=True), waits=[("ve", ve1)], inc="pe")
    b.ve(lambda e: e.tensor_copy(cb[:], PN2[:, 0:1]), waits=[("pe", p0)])
    esB.close()
    es_xn2T_close = None

    esA = b.scope()
    PS = b.ps("l0f_PS", [128, 3072], F32, esA)
    PAV = b.ps("l0f_PAV", [128, 512], F32, esA)
    PSM = b.ps("l0f_PSM", [128, 512], F32, esA)
    PT = b.sb("l0f_PT", [128, 3072], BF16, esA)
    qwin = [b.sb(f"l0f_qw{i}", [128, 512], BF16, esA) for i in range(2)]
    kTh = [b.sb(f"l0f_kTh{i}", [128, T], BF16, esA) for i in range(2)]
    srow = b.sb("l0f_srow", [1, 512], F32, esA)
    rrow = b.sb("l0f_rrow", [1, 512], F32, esA)
    bcast = b.sb("l0f_bcast", [128, 512], F32, esA)
    pavc = b.sb("l0f_pavc", [128, 512], F32, esA)
    aosl = [b.sb(f"l0f_aosl{i}", [128, 512], BF16, esA) for i in range(2)]

    GROUPS = [(0, 3), (3, 3), (6, 3), (9, 3), (12, 3), (15, 3), (18, 3), (21, 3),
              (24, 3), (27, 3), (30, 2)]
    wins = [(h, w) for h in range(H) for w in range(NW)]
    ps_free = [b.cnt["act"], b.cnt["act"]]
    pt_free = [b.cnt["pe"], b.cnt["pe"]]
    qw_free = [b.cnt["pe"], b.cnt["pe"]]
    kh_free = [b.cnt["pe"], b.cnt["pe"]]
    pav_ready = b.cnt["ve"]
    psm_ready = b.cnt["ve"]
    ao_st = [("st2", 0), ("st3", 0)]
    lq = {}
    kld = {}

    def issue_lq(idx):
        if idx >= len(wins):
            return
        h, w = wins[idx]
        qb_ = idx % 2
        lq[idx] = b.ld(lambda e, h=h, w=w, qb_=qb_: e.dma_start(
            out=qwin[qb_][:], in_=qT0f[128 * h:128 * h + 128, 512 * w:512 * w + 512]),
            waits=[("pe", qw_free[qb_])], slot=qb_)

    def issue_kld(h):
        if h >= H:
            return
        hb = h % 2
        kld[h] = b.ld(lambda e, h=h, hb=hb: e.dma_start(
            out=kTh[hb][:], in_=kT0f[128 * h:128 * h + 128, :]),
            waits=[("pe", kh_free[hb])], slot=4 + hb)

    issue_lq(0)
    issue_kld(0)
    issue_kld(1)
    for idx, (h, w) in enumerate(wins):
        hb = h % 2
        qb_ = idx % 2
        pending = None

        def mk_av(h, kt0, gn, bb, exp_t, lastg, first):
            def emit_av():
                for i in range(gn):
                    kt = kt0 + i
                    wl = [exp_t] if i == 0 else ()
                    if i == 0 and first:
                        wl = [exp_t, ("ve", pav_ready)]
                    b.pe(lambda e, h=h, kt=kt, i=i, bb=bb, lastg=lastg, gn=gn:
                         e.matmul(
                        PAV[:], V0[:, 1024 * kt + 128 * h:1024 * kt + 128 * h + 128],
                        PT[:, 1536 * bb + 512 * i:1536 * bb + 512 * i + 512],
                        start=(kt == 0), stop=(lastg and i == gn - 1),
                        skip_group_check=True),
                        waits=wl)
                r_ = None
                for i in range(gn):
                    kt = kt0 + i
                    wl = [("ve", psm_ready)] if (i == 0 and first) else ()
                    r_ = b.pe(lambda e, kt=kt, i=i, bb=bb, lastg=lastg, gn=gn:
                              e.matmul(
                        PSM[0:1, :], consts["onescol_bf"][:],
                        PT[:, 1536 * bb + 512 * i:1536 * bb + 512 * i + 512],
                        start=(kt == 0), stop=(lastg and i == gn - 1),
                        skip_group_check=True),
                        waits=wl, inc="pe" if i == gn - 1 else None)
                pt_free[bb] = r_
            return emit_av

        for gi, (kt0, gn) in enumerate(GROUPS):
            bb = gi % 2
            pe_s = None
            for i in range(gn):
                kt = kt0 + i
                wl = []
                if i == 0:
                    wl = [("act", ps_free[bb]), lq[idx], kld[h]]
                pe_s = b.pe(lambda e, hb=hb, kt=kt, i=i, bb=bb, qb_=qb_, gn=gn:
                            e.matmul(
                    PS[:, 1536 * bb + 512 * i:1536 * bb + 512 * i + 512],
                    kTh[hb][:, 128 * kt:128 * kt + 128], qwin[qb_][:],
                    start=True, stop=True),
                    waits=wl, inc="pe" if i == gn - 1 else None)
            qw_free[qb_] = pe_s
            kh_free[hb] = pe_s
            if gi == 2:
                issue_lq(idx + 1)
            exp_t = b.act(lambda e, bb=bb, gn=gn: e.activation(
                PT[:, 1536 * bb:1536 * bb + 512 * gn],
                PS[:, 1536 * bb:1536 * bb + 512 * gn], AF.Exp, bias=cb[:],
                scale=1.0),
                waits=[("pe", pe_s), ("pe", pt_free[bb])])
            ps_free[bb] = exp_t
            lastg = (gi == len(GROUPS) - 1)
            if pending is not None:
                pending()
            pending = mk_av(h, kt0, gn, bb, ("act", exp_t), lastg, gi == 0)
        pending()
        if w == NW - 1:
            issue_kld(h + 2)
        pv = b.ve(lambda e: e.tensor_copy(pavc[:], PAV[:]),
                  waits=[("pe", pt_free[(len(GROUPS) - 1) % 2]), ao_st[idx % 2]])
        sv = b.ve(lambda e: e.tensor_copy(srow[:], PSM[0:1, :]))
        psm_ready = sv
        rv = b.ve(lambda e: e.reciprocal(rrow[:], srow[:]))
        pb_ = b.pe(lambda e: e.matmul(PAV[:], consts["onesrow_f"][:], rrow[:],
                                      start=True, stop=True),
                   waits=[("ve", rv)], inc="pe")
        pav_ready = b.ve(lambda e: e.tensor_copy(bcast[:], PAV[:]), waits=[("pe", pb_)])
        ve_f = b.ve(lambda e, qb_=qb_: e.tensor_tensor(aosl[qb_][:], pavc[:], bcast[:],
                                                       OP.mult))
        s_, v_ = b.st(lambda e, h=h, w=w, qb_=qb_: e.dma_start(
            out=aoT0f[128 * h:128 * h + 128, 512 * w:512 * w + 512], in_=aosl[qb_][:]),
            waits=[("ve", ve_f)], slot=2 + idx % 2)
        ao_st[idx % 2] = (s_, v_)
    esA.close()

    esO = b.scope()
    stg4 = [b.sb(f"l0f_stage4{i}", [128, 1024], F32, esO) for i in range(3)]
    owT_b = _cast_load(b, esO, "l0f_ow", caps["owT"], 1024, 1024, None,
                       stages=stg4, slots=[4, 6, 7])
    ob = b.sb("l0f_ob", [128, 8], F32, esO)
    b.ld(lambda e: e.dma_start(out=ob[:], in_=caps["ob"][:]), slot=5)
    aoseg = b.sb("l0f_aoseg", [128, 8 * 1024], BF16, esO)
    PO = b.ps("l0f_PO", [128, 1024], F32, esO)
    PTf = [b.ps(f"l0f_PTf{i}", [128, 128], F32, esO) for i in range(2)]
    of32 = [b.sb(f"l0f_of32_{i}", [128, 512], F32, esO) for i in range(2)]
    obuf = b.sb("l0f_obuf", [128, 4096], F32, esO)
    zbuf = b.sb("l0f_zbuf", [128, 1024], F32, esO)
    vz = b.ve(lambda e: e.memset(zbuf[:], 0.0))
    for t in range(NT):
        b.st(lambda e, t=t: e.dma_start(out=f0_dram[128 * t:128 * t + 128, :],
                                        in_=zbuf[:]), waits=[("ve", vz)], slot=t % 4)
    zdone = [(f"st{i}", b.cnt[f"st{i}"]) for i in range(4)]

    ptf_evac = [b.cnt["ve"], b.cnt["ve"]]
    for e_ in range(4):
        mch = NC >> e_
        ldd = {}
        for ci in range(mch):
            s_, v_ = b.ld(lambda e, e_=e_, ci=ci: e.dma_start(
                out=aoseg[:, 1024 * ci:1024 * ci + 1024],
                in_=aoT0f[128 * ci:128 * ci + 128, 1024 * e_:1024 * e_ + 1024]),
                waits=[("ve", b.cnt["ve"])], slot=ci % 4)
            ldd[s_] = v_
        lds = list(ldd.items())
        ve_o = b.cnt["ve"]
        st_w = []
        po_free = [b.cnt["ve"], b.cnt["ve"]]
        sidx = 0
        for ws in range(2):
            for do in range(mch):
                bb = sidx % 2
                pe_o = None
                for ci in range(mch):
                    pe_o = b.pe(lambda e, ci=ci, do=do, ws=ws, mch=mch, bb=bb: e.matmul(
                        PO[:, 512 * bb:512 * bb + 512],
                        owT_b[:, 1024 * ci + 128 * do:1024 * ci + 128 * do + 128],
                        aoseg[:, 1024 * ci + 512 * ws:1024 * ci + 512 * ws + 512],
                        start=(ci == 0), stop=(ci == mch - 1)),
                        waits=[("ve", po_free[bb])] + (lds if (ws == 0 and do == 0)
                                                      else [])
                        if ci == 0 else (),
                        inc="pe" if ci == mch - 1 else None)
                po_free[bb] = b.ve(lambda e, do=do, bb=bb: e.tensor_scalar(
                    of32[bb][:], PO[:, 512 * bb:512 * bb + 512],
                    ob[:, do:do + 1], None, OP.add),
                    waits=[("pe", pe_o), ("ld5", b.cnt["ld5"])])
                for tt in range(4):
                    pb = tt % 2
                    tp = b.pe(lambda e, tt=tt, pb=pb, bb=bb: e.transpose(
                        PTf[pb][:], of32[bb][:, 128 * tt:128 * tt + 128],
                        consts["identf"][:]),
                        waits=[("ve", po_free[bb]), ("ve", ptf_evac[pb])], inc="pe")
                    ptf_evac[pb] = b.ve(lambda e, tt=tt, do=do, pb=pb: e.tensor_copy(
                        obuf[:, 1024 * tt + 128 * do:1024 * tt + 128 * do + 128],
                        PTf[pb][:]), waits=[("pe", tp)])
                sidx += 1
            for tt in range(4):
                t_global = 8 * e_ + 4 * ws + tt
                st_w = b.st(lambda e, t_global=t_global, tt=tt, mch=mch: e.dma_start(
                    out=f0_dram[128 * t_global:128 * t_global + 128, 0:128 * mch],
                    in_=obuf[:, 1024 * tt:1024 * tt + 128 * mch]),
                    waits=[("ve", max(ptf_evac))] + (zdone if e_ == 0 and ws == 0 else []),
                    slot=tt)
            ve_o = b.emit("ve", lambda e: e.nop(),
                          waits=[(f"st{i}", b.cnt[f"st{i}"]) for i in range(4)])
    esO.close()
    es_main.close()



def emit_assembly_ln(b, consts, x_dram, a_dram, f_dram, stats_sb, add_xn,
                     x1_dram, stats1_sb, xnT1, xnT1_all):
    es = b.scope()
    xs = [b.sb(f"as_x{i}", [128, D], F32, es) for i in range(2)]
    as_ = [b.sb(f"as_a{i}", [128, D], F32, es) for i in range(2)]
    fs = [b.sb(f"as_f{i}", [128, D], F32, es) for i in range(2)]
    x1fs = [b.sb(f"as_x1f{i}", [128, D], F32, es) for i in range(2)]
    tmp = b.sb("as_tmp", [128, D], F32, es)
    st6 = b.sb("as_st6", [128, 12], F32, es)
    mvs = [b.sb(f"as_mv{i}", [128, 2], F32, es) for i in range(2)]
    lnv = b.sb("as_lnv", [128, 1], F32, es)
    xnb = [b.sb(f"as_xnb{i}", [128, D], BF16, es) for i in range(2)]
    ptr = [b.ps(f"as_pt{i}", [128, 1024], BF16, es) for i in range(2)]
    ident = consts["ident"]
    eps5 = consts["eps5"]

    for c in range(NC):
        b.ve(lambda e, c=c: e.memset(xnT1[c][:, 0:1], 0.0))

    pe_done = b.cnt["pe"]
    evac_cnt = [b.cnt["ve"], b.cnt["ve"]]
    scal_hist = [b.cnt["ve"], b.cnt["ve"]]
    ars = {}
    for tl in range(NT + 1):
        if tl < NT:
            t, buf, nb = tl, tl % 2, tl % 2
            w0 = [("ve", scal_hist[buf])] if t >= 2 else ()
            l1 = b.ld(lambda e, t=t, buf=buf: e.dma_start(
                out=xs[buf][:], in_=x_dram[128 * t:128 * t + 128, :]), waits=w0,
                slot=buf)
            l2 = b.ld(lambda e, t=t, buf=buf: e.dma_start(
                out=as_[buf][:], in_=a_dram[128 * t:128 * t + 128, :]), waits=w0,
                slot=2 + buf)
            l3 = b.ld(lambda e, t=t, buf=buf: e.dma_start(
                out=fs[buf][:], in_=f_dram[128 * t:128 * t + 128, :]), waits=w0,
                slot=4 + buf)
            x1f = x1fs[nb]
            b.ve(lambda e, buf=buf, x1f=x1f: e.tensor_tensor(
                x1f[:], as_[buf][:], fs[buf][:], OP.add),
                waits=[l2, l3, (f"st{t % 2}", b.cnt[f"st{t % 2}"])])
            scal_hist[buf] = b.ve(lambda e, buf=buf, x1f=x1f: e.tensor_tensor(
                x1f[:], x1f[:], xs[buf][:], OP.add), waits=[l1])
            if add_xn:
                scal_hist[buf] = b.ve(lambda e, t=t, buf=buf: e.tensor_scalar(
                    tmp[:], xs[buf][:], stats_sb[:, 2 * t:2 * t + 1],
                    stats_sb[:, 2 * t + 1:2 * t + 2], OP.subtract, OP.mult))
                b.ve(lambda e, x1f=x1f: e.tensor_tensor(x1f[:], x1f[:], tmp[:],
                                                        OP.add))
            vx = b.ve(lambda e: e.nop())
            b.st(lambda e, t=t, x1f=x1f: e.dma_start(
                out=x1_dram[128 * t:128 * t + 128, :], in_=x1f[:]),
                waits=[("ve", vx)], slot=t % 2)
            s1 = b.ve(lambda e, x1f=x1f: e.bn_stats(st6[:, 0:6], x1f[:, 0:512]))
            s2 = b.ve(lambda e, x1f=x1f: e.bn_stats(st6[:, 6:12], x1f[:, 512:1024]),
                      waits=[("ve", s1)])
            ve_mv = b.ve(lambda e, nb=nb: e.bn_aggr(mvs[nb][:], st6[:]),
                         waits=[("ve", s2)])
            b.ve(lambda e, t=t, nb=nb: e.tensor_copy(stats1_sb[:, 2 * t:2 * t + 1],
                                                     mvs[nb][:, 0:1]),
                 waits=[("ve", ve_mv)])
            b.act(lambda e, nb=nb: e.activation(lnv[:], mvs[nb][:, 1:2], AF.Ln,
                                                bias=eps5[:], scale=1.0),
                  waits=[("ve", ve_mv)])
            ars[t] = b.act(lambda e, t=t: e.activation(
                stats1_sb[:, 2 * t + 1:2 * t + 2], lnv[:], AF.Exp, bias=0.0,
                scale=-0.5))
        if tl >= 1:
            t, nb = tl - 1, (tl - 1) % 2
            x1f = x1fs[nb]
            ve_done = b.ve(lambda e, t=t, nb=nb, x1f=x1f: e.tensor_scalar(
                xnb[nb][:], x1f[:], stats1_sb[:, 2 * t:2 * t + 1],
                stats1_sb[:, 2 * t + 1:2 * t + 2], OP.subtract, OP.mult),
                waits=[("act", ars.pop(t)), ("pe", pe_done)])
            pb = t % 2
            tp = None
            for c in range(NC):
                w2 = [("ve", ve_done), ("ve", evac_cnt[pb])] if c == 0 else ()
                tp = b.pe(lambda e, c=c, nb=nb, pb=pb: e.transpose(
                    ptr[pb][:, 128 * c:128 * c + 128],
                    xnb[nb][:, 128 * c:128 * c + 128], ident[:]),
                    waits=w2, inc="pe" if c == NC - 1 else None)
            ov = xnT1_all.rearrange("p (c t) -> p c t", c=NC)[
                :, :, 1 + 128 * t:1 + 128 * t + 128]
            evac_cnt[pb] = b.ve(lambda e, ov=ov, pb=pb: e.tensor_copy(
                ov, ptr[pb][:].rearrange("p (c t) -> p c t", c=NC)),
                waits=[("pe", tp)])
            pe_done = tp
    es.close()


def emit_l1a(b, consts, caps, xnT1, qT1f, a1_dram, nrm_dram):
    es_main = b.scope()
    es_kv = b.scope()
    kT64 = [b.sb(f"l1_kT64_{v}", [128, T], BF16, es_kv) for v in range(2)]
    V1 = b.sb("l1_V1", [128, NT * 256], BF16, es_kv)
    cb = b.sb("l1_cb", [128, 1], F32, es_kv)
    Qn = b.sb("l1_Qn", [128, 16], F32, es_kv)
    Kn = b.sb("l1_Kn", [128, 16], F32, es_kv)
    b.ve(lambda e: e.memset(Qn[:], 0.0))
    b.ve(lambda e: e.memset(Kn[:], 0.0))

    esP = b.scope()
    stg1 = [b.sb(f"l1_stage{i}", [128, 1024], F32, esP) for i in range(4)]
    cos4 = b.sb("l1_cos4", [128, T], F32, esP)
    sin4 = b.sb("l1_sin4", [128, T], F32, esP)
    lc = b.ld(lambda e: e.dma_start(out=cos4[:], in_=caps["cos4"][:]), slot=0)
    ls = b.ld(lambda e: e.dma_start(out=sin4[:], in_=caps["sin4"][:]), slot=1)
    wq1_b = _cast_load(b, esP, "l1_wq", caps["wq1"], 1024, 1024, None,
                       stages=stg1, slots=[4, 5, 6, 7])
    wk1_b = _cast_load(b, esP, "l1_wk", caps["wk1"], 1024, 256, None,
                       stages=stg1, slots=[4, 5, 6, 7])
    wv1_b = _cast_load(b, esP, "l1_wv", caps["wv1"], 1024, 256, None,
                       stages=stg1, slots=[4, 5, 6, 7])
    P1m = []
    P2m = []
    for v in range(2):
        t1_ = b.sb(f"cs2_P1_{v}", [128, 128], BF16, esP)
        t2_ = b.sb(f"cs2_P2_{v}", [128, 128], BF16, esP)
        b.ld(lambda e, t1_=t1_, v=v: e.dma_start(out=t1_[:], in_=caps[f"P1_{v}"][:]),
             waits=[("ve", b.cnt["ve"])], slot=7)
        b.ld(lambda e, t2_=t2_, v=v: e.dma_start(out=t2_[:], in_=caps[f"P2_{v}"][:]),
             waits=[("ve", b.cnt["ve"])], slot=7)
        P1m.append(t1_)
        P2m.append(t2_)
    ldP = ("ld7", b.cnt["ld7"])
    PL = b.ps("l1_PL", [128, 512], F32, esP)
    PH_ = b.ps("l1_PH", [128, 512], F32, esP)
    PX = b.ps("l1_PX", [128, 1024], F32, esP)
    PNq = b.ps("l1_PNq", [128, 512], F32, esP)
    t1 = b.sb("l1_t1", [128, 512], F32, esP)
    t2 = b.sb("l1_t2", [128, 512], F32, esP)
    rlo = b.sb("l1_rlo", [128, 512], BF16, esP)
    rhi = b.sb("l1_rhi", [128, 512], BF16, esP)
    q64 = b.sb("l1_q64", [128, 512], BF16, esP)
    sqr = b.sb("l1_sqr", [128, 512], F32, esP)
    nr1 = b.sb("l1_nr1", [128, 1], F32, esP)

    def rope_perm_block(jp, w, lo_cols, hi_cols, wmat, qdst, Nacc, ncol, wstride):
        pe_l = pe_h = None
        for ci in range(NC):
            pe_l = b.pe(lambda e, ci=ci, w=w, lo_cols=lo_cols, ws_=wstride,
                        wmat=wmat: e.matmul(
                PL[:], wmat[:, ws_ * ci + lo_cols:ws_ * ci + lo_cols + 128],
                xnT1[ci][:, 1 + 512 * w:1 + 512 * w + 512],
                start=(ci == 0), stop=(ci == NC - 1)),
                waits=[("ve", b.cnt["ve"])] if ci == 0 else (),
                inc="pe" if ci == NC - 1 else None)
        for ci in range(NC):
            pe_h = b.pe(lambda e, ci=ci, w=w, hi_cols=hi_cols, ws_=wstride,
                        wmat=wmat: e.matmul(
                PH_[:], wmat[:, ws_ * ci + hi_cols:ws_ * ci + hi_cols + 128],
                xnT1[ci][:, 1 + 512 * w:1 + 512 * w + 512],
                start=(ci == 0), stop=(ci == NC - 1)),
                inc="pe" if ci == NC - 1 else None)
        cw = [lc, ls, ldP] if (jp == 0 and w == 0) else []
        b.ve(lambda e, w=w: e.tensor_tensor(t1[:], PL[:], cos4[:, 512 * w:512 * w + 512],
                                            OP.mult), waits=[("pe", pe_l)] + cw)
        b.ve(lambda e, w=w: e.tensor_tensor(t2[:], PH_[:], sin4[:, 512 * w:512 * w + 512],
                                            OP.mult), waits=[("pe", pe_h)])
        b.ve(lambda e: e.tensor_tensor(rlo[:], t1[:], t2[:], OP.subtract))
        b.ve(lambda e, w=w: e.tensor_tensor(t1[:], PL[:], sin4[:, 512 * w:512 * w + 512],
                                            OP.mult))
        b.ve(lambda e, w=w: e.tensor_tensor(t2[:], PH_[:], cos4[:, 512 * w:512 * w + 512],
                                            OP.mult))
        vr = b.ve(lambda e: e.tensor_tensor(rhi[:], t1[:], t2[:], OP.add))
        mzn = b.ve(lambda e: e.memset(PNq[:], 0.0), waits=[("pe", b.cnt["pe"])])
        sq_l = b.ve(lambda e: e.tensor_tensor(sqr[:], rlo[:], rlo[:], OP.mult))
        pn = None
        for i in range(4):
            pn = b.pe(lambda e, i=i: e.matmul(
                PNq[32 * i:32 * i + 1, :], consts["onescol_f"][32 * i:32 * i + 32, :],
                sqr[32 * i:32 * i + 32, :], start=(i == 0), stop=False,
                tile_position=(32 * i, 32 * i), skip_group_check=True),
                waits=[("ve", sq_l), ("ve", mzn)] if i == 0 else (),
                inc="pe" if i == 3 else None)
        sq_h = b.ve(lambda e: e.tensor_tensor(sqr[:], rhi[:], rhi[:], OP.mult),
                    waits=[("pe", pn)])
        for i in range(4):
            pn = b.pe(lambda e, i=i: e.matmul(
                PNq[32 * i:32 * i + 1, :], consts["onescol_f"][32 * i:32 * i + 32, :],
                sqr[32 * i:32 * i + 32, :], start=False, stop=(i == 3),
                tile_position=(32 * i, 32 * i), skip_group_check=True),
                waits=[("ve", sq_h)] if i == 0 else (),
                inc="pe" if i == 3 else None)
        b.ve(lambda e: e.reduce_max(nr1[:], PNq[:], axis=AX.X), waits=[("pe", pn)])
        vq = b.ve(lambda e, ncol=ncol: e.tensor_tensor(
            Nacc[:, ncol:ncol + 1], Nacc[:, ncol:ncol + 1], nr1[:], OP.max))
        for v in range(2):
            pp = b.pe(lambda e, v=v: e.matmul(
                PX[:, 512 * v:512 * v + 512], P1m[v][:], rlo[:],
                start=True, stop=False), waits=[("ve", vq)] if v == 0 else (),
                inc=None)
            pp = b.pe(lambda e, v=v: e.matmul(
                PX[:, 512 * v:512 * v + 512], P2m[v][:], rhi[:],
                start=False, stop=True), inc="pe")
            qdst(v, pp)

    st_q = {"n": 0}

    def qdst_q(jp, w):
        def f(v, pp):
            u = 2 * jp + v
            vv = b.ve(lambda e, v=v: e.tensor_copy(q64[:], PX[:, 512 * v:512 * v + 512]),
                      waits=[("pe", pp)])
            b.st(lambda e, u=u, w=w: e.dma_start(
                out=qT1f[128 * u:128 * u + 128, 512 * w:512 * w + 512], in_=q64[:]),
                waits=[("ve", vv)], slot=st_q["n"] % 2)
            st_q["n"] += 1
            b.emit("ve", lambda e: e.nop(),
                   waits=[("st0", b.cnt["st0"]), ("st1", b.cnt["st1"])])
        return f

    def qdst_k(w):
        def f(v, pp):
            b.ve(lambda e, v=v, w=w: e.tensor_copy(
                kT64[v][:, 512 * w:512 * w + 512], PX[:, 512 * v:512 * v + 512]),
                waits=[("pe", pp)])
        return f

    for jp in range(4):
        for w in range(NW):
            rope_perm_block(jp, w, 128 * jp, 512 + 128 * jp, wq1_b,
                            qdst_q(jp, w), Qn, jp, 1024)
    for w in range(NW):
        rope_perm_block(0, w, 0, 128, wk1_b, qdst_k(w), Kn, 0, 256)
    vv = b.cnt["ve"]
    for t in range(NT):
        pe_v = None
        for ci in range(NC):
            pe_v = b.pe(lambda e, ci=ci, t=t: e.matmul(
                PL[:, 0:256], xnT1[ci][:, 1 + 128 * t:1 + 128 * t + 128],
                wv1_b[:, 256 * ci:256 * ci + 256],
                start=(ci == 0), stop=(ci == NC - 1)),
                waits=[("ve", vv)] if ci == 0 else (),
                inc="pe" if ci == NC - 1 else None)
        vv = b.ve(lambda e, t=t: e.tensor_copy(V1[:, 256 * t:256 * t + 256],
                                               PL[:, 0:256]), waits=[("pe", pe_v)])
    _build_shift_bias(b, esP, consts, Qn, Kn, nrm_dram, cb, "l1a", extra_scale=0.125)
    esP.close()

    aoT1 = [b.sb(f"l1_aoT{t}", [128, T], BF16, es_main, side="right") for t in range(8)]
    esA = b.scope()
    PS = b.ps("l1_PS", [128, 2048], F32, esA)
    PAV = b.ps("l1_PAV", [128, 512], F32, esA)
    PSMn = b.ps("l1_PSM", [128, 512], F32, esA)
    PB2 = b.ps("l1_PB2", [128, 512], F32, esA)
    PT = b.sb("l1_PT", [128, 2048], BF16, esA)
    qw64 = [b.sb(f"l1_qw64_{i}", [128, 512], BF16, esA) for i in range(2)]
    sums_sb = b.sb("l1_sums", [128, 512], F32, esA)
    b.ve(lambda e: e.memset(sums_sb[:], 0.0))
    pavc = b.sb("l1_pavc", [128, 512], F32, esA)
    bcast_sb = b.sb("l1_bc", [128, 512], F32, esA)
    recipb = b.sb("l1_recip", [128, 512], F32, esA)
    EL1 = []
    for nm in ("Ep1", "Ep2"):
        et = b.sb("cs2_" + nm, [128, 128], F32, esA)
        b.ld(lambda e, et=et, nm=nm: e.dma_start(out=et[:], in_=caps[nm][:]),
             waits=[("ve", b.cnt["ve"])], slot=7)
        EL1.append(et)
    b.emit("ve", lambda e: e.nop(), waits=[("ld7", b.cnt["ld7"])])

    wins = [(jp, v, w) for jp in range(4) for v in range(2) for w in range(NW)]
    ps_free = [b.cnt["act"], b.cnt["act"]]
    pt_free = [b.cnt["pe"], b.cnt["pe"]]
    qw_free = [b.cnt["pe"], b.cnt["pe"]]
    pav_ready = b.cnt["ve"]
    psm_ready = b.cnt["ve"]
    lq = {}

    def issue_lq(idx):
        if idx >= len(wins):
            return
        jp, v, w = wins[idx]
        qb_ = idx % 2
        u = 2 * jp + v
        lq[idx] = b.ld(lambda e, u=u, w=w, qb_=qb_: e.dma_start(
            out=qw64[qb_][:], in_=qT1f[128 * u:128 * u + 128, 512 * w:512 * w + 512]),
            waits=[("pe", qw_free[qb_])], slot=qb_)

    issue_lq(0)
    for idx, (jp, v, w) in enumerate(wins):
        qb_ = idx % 2
        mz = b.ve(lambda e: e.memset(PSMn[:], 0.0), waits=[("pe", min(pt_free)),
                                                           ("ve", psm_ready)])
        mz = b.ve(lambda e: e.memset(PAV[:], 0.0), waits=[("ve", pav_ready)])
        pending = None

        def mk_av(v, kt, bb, exp_t, last, mzw):
            def emit_av():
                r_ = None
                for ii in range(2):
                    g = 2 * v + ii
                    wl = list(mzw) if ii == 0 else []
                    if ii == 0:
                        wl.append(exp_t)
                    b.pe(lambda e, kt=kt, g=g, ii=ii, bb=bb, last=last: e.matmul(
                        PAV[64 * ii:64 * ii + 64, :],
                        V1[:, 256 * kt + 64 * g:256 * kt + 64 * g + 64],
                        PT[:, 1024 * bb + 512 * ii:1024 * bb + 512 * ii + 512],
                        start=(kt == 0 and ii == 0), stop=last,
                        tile_position=(0, 64 * ii), skip_group_check=True),
                        waits=wl)
                for ii in range(2):
                    g = 2 * v + ii
                    r_ = b.pe(lambda e, kt=kt, g=g, ii=ii, bb=bb, last=last: e.matmul(
                        PSMn[32 * g:32 * g + 1, :], consts["onescol_bf"][:],
                        PT[:, 1024 * bb + 512 * ii:1024 * bb + 512 * ii + 512],
                        start=(kt == 0 and ii == 0), stop=last,
                        tile_position=(0, 32 * g), skip_group_check=True),
                        inc="pe" if ii == 1 else None)
                pt_free[bb] = r_
            return emit_av

        for kt in range(NT):
            bb = kt % 2
            pe_s = None
            for ii in range(2):
                wl = [("act", ps_free[bb]), lq[idx]] if ii == 0 else []
                if ii == 0 and kt == 0:
                    wl.append(("ve", pav_ready))
                pe_s = b.pe(lambda e, kt=kt, v=v, ii=ii, bb=bb, qb_=qb_: e.matmul(
                    PS[:, 1024 * bb + 512 * ii:1024 * bb + 512 * ii + 512],
                    kT64[v][64 * ii:64 * ii + 64, 128 * kt:128 * kt + 128],
                    qw64[qb_][64 * ii:64 * ii + 64, :],
                    start=True, stop=True, tile_position=(64 * ii, 0)),
                    waits=wl, inc="pe" if ii == 1 else None)
            qw_free[qb_] = pe_s
            if kt == 2:
                issue_lq(idx + 1)
            exp_t = b.act(lambda e, bb=bb: e.activation(
                PT[:, 1024 * bb:1024 * bb + 1024],
                PS[:, 1024 * bb:1024 * bb + 1024], AF.Exp, bias=cb[:], scale=0.125),
                waits=[("pe", pe_s), ("pe", pt_free[bb])])
            ps_free[bb] = exp_t
            if pending is not None:
                pending()
            pending = mk_av(v, kt, bb, ("act", exp_t), kt == NT - 1,
                            [("ve", mz)] if kt == 0 else [])
        pending()
        pv = b.ve(lambda e: e.tensor_copy(pavc[:], PAV[:]),
                  waits=[("pe", pt_free[(NT - 1) % 2])])
        sv = None
        for ii in range(2):
            g = 2 * v + ii
            sv = b.ve(lambda e, g=g: e.tensor_copy(
                sums_sb[32 * g:32 * g + 1, :], PSMn[32 * g:32 * g + 1, :]))
        pav_ready = pv
        psm_ready = sv
        pb_ = b.pe(lambda e, v=v: e.matmul(PB2[:], EL1[v][:], sums_sb[:],
                                           start=True, stop=True),
                   waits=[("ve", sv)], inc="pe")
        b.ve(lambda e: e.tensor_copy(bcast_sb[:], PB2[:]), waits=[("pe", pb_)])
        b.ve(lambda e: e.reciprocal(recipb[:], bcast_sb[:]))
        b.ve(lambda e, jp=jp, v=v, w=w: e.tensor_tensor(
            aoT1[2 * jp + v][0:64, 512 * w:512 * w + 512], pavc[0:64, :],
            recipb[0:64, :], OP.mult))
        b.ve(lambda e, jp=jp, v=v, w=w: e.tensor_tensor(
            aoT1[2 * jp + v][64:128, 512 * w:512 * w + 512], pavc[64:128, :],
            recipb[64:128, :], OP.mult))
    esA.close()
    es_kv.close()

    esO = b.scope()
    stgF = [b.sb(f"l1_stage3{i}", [128, 1024], F32, esO) for i in range(4)]
    wo1_b = _cast_load(b, esO, "l1_wo", caps["wo1"], 1024, 1024, None,
                       stages=stgF, slots=[4, 5, 6, 7])
    PO = b.ps("l1_PO", [128, 1024], F32, esO)
    PTf = [b.ps(f"l1_PTf{i}", [128, 128], F32, esO) for i in range(2)]
    of32 = [b.sb(f"l1_of32_{i}", [128, 512], F32, esO) for i in range(2)]
    obuf = b.sb("l1_obuf", [128, 4096], F32, esO)

    ptf_evac = [b.cnt["ve"], b.cnt["ve"]]
    po_free = [b.cnt["ve"], b.cnt["ve"]]
    for w in range(NW):
        for do in range(NC):
            bb = do % 2
            pe_o = None
            for t in range(8):
                pe_o = b.pe(lambda e, w=w, do=do, t=t, bb=bb: e.matmul(
                    PO[:, 512 * bb:512 * bb + 512],
                    wo1_b[:, 1024 * t + 128 * do:1024 * t + 128 * do + 128],
                    aoT1[t][:, 512 * w:512 * w + 512],
                    start=(t == 0), stop=(t == 7)),
                    waits=[("ve", po_free[bb])] if t == 0 else (),
                    inc="pe" if t == 7 else None)
            po_free[bb] = b.ve(lambda e, bb=bb: e.tensor_copy(
                of32[bb][:], PO[:, 512 * bb:512 * bb + 512]), waits=[("pe", pe_o)])
            for tt in range(4):
                pb = tt % 2
                tp = b.pe(lambda e, tt=tt, pb=pb, bb=bb: e.transpose(
                    PTf[pb][:], of32[bb][:, 128 * tt:128 * tt + 128],
                    consts["identf"][:]),
                    waits=[("ve", po_free[bb]), ("ve", ptf_evac[pb])], inc="pe")
                ptf_evac[pb] = b.ve(lambda e, tt=tt, do=do, pb=pb: e.tensor_copy(
                    obuf[:, 1024 * tt + 128 * do:1024 * tt + 128 * do + 128],
                    PTf[pb][:]), waits=[("pe", tp)])
        for tt in range(4):
            b.st(lambda e, w=w, tt=tt: e.dma_start(
                out=a1_dram[512 * w + 128 * tt:512 * w + 128 * tt + 128, :],
                in_=obuf[:, 1024 * tt:1024 * tt + 1024]),
                waits=[("ve", max(ptf_evac))], slot=tt)
        nv = b.emit("ve", lambda e: e.nop(),
                    waits=[(f"st{i}", b.cnt[f"st{i}"]) for i in range(4)])
        ptf_evac = [nv, nv]
    esO.close()
    es_main.close()
    return aoT1


def emit_l1ffn_final(b, consts, caps, xnT1, x1_dram, a1_dram, out_dram):
    es = b.scope()
    mur = b.sb("g_mur", [128, 8], F32, es)
    muk = b.sb("g_muk", [128, 8], F32, es)
    l1_ = b.ld(lambda e: e.dma_start(out=mur[:], in_=caps["mur"][:]),
               waits=[("ve", b.cnt["ve"])], slot=0)
    l2_ = b.ld(lambda e: e.dma_start(out=muk[:], in_=caps["muk"][:]),
               waits=[("ve", b.cnt["ve"])], slot=1)
    dbuf = b.sb("g_d", [128, 512], F32, es)
    xrb = b.sb("g_xrb", [128, 4096], BF16, es)
    xkb = b.sb("g_xkb", [128, 4096], BF16, es)
    tsb = b.sb("g_t", [128, 4096], F32, es)
    kab = b.sb("g_ka", [128, 4096], BF16, es)
    kaf = b.sb("g_kaf", [128, 512], F32, es)
    of32 = b.sb("g_of32", [128, 512], F32, es)
    obuf = b.sb("g_obuf", [128, 4096], F32, es)
    x1s = [b.sb(f"g_x1{i}", [128, D], F32, es) for i in range(2)]
    a1s = [b.sb(f"g_a1{i}", [128, D], F32, es) for i in range(2)]
    outbs = [b.sb(f"g_outb{i}", [128, D], F32, es) for i in range(2)]
    PR = b.ps("g_PR", [128, 512], F32, es)
    PK = b.ps("g_PK", [128, 512], F32, es)
    PV = b.ps("g_PV", [128, 512], F32, es)
    PTf = [b.ps(f"g_PTf{i}", [128, 128], F32, es) for i in range(2)]

    castc = [b.cnt["ve"]] * 4
    wstages = [x1s[0], x1s[1], a1s[0], a1s[1]]

    def wload(name, cap, scale=None):
        out = b.sb(name + "_b", [128, 8192], BF16, es)
        for c in range(8):
            sbf = c % 4
            ld = b.ld(lambda e, c=c, sbf=sbf, cap=cap: e.dma_start(
                out=wstages[sbf][:], in_=cap[128 * c:128 * c + 128, :]),
                waits=[("ve", castc[sbf])], slot=4 + sbf)
            if scale is None:
                castc[sbf] = b.ve(lambda e, c=c, sbf=sbf, out=out: e.tensor_copy(
                    out[:, 1024 * c:1024 * c + 1024], wstages[sbf][:]), waits=[ld])
            else:
                castc[sbf] = b.ve(lambda e, c=c, sbf=sbf, out=out: e.tensor_scalar(
                    out[:, 1024 * c:1024 * c + 1024], wstages[sbf][:], scale,
                    None, OP.mult), waits=[ld])
        return out

    wrT_b = wload("g_wr", caps["wrT"])
    wkT_b = wload("g_wk", caps["wkT1"])
    wvT_b = wload("g_wv", caps["wvT1"], scale=0.5)
    mu_w = b.ve(lambda e: e.nop(), waits=[l1_, l2_])
    ptf_evac = [b.cnt["ve"], b.cnt["ve"]]
    lastx = {0: mu_w, 1: mu_w}
    lasta = {0: mu_w, 1: mu_w}
    outg = {0: ("st0", 0), 1: ("st1", 0)}
    stg = [0, 0, 0, 0]
    pload = {}
    vmix = 0
    kafmax = 0
    kafmult = 0
    sttc = 0

    def mix_chunk(w, c, pe_guard):
        wl = []
        if c == 0:
            wl = [("pe", pe_guard)]
            if w == 0:
                wl.append(("ve", mu_w))
        b.ve(lambda e, c=c, w=w: e.tensor_tensor(
            dbuf[:], xnT1[c][:, 1 + 512 * w:1 + 512 * w + 512],
            xnT1[c][:, 512 * w:512 * w + 512], OP.subtract),
            waits=wl)
        b.ve(lambda e, c=c, w=w: e.scalar_tensor_tensor(
            xrb[:, 512 * c:512 * c + 512], dbuf[:], mur[:, c:c + 1],
            xnT1[c][:, 512 * w:512 * w + 512], OP.mult, OP.add))
        return b.ve(lambda e, c=c, w=w: e.scalar_tensor_tensor(
            xkb[:, 512 * c:512 * c + 512], dbuf[:], muk[:, c:c + 1],
            xnT1[c][:, 512 * w:512 * w + 512], OP.mult, OP.add))

    def prefetch(wa, tt):
        t = 4 * wa + tt
        buf = tt % 2
        lx = b.ld(lambda e, t=t, buf=buf: e.dma_start(
            out=x1s[buf][:], in_=x1_dram[128 * t:128 * t + 128, :]),
            waits=[("ve", lastx[buf])], slot=buf)
        la = b.ld(lambda e, t=t, buf=buf: e.dma_start(
            out=a1s[buf][:], in_=a1_dram[128 * t:128 * t + 128, :]),
            waits=[("ve", lasta[buf])], slot=2 + buf)
        pload[(wa, tt)] = (lx, la)

    def emit_assemble(wa):
        for tt in range(4):
            t = 4 * wa + tt
            buf = tt % 2
            lx, la = pload.pop((wa, tt))
            a1c = b.ve(lambda e, tt=tt, buf=buf: e.tensor_tensor(
                outbs[buf][:], obuf[:, 1024 * tt:1024 * tt + 1024], x1s[buf][:],
                OP.add), waits=[lx, outg[buf]])
            lastx[buf] = a1c
            a2c = b.ve(lambda e, buf=buf: e.tensor_tensor(
                outbs[buf][:], outbs[buf][:], a1s[buf][:], OP.add), waits=[la])
            lasta[buf] = a2c
            sret = b.st(lambda e, t=t, buf=buf: e.dma_start(
                out=out_dram[128 * t:128 * t + 128, :], in_=outbs[buf][:]),
                waits=[("ve", a2c), (f"st{tt}", stg[tt])], slot=tt)
            stg[tt] = sret[1]
            outg[buf] = sret
            if tt < 2:
                prefetch(wa, tt + 2)

    for w in range(NW):
        if w == 0:
            for c in range(NC):
                vmix = mix_chunk(0, c, b.cnt["pe"])
        else:
            emit_assemble(w - 1)
        for do in range(NC):
            pe_r = None
            for ci in range(NC):
                pe_r = b.pe(lambda e, ci=ci, do=do: e.matmul(
                    PR[:], wrT_b[:, 1024 * ci + 128 * do:1024 * ci + 128 * do + 128],
                    xrb[:, 512 * ci:512 * ci + 512],
                    start=(ci == 0), stop=(ci == NC - 1)),
                    waits=[("ve", vmix), ("act", b.cnt["act"])] if ci == 0 else (),
                    inc="pe" if ci == NC - 1 else None)
            b.act(lambda e, do=do: e.activation(tsb[:, 512 * do:512 * do + 512], PR[:],
                                                AF.Tanh, bias=0.0, scale=0.5),
                  waits=[("pe", pe_r)])
            kw = vmix if do == 0 else kafmax
            pe_k = None
            for ci in range(NC):
                pe_k = b.pe(lambda e, ci=ci, do=do: e.matmul(
                    PK[:], wkT_b[:, 1024 * ci + 128 * do:1024 * ci + 128 * do + 128],
                    xkb[:, 512 * ci:512 * ci + 512],
                    start=(ci == 0), stop=(ci == NC - 1)),
                    waits=[("ve", kw)] if ci == 0 else (),
                    inc="pe" if ci == NC - 1 else None)
            kafmax = b.ve(lambda e: e.tensor_scalar(kaf[:], PK[:], 0.0, None, OP.max),
                          waits=[("pe", pe_k)])
            kafmult = b.ve(lambda e, do=do: e.tensor_tensor(
                kab[:, 512 * do:512 * do + 512], kaf[:], kaf[:], OP.mult))
        aw = b.cnt["act"]
        pe_guard_mix = b.cnt["pe"]
        for do in range(NC):
            pvw = kafmult if do == 0 else sttc
            pe_v = None
            for ci in range(NC):
                pe_v = b.pe(lambda e, ci=ci, do=do: e.matmul(
                    PV[:], wvT_b[:, 1024 * ci + 128 * do:1024 * ci + 128 * do + 128],
                    kab[:, 512 * ci:512 * ci + 512],
                    start=(ci == 0), stop=(ci == NC - 1)),
                    waits=[("ve", pvw)] if ci == 0 else (),
                    inc="pe" if ci == NC - 1 else None)
            sttc = b.ve(lambda e, do=do: e.scalar_tensor_tensor(
                of32[:], tsb[:, 512 * do:512 * do + 512], 1.0, PV[:],
                OP.add, OP.mult), waits=[("pe", pe_v), ("act", aw)])
            for tt in range(4):
                pb = tt % 2
                tp = b.pe(lambda e, tt=tt, pb=pb: e.transpose(
                    PTf[pb][:], of32[:, 128 * tt:128 * tt + 128], consts["identf"][:]),
                    waits=[("ve", sttc), ("ve", ptf_evac[pb])], inc="pe")
                ptf_evac[pb] = b.ve(lambda e, tt=tt, do=do, pb=pb: e.tensor_copy(
                    obuf[:, 1024 * tt + 128 * do:1024 * tt + 128 * do + 128],
                    PTf[pb][:]), waits=[("pe", tp)])
            if w + 1 < NW:
                vmix = mix_chunk(w + 1, do, pe_guard_mix)
            if do in (4, 6):
                prefetch(w, (do - 4) // 2)
    emit_assemble(NW - 1)
    es.close()



def _consts_host():
    c = {}
    c["ident"] = np.eye(128, dtype=BF)
    c["identf"] = np.eye(128, dtype=np.float32)
    ones = np.ones((128, 1), dtype=BF)
    c["onescol_bf"] = ones
    c["onescol_f"] = np.ones((128, 1), dtype=np.float32)
    c["onesrow_f"] = np.ones((1, 128), dtype=np.float32)
    for d in range(4):
        p = np.arange(128)[:, None]
        f = np.arange(512)[None, :]
        c[f"mask{d}"] = ((128 * d + p) <= f).astype(BF)
        c[f"mask2_{d}"] = np.concatenate([c[f"mask{d}"], c[f"mask{d}"]], axis=1)
    epair = np.zeros((128, 128), dtype=np.float32)
    epair[32, 0:32] = 1.0
    epair[96, 64:96] = 1.0
    c["Epair"] = epair
    e4 = np.zeros((128, 128), dtype=np.float32)
    for m in range(128):
        e4[32 * (m // 32), m] = 1.0
    c["E4s"] = e4
    er0 = np.zeros((128, 128), dtype=np.float32)
    er0[0, :] = 1.0
    c["Erow0"] = er0
    ep1 = np.zeros((128, 128), dtype=np.float32)
    ep1[0, 0:64] = 1.0
    ep1[32, 64:128] = 1.0
    c["Ep1"] = ep1
    ep2 = np.zeros((128, 128), dtype=np.float32)
    ep2[64, 0:64] = 1.0
    ep2[96, 64:128] = 1.0
    c["Ep2"] = ep2
    for v in range(2):
        p1 = np.zeros((128, 128), dtype=BF)
        p2 = np.zeros((128, 128), dtype=BF)
        for d in range(32):
            p1[64 * v + d, d] = 1
            p2[64 * v + d, 32 + d] = 1
            p1[64 * v + 32 + d, 64 + d] = 1
            p2[64 * v + 32 + d, 96 + d] = 1
        c[f"P1_{v}"] = p1
        c[f"P2_{v}"] = p2
    half = 32
    inv = 1.0 / (10000.0 ** (np.arange(half, dtype=np.float64) * 2.0 / 64.0))
    pos = np.arange(T, dtype=np.float64)
    fr = pos[:, None] * inv[None, :]
    cos = np.cos(fr).astype(np.float32).T
    sin = np.sin(fr).astype(np.float32).T
    c["cos4"] = np.tile(cos, (4, 1))
    c["sin4"] = np.tile(sin, (4, 1))
    return c


def _declare_consts(nc, host):
    aps = {}
    for k, v in host.items():
        dt = BF16 if v.dtype == BF else F32
        aps[k] = nc.dram_tensor("c_" + k, list(v.shape), dt, kind="ExternalInput").ap()
    return aps


def load_const(b, caps, name, shape, dt):
    t = b.sb("cs_" + name, shape, dt)
    ld = b.ld(lambda e: e.dma_start(out=t[:], in_=caps[name][:]), slot=7)
    return t, ld


def build_program():
    nc = bass.Bass("TRN2", target_bir_lowering=False)
    x = nc.dram_tensor("x", [T, D], F32, kind="ExternalInput").ap()
    host_consts = _consts_host()
    caps = _declare_consts(nc, host_consts)
    caps["wq"] = nc.dram_tensor("wq", [D, D], F32, kind="ExternalInput").ap()
    caps["wkv"] = nc.dram_tensor("wkv", [D, 128], F32, kind="ExternalInput").ap()
    caps["wk"] = nc.dram_tensor("wk", [128, 512], F32, kind="ExternalInput").ap()
    caps["wv"] = nc.dram_tensor("wv", [128, 512], F32, kind="ExternalInput").ap()
    caps["wosel"] = nc.dram_tensor("wosel", [D, D], F32, kind="ExternalInput").ap()
    caps["qgv"] = nc.dram_tensor("qgv", [1, 64], F32, kind="ExternalInput").ap()
    caps["kgv"] = nc.dram_tensor("kgv", [1, 32], F32, kind="ExternalInput").ap()
    for nm in ("qwT", "kwT", "vwT", "owT"):
        caps[nm] = nc.dram_tensor(nm, [D, D], F32, kind="ExternalInput").ap()
    for nm in ("qb", "kb", "ob"):
        caps[nm] = nc.dram_tensor(nm, [128, 8], F32, kind="ExternalInput").ap()
    caps["vb"] = nc.dram_tensor("vb", [1, D], F32, kind="ExternalInput").ap()
    caps["nw"] = nc.dram_tensor("nw", [1, D], F32, kind="ExternalInput").ap()
    caps["nb"] = nc.dram_tensor("nb", [1, D], F32, kind="ExternalInput").ap()
    caps["wq1"] = nc.dram_tensor("wq1", [D, D], F32, kind="ExternalInput").ap()
    caps["wk1"] = nc.dram_tensor("wk1", [D, 256], F32, kind="ExternalInput").ap()
    caps["wv1"] = nc.dram_tensor("wv1", [D, 256], F32, kind="ExternalInput").ap()
    caps["wo1"] = nc.dram_tensor("wo1", [D, D], F32, kind="ExternalInput").ap()
    for nm in ("wrT", "wkT1", "wvT1"):
        caps[nm] = nc.dram_tensor(nm, [D, D], F32, kind="ExternalInput").ap()
    caps["mur"] = nc.dram_tensor("mur", [128, 8], F32, kind="ExternalInput").ap()
    caps["muk"] = nc.dram_tensor("muk", [128, 8], F32, kind="ExternalInput").ap()
    nrm_dram = nc.dram_tensor("nrm", [1, 4096], F32, kind="Internal").ap()
    qT0f = nc.dram_tensor("qT0f", [D, T], BF16, kind="Internal").ap()
    kT0f = nc.dram_tensor("kT0f", [D, T], BF16, kind="Internal").ap()
    qT1f = nc.dram_tensor("qT1f", [D, T], BF16, kind="Internal").ap()
    aoT0f = nc.dram_tensor("aoT0f", [D, T], BF16, kind="Internal").ap()
    upto = _DBG["upto"] or "Z"
    kd = "ExternalOutput" if _DBG["dump"] else "Internal"
    a0_dram = nc.dram_tensor("a0", [T, D], F32, kind=kd).ap()
    f0_dram = nc.dram_tensor("f0", [T, D], F32, kind=kd).ap()
    x1_dram = nc.dram_tensor("x1", [T, D], F32, kind=kd).ap()
    a1_dram = nc.dram_tensor("a1", [T, D], F32, kind=kd).ap()
    out_dram = nc.dram_tensor("out", [T, D], F32, kind="ExternalOutput").ap()

    dbg = {}
    if _DBG["dump"]:
        dbg["xnT"] = nc.dram_tensor("dbg_xnT", [NC * 128, T], BF16, kind="ExternalOutput").ap()
        dbg["stats"] = nc.dram_tensor("dbg_stats", [128, 64], F32, kind="ExternalOutput").ap()

    b = Bld(nc)
    with b.ctx:
        consts = {}
        for nm in ("ident", "identf", "onescol_bf", "onescol_f", "onesrow_f"):
            consts[nm], _ = load_const(b, caps, nm, list(_chost_shape(host_consts[nm])),
                                       BF16 if host_consts[nm].dtype == BF else F32)
        consts["eps5"] = b.fconst(1e-5)
        consts["eps6"] = b.fconst(1e-6)
        b.barrier()

        stats0 = b.sb("stats0", [128, 64], F32)
        es_xnT = b.scope()
        xnT_all = b.sb("xnT_all", [128, NC * T], BF16, es_xnT, side="right")
        xnT = [xnT_all[:, c * T:(c + 1) * T] for c in range(NC)]
        emit_ln_xnT(b, consts, x, stats0, xnT, ext=False, xnT_all=xnT_all)

        if _DBG["dump"]:
            for c in range(NC):
                b.st(lambda e, c=c: e.dma_start(out=dbg["xnT"][128 * c:128 * c + 128, :],
                                                in_=xnT[c][:]),
                     waits=[("ve", b.cnt["ve"])], slot=c % 4)
            b.st(lambda e: e.dma_start(out=dbg["stats"][:], in_=stats0[:]),
                 waits=[("ve", b.cnt["ve"])], slot=3)
            b.barrier()

        if upto.startswith("B") or upto >= "C":
            emit_l0a(b, consts, caps, xnT, es_xnT, a0_dram, nrm_dram)
        else:
            es_xnT.close()

        if upto >= "C" and not upto.startswith("B"):
            emit_l0ffn(b, consts, caps, x, stats0, f0_dram, qT0f, kT0f, aoT0f)

        if upto >= "D" and not upto.startswith("B"):
            stats1 = b.sb("stats1", [128, 64], F32)
            es_xnT1 = b.scope()
            xnT1_all = b.sb("xnT1_all", [128, NC * (T + 1)], BF16, es_xnT1,
                            side="right")
            xnT1 = [xnT1_all[:, c * (T + 1):(c + 1) * (T + 1)] for c in range(NC)]
            emit_assembly_ln(b, consts, x, a0_dram, f0_dram, stats0, True,
                             x1_dram, stats1, xnT1, xnT1_all)
        if upto >= "F" and not upto.startswith("B"):
            emit_l1a(b, consts, caps, xnT1, qT1f, a1_dram, nrm_dram)
        if upto >= "G" and not upto.startswith("B"):
            emit_l1ffn_final(b, consts, caps, xnT1, x1_dram, a1_dram, out_dram)
        if upto >= "D" and not upto.startswith("B"):
            es_xnT1.close()

        b.emit("gp", lambda e: e.nop(),
               waits=[(f"st{i}", b.cnt[f"st{i}"]) for i in range(4)])
        b.validate_counts()
        with nc.Block() as blk:
            b.replay(blk)
    return nc, host_consts


def _chost_shape(v):
    return v.shape


def _host_inputs(inputs, core):
    m = {}
    assert inputs["L0_ffn_router_prob"].shape[2] == 4
    m["x"] = np.ascontiguousarray(inputs["L0_attn_x"][core], dtype=np.float32)
    m["wq"] = inputs["L0_attn_Wq"]
    m["wkv"] = inputs["L0_attn_Wkv"]
    m["wk"] = inputs["L0_attn_Wk"]
    m["wv"] = inputs["L0_attn_Wv"]
    wo = inputs["L0_attn_Wo"]
    sel = np.zeros((D, D), dtype=np.float32)
    for j in range(4):
        for p in range(2):
            s_ = 2 * j + p
            h0, h1 = 4 * j + 2 * p, 4 * j + 2 * p + 1
            sel[128 * s_:128 * s_ + 32, :] = wo[64 * h0:64 * h0 + 32, :]
            sel[128 * s_ + 64:128 * s_ + 96, :] = wo[64 * h1:64 * h1 + 32, :]
    m["wosel"] = sel
    m["qgv"] = np.ascontiguousarray(inputs["L0_attn_q_g"].reshape(1, 64), dtype=np.float32)
    m["kgv"] = np.ascontiguousarray(inputs["L0_attn_k_g"].reshape(1, 32), dtype=np.float32)
    m["qwT"] = np.ascontiguousarray(inputs["L0_ffn_q_w"].T)
    m["kwT"] = np.ascontiguousarray(inputs["L0_ffn_k_w"].T)
    m["vwT"] = np.ascontiguousarray(inputs["L0_ffn_v_w"].T)
    m["owT"] = np.ascontiguousarray(inputs["L0_ffn_o_w"].T)
    for nm, key in (("qb", "L0_ffn_q_b"), ("kb", "L0_ffn_k_b"), ("ob", "L0_ffn_o_b")):
        m[nm] = np.ascontiguousarray(inputs[key].reshape(8, 128).T, dtype=np.float32)
    m["vb"] = np.ascontiguousarray(inputs["L0_ffn_v_b"].reshape(1, D), dtype=np.float32)
    m["nw"] = np.ascontiguousarray(inputs["L0_ffn_norm_w"].reshape(1, D), dtype=np.float32)
    m["nb"] = np.ascontiguousarray(inputs["L0_ffn_norm_b"].reshape(1, D), dtype=np.float32)
    perm_q = []
    for half in range(2):
        for jp in range(4):
            for i in range(4):
                h = 4 * i + jp
                perm_q.extend(range(64 * h + 32 * half, 64 * h + 32 * half + 32))
    m["wq1"] = np.ascontiguousarray(inputs["L1_attn_w_q"][:, perm_q])
    perm_k = []
    for half in range(2):
        for g in range(4):
            perm_k.extend(range(64 * g + 32 * half, 64 * g + 32 * half + 32))
    m["wk1"] = np.ascontiguousarray(inputs["L1_attn_w_k"][:, perm_k])
    m["wv1"] = inputs["L1_attn_w_v"]
    perm_o = []
    for u in range(8):
        jp, v = u // 2, u % 2
        for h in (8 * v + jp, 8 * v + 4 + jp):
            perm_o.extend(range(64 * h, 64 * h + 64))
    m["wo1"] = np.ascontiguousarray(inputs["L1_attn_w_o"][perm_o, :])
    m["wrT"] = np.ascontiguousarray(inputs["L1_ffn_W_r"].T)
    m["wkT1"] = np.ascontiguousarray(inputs["L1_ffn_W_k"].T)
    m["wvT1"] = np.ascontiguousarray(inputs["L1_ffn_W_v"].T)
    m["mur"] = np.ascontiguousarray(inputs["L1_ffn_mu_r"].reshape(8, 128).T,
                                    dtype=np.float32)
    m["muk"] = np.ascontiguousarray(inputs["L1_ffn_mu_k"].reshape(8, 128).T,
                                    dtype=np.float32)
    return m


def kernel(**inputs):
    nc, host_consts = build_program()
    in_maps = []
    for core in range(8):
        m = _host_inputs(inputs, core)
        for k, v in host_consts.items():
            m["c_" + k] = v
        in_maps.append(m)
    res = run_bass_kernel_spmd(nc, in_maps, core_ids=list(range(8)))
    out = np.stack([r["out"] for r in res.results], axis=0)
    return out


# version 19
# speedup vs baseline: 1.0026x; 1.0026x over previous
import numpy as np
import ml_dtypes
from contextlib import ExitStack

import concourse.bass as bass
import concourse.mybir as mybir
from concourse.alu_op_type import AluOpType as OP
from concourse.bass_utils import run_bass_kernel_spmd

F32 = mybir.dt.float32
BF16 = mybir.dt.bfloat16
AF = mybir.ActivationFunctionType
AX = mybir.AxisListType
BF = ml_dtypes.bfloat16

T = 4096
D = 1024
NT = T // 128
NW = T // 512
NC = D // 128

_DBG = {"upto": None, "dump": False}

ENGINES = ("pe", "act", "ve", "sync", "gp")
_ENG_ATTR = {"pe": "tensor", "act": "scalar", "ve": "vector", "sync": "sync", "gp": "gpsimd"}


class Bld:

    def __init__(self, nc):
        self.nc = nc
        self.q = {e: [] for e in ENGINES}
        self.ctx = ExitStack()
        self.sem = {}
        names = ["pe", "act", "ve", "gp"]
        names += [f"ld{i}" for i in range(8)] + [f"st{i}" for i in range(4)]
        for name in names:
            self.sem[name] = self.ctx.enter_context(nc.semaphore("sem_" + name))
        self.cnt = {k: 0 for k in self.sem}
        self.maxwait = {k: 0 for k in self.sem}

    AUTOSYNC = ("act", "ve", "gp")

    def emit(self, eng, fn, waits=(), inc=None, n=1, dma=False):
        waits = list(waits)
        if eng in self.AUTOSYNC and dma:
            waits.append((eng, self.cnt[eng]))
        elif eng in self.AUTOSYNC:
            sname = eng
            waits.append((sname, self.cnt[sname]))
            assert inc in (None, sname)
            inc, n = sname, 1
        elif eng == "pe" and inc == "pe":
            waits.append(("pe", self.cnt["pe"]))
        for s, v in waits:
            self.maxwait[s] = max(self.maxwait[s], v)
        if inc is not None:
            self.cnt[inc] += n
            target = self.cnt[inc]
        else:
            target = None
        incsem = self.sem[inc] if inc is not None else None
        w = [(self.sem[s], v) for s, v in waits]

        def run(e):
            for sh, v in w:
                e.wait_ge(sh, v)
            r = fn(e)
            if incsem is not None:
                r.then_inc(incsem, n)

        self.q[eng].append(run)
        return target

    def pe(self, fn, waits=(), inc=None):
        return self.emit("pe", fn, waits, inc)

    def act(self, fn, waits=(), inc=None):
        return self.emit("act", fn, waits, inc)

    def ve(self, fn, waits=(), inc=None):
        return self.emit("ve", fn, waits, inc)

    def ld(self, fn, waits=(), slot=0):
        s = f"ld{slot % 8}"
        return s, self.emit("sync", fn, waits, s, 16)

    def st(self, fn, waits=(), slot=0):
        s = f"st{slot % 4}"
        return s, self.emit("gp", fn, waits, s, 16, dma=True)

    def gp(self, fn, waits=(), inc=None):
        return self.emit("gp", fn, waits, inc if inc else None)

    def barrier(self):
        snap = {}
        for e, s in (("pe", "pe"), ("act", "act"), ("ve", "ve"), ("gp", "gp")):
            snap[s] = self.emit(e, lambda en: en.nop(), inc=s if e == "pe" else None)
        for i in range(8):
            snap[f"ld{i}"] = self.cnt[f"ld{i}"]
        for i in range(4):
            snap[f"st{i}"] = self.cnt[f"st{i}"]
        for e in ("pe", "act", "ve", "gp", "sync"):
            waits = [(s, snap[s]) for s in snap]

            def mk(waits=waits):
                def run(en):
                    for sname, v in waits:
                        self.maxwait[sname] = max(self.maxwait[sname], v)
                        en.wait_ge(self.sem[sname], v)
                return run
            self.q[e].append(mk())

    def replay(self, blk):
        for e in ENGINES:
            lst = self.q[e]
            if not lst:
                continue
            getattr(blk, _ENG_ATTR[e])(lambda eng, lst=lst: [f(eng) for f in lst])

    def validate_counts(self):
        for s in self.sem:
            assert self.maxwait[s] <= self.cnt[s], (
                f"wait on {s} for {self.maxwait[s]} but final count {self.cnt[s]}")

    def sb(self, name, shape, dt, es=None, side=None):
        kw = {"side": side} if side else {}
        return (es or self.ctx).enter_context(self.nc.sbuf_tensor(name, shape, dt, **kw))

    def ps(self, name, shape, dt, es=None):
        return (es or self.ctx).enter_context(self.nc.psum_tensor(name, shape, dt))

    def scope(self):
        b = self

        class _S(ExitStack):
            def __exit__(self, *a):
                r = ExitStack.__exit__(self, *a)
                b.barrier()
                return r
        return _S()

    def fconst(self, value):
        key = ("fc", value)
        if not hasattr(self, "_fc"):
            self._fc = {}
        if key not in self._fc:
            t = self.sb(f"fc_{len(self._fc)}", [128, 1], F32)
            self.ve(lambda e: e.memset(t[:], float(value)), inc="ve")
            self._fc[key] = t
        return self._fc[key]



def emit_ln_xnT(b, consts, x_dram, stats_sb, xnT, ext=False, stats_dram=None,
                xnT_all=None):
    nc = b.nc
    off = 1 if ext else 0
    es = b.scope()
    tag = x_dram.tensor.name
    xs = [b.sb(f"ln_x{i}_{tag}", [128, D], F32, es) for i in range(4)]
    st6 = b.sb(f"ln_st6_{tag}", [128, 12], F32, es)
    mvs = [b.sb(f"ln_mv{i}_{tag}", [128, 2], F32, es) for i in range(2)]
    lnv = b.sb(f"ln_lnv_{tag}", [128, 1], F32, es)
    xnb = [b.sb(f"ln_xnb{i}_{tag}", [128, D], BF16, es) for i in range(2)]
    ptr = [b.ps(f"ln_pt{i}_{tag}", [128, 1024], BF16, es) for i in range(2)]
    ident = consts["ident"]

    if ext:
        for c in range(NC):
            b.ve(lambda e, c=c: e.memset(xnT[c][:, 0:1], 0.0), inc="ve")

    pe_done = b.cnt["pe"]
    evac_cnt = [b.cnt["ve"], b.cnt["ve"]]
    scal_hist = [b.cnt["ve"]] * 4
    eps5 = consts["eps5"]
    ars = {}
    for tl in range(NT + 1):
        if tl < NT:
            t, buf, nb = tl, tl % 4, tl % 2
            ld_t = b.ld(lambda e, t=t, buf=buf: e.dma_start(
                out=xs[buf][:], in_=x_dram[128 * t:128 * t + 128, :]),
                waits=[("ve", scal_hist[buf])] if t >= 4 else (), slot=buf)
            s1 = b.ve(lambda e, buf=buf: e.bn_stats(st6[:, 0:6], xs[buf][:, 0:512]),
                      waits=[ld_t], inc="ve")
            s2 = b.ve(lambda e, buf=buf: e.bn_stats(st6[:, 6:12], xs[buf][:, 512:1024]),
                      waits=[("ve", s1)], inc="ve")
            ve_mv = b.ve(lambda e, nb=nb: e.bn_aggr(mvs[nb][:], st6[:]),
                         waits=[("ve", s2)], inc="ve")
            b.ve(lambda e, t=t, nb=nb: e.tensor_copy(stats_sb[:, 2 * t:2 * t + 1],
                                                     mvs[nb][:, 0:1]),
                 waits=[("ve", ve_mv)])
            b.act(lambda e, nb=nb: e.activation(lnv[:], mvs[nb][:, 1:2], AF.Ln,
                                                bias=eps5[:], scale=1.0),
                  waits=[("ve", ve_mv)])
            ars[t] = b.act(lambda e, t=t: e.activation(
                stats_sb[:, 2 * t + 1:2 * t + 2], lnv[:], AF.Exp, bias=0.0, scale=-0.5),
                inc="act")
        if tl >= 1:
            t, buf, nb = tl - 1, (tl - 1) % 4, (tl - 1) % 2
            ve_done = b.ve(lambda e, t=t, buf=buf, nb=nb: e.tensor_scalar(
                xnb[nb][:], xs[buf][:], stats_sb[:, 2 * t:2 * t + 1],
                stats_sb[:, 2 * t + 1:2 * t + 2], OP.subtract, OP.mult),
                waits=[("act", ars.pop(t)), ("pe", pe_done)], inc="ve")
            scal_hist[buf] = ve_done
            pb = t % 2
            tp = None
            for c in range(NC):
                w2 = [("ve", ve_done), ("ve", evac_cnt[pb])] if c == 0 else ()
                tp = b.pe(lambda e, c=c, nb=nb, pb=pb: e.transpose(
                    ptr[pb][:, 128 * c:128 * c + 128],
                    xnb[nb][:, 128 * c:128 * c + 128], ident[:]),
                    waits=w2, inc="pe" if c == NC - 1 else None)
            if xnT_all is not None:
                ov = xnT_all.rearrange("p (c t) -> p c t", c=NC)[
                    :, :, off + 128 * t: off + 128 * t + 128]
                evac_cnt[pb] = b.ve(lambda e, ov=ov, pb=pb: e.tensor_copy(
                    ov, ptr[pb][:].rearrange("p (c t) -> p c t", c=NC)),
                    waits=[("pe", tp)])
            else:
                for c in range(NC):
                    evac_cnt[pb] = b.ve(lambda e, c=c, t=t, pb=pb: e.tensor_copy(
                        xnT[c][:, off + 128 * t: off + 128 * t + 128],
                        ptr[pb][:, 128 * c:128 * c + 128]),
                        waits=[("pe", tp)] if c == 0 else ())
            pe_done = tp
    if stats_dram is not None:
        b.st(lambda e: e.dma_start(out=stats_dram[:], in_=stats_sb[:]),
             waits=[("act", b.cnt["act"]), ("ve", b.cnt["ve"])], slot=3)
    es.close()



def _cast_load(b, es, name, dram_ap, rows, cols, stage, slot=6, scale=None,
               stage2=None, slot2=7, stages=None, slots=None):
    n = rows // 128
    out = b.sb(name + "_b", [128, n * cols], BF16, es)
    if stages is None:
        stages = [stage, stage2] if stage2 is not None else [stage]
        slots = [slot, slot2] if stage2 is not None else [slot]
    castc = [b.cnt["ve"]] * len(stages)
    for c in range(n):
        sb_ = c % len(stages)
        stg = stages[sb_]
        ld = b.ld(lambda e, c=c, stg=stg: e.dma_start(
            out=stg[:, 0:cols], in_=dram_ap[128 * c:128 * c + 128, :]),
            waits=[("ve", castc[sb_])], slot=slots[sb_])
        if scale is None:
            castc[sb_] = b.ve(lambda e, c=c, stg=stg: e.tensor_copy(
                out[:, cols * c:cols * c + cols], stg[:, 0:cols]),
                waits=[ld], inc="ve")
        else:
            castc[sb_] = b.ve(lambda e, c=c, stg=stg: e.tensor_scalar(
                out[:, cols * c:cols * c + cols], stg[:, 0:cols], scale, None,
                OP.mult),
                waits=[ld], inc="ve")
    return out


def _build_shift_bias(b, es, consts, Qm2, Km2, nrm_dram, cb, tag, extra_scale=1.0):
    nq = Qm2.shape[1]
    nk = Km2.shape[1]
    v = nrm_dram.rearrange("a (p c) -> (a p) c", p=128)
    st_q = b.st(lambda e: e.dma_start(out=v[:, 0:nq], in_=Qm2[:]),
                waits=[("ve", b.cnt["ve"])], slot=2)
    st_k = b.st(lambda e: e.dma_start(out=v[:, 16:16 + nk], in_=Km2[:]),
                waits=[("ve", b.cnt["ve"])], slot=1)
    flat = b.sb("nrmflat_" + tag, [1, 4096], F32, es)
    ld = b.ld(lambda e: e.dma_start(out=flat[:], in_=nrm_dram[:]),
              waits=[st_q, st_k], slot=6)
    mq = b.sb("nrm_mq_" + tag, [1, 4], F32, es)
    mrow = b.sb("nrm_mrow_" + tag, [1, 256], F32, es)
    fv = flat[:].rearrange("a (p c) -> a p c", c=32)
    b.ve(lambda e: e.reduce_max(mrow[:, 0:128], fv[:, :, 0:nq], axis=AX.X), waits=[ld])
    b.ve(lambda e: e.reduce_max(mq[:, 0:1], mrow[:, 0:128], axis=AX.X))
    b.ve(lambda e: e.reduce_max(mrow[:, 128:256], fv[:, :, 16:16 + nk], axis=AX.X))
    b.ve(lambda e: e.reduce_max(mq[:, 1:2], mrow[:, 128:256], axis=AX.X))
    ve0 = b.ve(lambda e: e.tensor_tensor(mq[:, 2:3], mq[:, 0:1], mq[:, 1:2], OP.mult),
               inc="ve")
    b.act(lambda e: e.activation(mq[:, 3:4], mq[:, 2:3], AF.Ln, bias=0.0, scale=1.0),
          waits=[("ve", ve0)])
    a1 = b.act(lambda e: e.activation(mq[:, 2:3], mq[:, 3:4], AF.Exp, bias=0.0, scale=0.5),
               inc="act")
    ve1 = b.ve(lambda e: e.tensor_scalar(mq[:, 3:4], mq[:, 2:3],
                                         -1.01 * extra_scale, None, OP.mult),
               waits=[("act", a1)], inc="ve")
    pbias = b.ps("pb_" + tag, [128, 8], F32, es)
    p0 = b.pe(lambda e: e.matmul(pbias[:, 0:1], consts["onesrow_f"][:], mq[:, 3:4],
                                 start=True, stop=True), waits=[("ve", ve1)], inc="pe")
    b.ve(lambda e: e.tensor_copy(cb[:], pbias[:, 0:1]), waits=[("pe", p0)], inc="ve")


def emit_l0a(b, consts, caps, xnT, es_xnT, a0_dram, nrm_dram, dbg=None):
    nc = b.nc
    ident = consts["ident"]
    H = 16

    es_main = b.scope()
    es_qkv = b.scope()
    esQ = b.scope()
    qTs_all = b.sb("l0a_qT_all", [128, 4 * T], BF16, es_qkv)
    kTs_all = b.sb("l0a_kT_all", [128, 4 * T], BF16, es_qkv)
    qTs = [qTs_all[:, j * T:(j + 1) * T] for j in range(4)]
    kTs = [kTs_all[:, j * T:(j + 1) * T] for j in range(4)]
    V = b.sb("l0a_V", [128, NT * 528], BF16, es_qkv)
    Qm2 = b.sb("l0a_qm2", [128, 16], F32, es_qkv)
    Km2 = b.sb("l0a_km2", [128, 16], F32, es_qkv)
    kvT = b.sb("l0a_kvT", [128, T], BF16, es_qkv)
    cb = b.sb("l0a_cb", [128, 1], F32, es_qkv)
    b.ve(lambda e: e.memset(Qm2[:], 0.0), inc="ve")
    b.ve(lambda e: e.memset(Km2[:], 0.0), inc="ve")
    b.ve(lambda e: e.memset(V[:], 1.0), inc="ve")

    stage = b.sb("l0a_stage", [128, 512], F32, esQ)
    stageB = b.sb("l0a_stageB", [128, 512], F32, esQ)
    PQ = b.ps("l0a_PQ", [128, 1024], F32, esQ)
    PTx = [b.ps(f"l0a_PTx{i}", [128, 256], BF16, esQ) for i in range(2)]
    qf32s = [b.sb(f"l0a_qf32{i}", [128, 512], F32, esQ) for i in range(3)]
    sqs = [b.sb(f"l0a_sq{i}", [128, 512], F32, esQ) for i in range(3)]
    mss = [b.sb(f"l0a_ms{i}", [128, 8], F32, esQ) for i in range(3)]
    lns = b.sb("l0a_lns", [128, 8], F32, esQ)
    rinvs = [b.sb(f"l0a_rinv{i}", [128, 8], F32, esQ) for i in range(3)]
    qtb = b.sb("l0a_qtb", [128, 256], BF16, esQ)
    eps6 = consts["eps6"]

    ptx_evac = [b.cnt["ve"], b.cnt["ve"]]
    for half in range(2):
        wq_b = _cast_load(b, esQ, f"l0a_wq{half}",
                          caps["wq"][:, 512 * half:512 * half + 512], 1024, 512, stage,
                          stage2=stageB)
        pq_free = [b.cnt["ve"], b.cnt["ve"]]
        ars = {}
        for tl in range(NT + 2):
            if tl < NT:
                t, bb, nb = tl, tl % 2, tl % 3
                pe_q = None
                for c in range(NC):
                    w_ = [("ve", pq_free[bb])] if c == 0 else ()
                    pe_q = b.pe(lambda e, t=t, c=c, bb=bb, wq_b=wq_b: e.matmul(
                        PQ[:, 512 * bb:512 * bb + 512],
                        xnT[c][:, 128 * t:128 * t + 128],
                        wq_b[:, 512 * c:512 * c + 512],
                        start=(c == 0), stop=(c == NC - 1)),
                        waits=w_, inc="pe" if c == NC - 1 else None)
                pq_free[bb] = b.ve(lambda e, bb=bb, nb=nb: e.tensor_copy(
                    qf32s[nb][:], PQ[:, 512 * bb:512 * bb + 512]),
                    waits=[("pe", pe_q)])
                b.ve(lambda e, nb=nb: e.tensor_tensor(sqs[nb][:], qf32s[nb][:],
                                                      qf32s[nb][:], OP.mult))
                ve_ms = b.ve(lambda e, nb=nb: e.tensor_reduce(
                    mss[nb][:], sqs[nb][:].rearrange("p (h d) -> p h d", d=64),
                    AX.X, OP.add))
                b.act(lambda e, nb=nb: e.activation(lns[:], mss[nb][:], AF.Ln,
                                                    bias=eps6[:], scale=1.0 / 64),
                      waits=[("ve", ve_ms)])
                ars[t] = b.act(lambda e, nb=nb: e.activation(
                    rinvs[nb][:], lns[:], AF.Exp, bias=0.0, scale=-0.5))
            if tl >= 2:
                t, nb = tl - 2, (tl - 2) % 3
                for hh in range(8):
                    b.ve(lambda e, hh=hh, nb=nb: e.tensor_scalar(
                        qtb[:, 32 * hh:32 * hh + 32],
                        qf32s[nb][:, 64 * hh:64 * hh + 32],
                        rinvs[nb][:, hh:hh + 1], None, OP.mult),
                        waits=[("act", ars.pop(t))] if hh == 0 else ())
                b.ve(lambda e, nb=nb: e.tensor_tensor(sqs[nb][:, 0:256], qtb[:],
                                                      qtb[:], OP.mult))
                b.ve(lambda e, nb=nb: e.tensor_reduce(
                    mss[nb][:], sqs[nb][:, 0:256].rearrange("p (h d) -> p h d", d=32),
                    AX.X, OP.add))
                ve_tail = b.ve(lambda e, half=half, nb=nb: e.tensor_tensor(
                    Qm2[:, 8 * half:8 * half + 8], Qm2[:, 8 * half:8 * half + 8],
                    mss[nb][:], OP.max))
                pb = t % 2
                tp = None
                for jj in range(2):
                    for a in range(4):
                        hh = 4 * jj + a
                        w_ = [("ve", ve_tail), ("ve", ptx_evac[pb])] \
                            if (jj == 0 and a == 0) else ()
                        tp = b.pe(lambda e, hh=hh, a=a, jj=jj, pb=pb: e.transpose(
                            PTx[pb][32 * a:32 * a + 32, 128 * jj:128 * jj + 128],
                            qtb[:, 32 * hh:32 * hh + 32],
                            ident[:], tile_position=(0, 32 * a)),
                            waits=w_, inc="pe" if (jj == 1 and a == 3) else None)
                ov = qTs_all.rearrange("p (j t) -> p j t", j=4)[
                    :, 2 * half:2 * half + 2, 128 * t:128 * t + 128]
                ptx_evac[pb] = b.ve(lambda e, ov=ov, pb=pb: e.tensor_copy(
                    ov, PTx[pb][:].rearrange("p (j t) -> p j t", j=2)),
                    waits=[("pe", tp)])

    wkv_b = _cast_load(b, esQ, "l0a_wkv", caps["wkv"], 1024, 128, stage,
                       stage2=stageB)
    ve_prev = [b.cnt["ve"], b.cnt["ve"]]
    for w in range(NW):
        bb = w % 2
        pe_kv = None
        for c in range(NC):
            pe_kv = b.pe(lambda e, w=w, c=c, bb=bb: e.matmul(
                PQ[:, 512 * bb:512 * bb + 512], wkv_b[:, 128 * c:128 * c + 128],
                xnT[c][:, 512 * w:512 * w + 512],
                start=(c == 0), stop=(c == NC - 1)),
                waits=[("ve", ve_prev[bb])] if c == 0 else (),
                inc="pe" if c == NC - 1 else None)
        ve_prev[bb] = b.ve(lambda e, w=w, bb=bb: e.tensor_copy(
            kvT[:, 512 * w:512 * w + 512], PQ[:, 512 * bb:512 * bb + 512]),
            waits=[("pe", pe_kv)], inc="ve")
    esQ.close()
    es_xnT.close()
    aoTs = [b.sb(f"l0a_aoT{s}", [128, T], BF16, es_main, side="right")
            for s in range(8)]
    for s in range(8):
        b.ve(lambda e, s=s: e.memset(aoTs[s][:], 0.0))

    esK = b.scope()
    stage2 = b.sb("l0a_stage2", [128, 512], F32, esK)
    wk_b = _cast_load(b, esK, "l0a_wk", caps["wk"], 128, 512, stage2)
    wv_b = _cast_load(b, esK, "l0a_wv", caps["wv"], 128, 512, stage2)
    qg = b.sb("l0a_qg", [1, 64], F32, esK)
    kg = b.sb("l0a_kg", [1, 32], F32, esK)
    g512 = b.sb("l0a_g512", [1, 512], F32, esK)
    gbc = b.sb("l0a_gbc", [128, 512], F32, esK)
    ldg = b.ld(lambda e: e.dma_start(out=qg[:], in_=caps["qgv"][:]), slot=5)
    ldk = b.ld(lambda e: e.dma_start(out=kg[:], in_=caps["kgv"][:]), slot=4)
    b.ve(lambda e: e.tensor_tensor(g512[:, 0:32], qg[:, 0:32], kg[:], OP.mult),
         waits=[ldg, ldk])
    b.ve(lambda e: e.tensor_scalar(g512[:, 0:32], g512[:, 0:32], 0.125, None, OP.mult))
    for h in range(1, H):
        b.ve(lambda e, h=h: e.tensor_copy(g512[:, 32 * h:32 * h + 32], g512[:, 0:32]))
    veg = b.ve(lambda e: e.nop(), inc="ve")
    PK = b.ps("l0a_PK", [128, 512], F32, esK)
    PVs = [b.ps(f"l0a_PV{i}", [128, 512], F32, esK) for i in range(2)]
    PTk = [b.ps(f"l0a_PTk{i}", [128, 512], BF16, esK) for i in range(2)]
    pes_v = {}
    pg = b.pe(lambda e: e.matmul(PK[:], consts["onesrow_f"][:], g512[:],
                                 start=True, stop=True), waits=[("ve", veg)], inc="pe")
    veg = b.ve(lambda e: e.tensor_copy(gbc[:], PK[:]), waits=[("pe", pg)], inc="ve")

    lnsk = b.sb("l0a_lnsk", [128, 16], F32, esK)
    kf32s = [b.sb(f"l0a_kf32{i}", [128, 512], F32, esK) for i in range(2)]
    ktmp = b.sb("l0a_ktmp", [128, 512], F32, esK)
    ktb = b.sb("l0a_ktb", [128, 512], BF16, esK)
    sqk = b.sb("l0a_sqk", [128, 512], F32, esK)
    msks = [b.sb(f"l0a_msk{i}", [128, 16], F32, esK) for i in range(2)]
    rinvks = [b.sb(f"l0a_rinvk{i}", [128, 16], F32, esK) for i in range(2)]

    ptx_evac = [b.cnt["ve"], b.cnt["ve"]]
    kcopy = [veg, veg]
    vevac = [veg, veg]
    ars = {}
    for tl in range(NT + 1):
        if tl < NT:
            t, nb = tl, tl % 2
            pe_k = b.pe(lambda e, t=t: e.matmul(PK[:], kvT[:, 128 * t:128 * t + 128],
                                                wk_b[:], start=True, stop=True),
                        waits=[("ve", kcopy[(t + 1) % 2])], inc="pe")
            pe_v = b.pe(lambda e, t=t, nb=nb: e.matmul(
                PVs[nb][:], kvT[:, 128 * t:128 * t + 128],
                wv_b[:], start=True, stop=True),
                waits=[("ve", vevac[nb])], inc="pe")
            pes_v[t] = pe_v
            kcopy[nb] = b.ve(lambda e, nb=nb: e.tensor_copy(kf32s[nb][:], PK[:]),
                             waits=[("pe", pe_k)])
            b.ve(lambda e, nb=nb: e.tensor_tensor(sqk[:], kf32s[nb][:], kf32s[nb][:],
                                                  OP.mult))
            ve_ms = b.ve(lambda e, nb=nb: e.tensor_reduce(
                msks[nb][:], sqk[:].rearrange("p (h d) -> p h d", d=32), AX.X, OP.add),
                inc="ve")
            b.act(lambda e, nb=nb: e.activation(lnsk[:], msks[nb][:], AF.Ln,
                                                bias=eps6[:], scale=1.0 / 32),
                  waits=[("ve", ve_ms)])
            ars[t] = b.act(lambda e, nb=nb: e.activation(
                rinvks[nb][:], lnsk[:], AF.Exp, bias=0.0, scale=-0.5), inc="act")
        if tl >= 1:
            t, nb = tl - 1, (tl - 1) % 2
            for h in range(H):
                b.ve(lambda e, h=h, nb=nb: e.tensor_scalar(
                    ktmp[:, 32 * h:32 * h + 32], kf32s[nb][:, 32 * h:32 * h + 32],
                    rinvks[nb][:, h:h + 1], None, OP.mult),
                    waits=[("act", ars.pop(t))] if h == 0 else ())
            b.ve(lambda e: e.tensor_tensor(ktb[:], ktmp[:], gbc[:], OP.mult))
            b.ve(lambda e: e.tensor_tensor(sqk[:], ktb[:], ktb[:], OP.mult))
            b.ve(lambda e, nb=nb: e.tensor_reduce(
                msks[nb][:], sqk[:].rearrange("p (h d) -> p h d", d=32), AX.X, OP.add))
            km_t = b.ve(lambda e, nb=nb: e.tensor_tensor(Km2[:], Km2[:], msks[nb][:],
                                                         OP.max), inc="ve")
            vevac[nb] = b.ve(lambda e, t=t, nb=nb: e.tensor_copy(
                V[:, 528 * t:528 * t + 528].rearrange("p (h c) -> p h c", c=33)[:, :, 0:32],
                PVs[nb][:].rearrange("p (h c) -> p h c", c=32)),
                waits=[("pe", pes_v.pop(t))], inc="ve")
            pb = t % 2
            tp = None
            for j in range(4):
                for a in range(4):
                    h = 4 * j + a
                    w = [("ve", km_t), ("ve", ptx_evac[pb])] if (j == 0 and a == 0) else ()
                    tp = b.pe(lambda e, h=h, a=a, j=j, pb=pb: e.transpose(
                        PTk[pb][32 * a:32 * a + 32, 128 * j:128 * j + 128],
                        ktb[:, 32 * h:32 * h + 32],
                        ident[:], tile_position=(0, 32 * a)),
                        waits=w, inc="pe" if (j == 3 and a == 3) else None)
            ov = kTs_all.rearrange("p (j t) -> p j t", j=4)[:, :, 128 * t:128 * t + 128]
            ptx_evac[pb] = b.ve(lambda e, ov=ov, pb=pb: e.tensor_copy(
                ov, PTk[pb][:].rearrange("p (j t) -> p j t", j=4)),
                waits=[("pe", tp)])
    _build_shift_bias(b, esK, consts, Qm2, Km2, nrm_dram, cb, "l0a")
    esK.close()
    if _DBG.get("upto") == "B1":
        es_main.close()
        return

    esA = b.scope()
    PS = b.ps("l0a_PS", [128, 2048], F32, esA)
    PAV = [b.ps(f"l0a_PAV{p}", [128, 512], F32, esA) for p in range(2)]
    PB2 = b.ps("l0a_PB2", [128, 512], F32, esA)
    PT = b.sb("l0a_PTexp", [128, 2048], BF16, esA)
    pavc = b.sb("l0a_pavc", [128, 512], F32, esA)
    bcast_sb = b.sb("l0a_bc", [128, 512], F32, esA)
    recipb = b.sb("l0a_recip", [128, 512], F32, esA)
    Epair = b.sb("cs2_Epair", [128, 128], F32, esA)
    lE = b.ld(lambda e: e.dma_start(out=Epair[:], in_=caps["Epair"][:]),
              waits=[("ve", b.cnt["ve"])], slot=7)
    masks2 = []
    for d in range(4):
        mt = b.sb(f"cs2_mask2_{d}", [128, 1024], BF16, esA)
        b.ld(lambda e, mt=mt, d=d: e.dma_start(out=mt[:], in_=caps[f"mask2_{d}"][:]),
             waits=[lE] if d == 0 else (), slot=7)
        masks2.append(mt)
    b.emit("ve", lambda e: e.nop(), waits=[("ld7", b.cnt["ld7"])])

    ps_free = [b.cnt["act"], b.cnt["act"]]
    pt_free = [b.cnt["pe"], b.cnt["pe"]]
    pav_free = [b.cnt["ve"], b.cnt["ve"]]
    pb2_free = b.cnt["ve"]
    norm_done = b.cnt["ve"]
    for j in range(4):
        for w in range(NW):
            for p in range(2):
                ktn = 4 * (w + 1)
                mz = b.ve(lambda e, p=p: e.memset(PAV[p][:], 0.0),
                          waits=[("ve", pav_free[p])])
                pending = None

                def mk_av(j, w, p, kt, bb, gate, ktn, mzw):
                    def emit_av():
                        for ai in range(2):
                            a = 2 * p + ai
                            h = 4 * j + a
                            wl = [gate] + mzw if ai == 0 else ()
                            r_ = b.pe(lambda e, kt=kt, h=h, ai=ai, bb=bb, p=p, ktn=ktn:
                                      e.matmul(
                                PAV[p][64 * ai:64 * ai + 33, :],
                                V[:, 528 * kt + 33 * h:528 * kt + 33 * h + 33],
                                PT[:, 1024 * bb + 512 * ai:1024 * bb + 512 * ai + 512],
                                start=(kt == 0 and ai == 0), stop=(kt == ktn - 1),
                                tile_position=(0, 64 * ai), skip_group_check=True),
                                waits=wl, inc="pe" if ai == 1 else None)
                            if ai == 1:
                                pt_free[bb] = r_
                    return emit_av

                for kt in range(ktn):
                    bb = kt % 2
                    diag = kt >= 4 * w
                    pe_s = None
                    for ai in range(2):
                        a = 2 * p + ai
                        wl = [("act", ps_free[bb])] if ai == 0 else ()
                        pe_s = b.pe(lambda e, j=j, w=w, kt=kt, a=a, ai=ai, bb=bb:
                                    e.matmul(
                            PS[:, 1024 * bb + 512 * ai:1024 * bb + 512 * ai + 512],
                            kTs[j][32 * a:32 * a + 32, 128 * kt:128 * kt + 128],
                            qTs[j][32 * a:32 * a + 32, 512 * w:512 * w + 512],
                            start=True, stop=True, tile_position=(32 * a, 0)),
                            waits=wl, inc="pe" if ai == 1 else None)
                    exp_t = b.act(lambda e, bb=bb: e.activation(
                        PT[:, 1024 * bb:1024 * bb + 1024],
                        PS[:, 1024 * bb:1024 * bb + 1024], AF.Exp, bias=cb[:],
                        scale=1.0),
                        waits=[("pe", pe_s), ("pe", pt_free[bb])])
                    ps_free[bb] = exp_t
                    gate = ("act", exp_t)
                    if diag:
                        d = kt - 4 * w
                        mk = b.ve(lambda e, d=d, bb=bb: e.tensor_tensor(
                            PT[:, 1024 * bb:1024 * bb + 1024],
                            PT[:, 1024 * bb:1024 * bb + 1024], masks2[d][:], OP.mult),
                            waits=[("act", exp_t)])
                        gate = ("ve", mk)
                    if pending is not None:
                        pending()
                    pending = mk_av(j, w, p, kt, bb, gate, ktn,
                                    [("ve", mz)] if kt == 0 else [])
                pending()
                pv = b.ve(lambda e, p=p: e.tensor_copy(pavc[:], PAV[p][:]),
                          waits=[("pe", pt_free[(ktn - 1) % 2])])
                pav_free[p] = pv
                pb_ = b.pe(lambda e: e.matmul(PB2[:], Epair[:], pavc[:],
                                              start=True, stop=True),
                           waits=[("ve", pv), ("ve", pb2_free)], inc="pe")
                b.ve(lambda e: e.tensor_copy(bcast_sb[:], PB2[:]), waits=[("pe", pb_)])
                pb2_free = b.ve(lambda e: e.reciprocal(recipb[:], bcast_sb[:]))
                b.ve(lambda e, j=j, w=w, p=p: e.tensor_tensor(
                    aoTs[2 * j + p][0:32, 512 * w:512 * w + 512], pavc[0:32, :],
                    recipb[0:32, :], OP.mult))
                b.ve(lambda e, j=j, w=w, p=p: e.tensor_tensor(
                    aoTs[2 * j + p][64:96, 512 * w:512 * w + 512], pavc[64:96, :],
                    recipb[64:96, :], OP.mult))
    esA.close()
    es_qkv.close()

    esO = b.scope()
    stg3 = [b.sb(f"l0a_stage3{i}", [128, 1024], F32, esO) for i in range(4)]
    wo_b = _cast_load(b, esO, "l0a_wo", caps["wosel"], 1024, 1024, None,
                      stages=stg3, slots=[4, 5, 6, 7])
    PO = b.ps("l0a_PO", [128, 1024], F32, esO)
    PTf = [b.ps(f"l0a_PTf{i}", [128, 128], F32, esO) for i in range(2)]
    of32 = [b.sb(f"l0a_of32_{i}", [128, 512], F32, esO) for i in range(2)]
    obuf = b.sb("l0a_obuf", [128, 4096], F32, esO)

    ptf_evac = [b.cnt["ve"], b.cnt["ve"]]
    po_free = [b.cnt["ve"], b.cnt["ve"]]
    ve_o = b.cnt["ve"]
    st_o = ("st0", b.cnt["st0"])
    for w in range(NW):
        for do in range(NC):
            bb = do % 2
            pe_o = None
            for s in range(8):
                pe_o = b.pe(lambda e, w=w, do=do, s=s, bb=bb: e.matmul(
                    PO[:, 512 * bb:512 * bb + 512],
                    wo_b[:, 1024 * s + 128 * do:1024 * s + 128 * do + 128],
                    aoTs[s][:, 512 * w:512 * w + 512],
                    start=(s == 0), stop=(s == 7)),
                    waits=[("ve", po_free[bb])] if s == 0 else (),
                    inc="pe" if s == 7 else None)
            po_free[bb] = b.ve(lambda e, bb=bb: e.tensor_copy(
                of32[bb][:], PO[:, 512 * bb:512 * bb + 512]), waits=[("pe", pe_o)])
            for tt in range(4):
                pb = tt % 2
                tp = b.pe(lambda e, tt=tt, pb=pb, bb=bb: e.transpose(
                    PTf[pb][:], of32[bb][:, 128 * tt:128 * tt + 128],
                    consts["identf"][:]),
                    waits=[("ve", po_free[bb]), ("ve", ptf_evac[pb])], inc="pe")
                ptf_evac[pb] = b.ve(lambda e, tt=tt, do=do, pb=pb: e.tensor_copy(
                    obuf[:, 1024 * tt + 128 * do:1024 * tt + 128 * do + 128], PTf[pb][:]),
                    waits=[("pe", tp)])
        for tt in range(4):
            st_o = b.st(lambda e, w=w, tt=tt: e.dma_start(
                out=a0_dram[512 * w + 128 * tt:512 * w + 128 * tt + 128, :],
                in_=obuf[:, 1024 * tt:1024 * tt + 1024]),
                waits=[("ve", max(ptf_evac))], slot=tt)
        if w < NW - 1:
            nv = b.emit("ve", lambda e: e.nop(),
                        waits=[(f"st{i}", b.cnt[f"st{i}"]) for i in range(4)])
            ptf_evac = [nv, nv]
    esO.close()
    es_main.close()



def emit_xn2T(b, consts, caps, x_dram, stats_sb, xn2T, xn2T_all):
    es = b.scope()
    xs = [b.sb(f"x2_{i}", [128, D], F32, es) for i in range(4)]
    xnfs = [b.sb(f"x2_xnf{i}", [128, D], F32, es) for i in range(2)]
    st6 = b.sb("x2_st6", [128, 12], F32, es)
    mvs = [b.sb(f"x2_mv{i}", [128, 2], F32, es) for i in range(2)]
    lnv = b.sb("x2_lnv", [128, 1], F32, es)
    rsts = [b.sb(f"x2_rst2{i}", [128, 1], F32, es) for i in range(2)]
    xnb = [b.sb(f"x2_xnb{i}", [128, D], BF16, es) for i in range(2)]
    ptr = [b.ps(f"x2_pt{i}", [128, 1024], BF16, es) for i in range(2)]
    wbc = b.sb("x2_wbc", [128, D], F32, es)
    bbc = b.sb("x2_bbc", [128, D], F32, es)
    nwv = b.sb("x2_nw", [1, D], F32, es)
    nbv = b.sb("x2_nb", [1, D], F32, es)
    PB = b.ps("x2_PB", [128, 1024], F32, es)
    ident = consts["ident"]
    eps5 = consts["eps5"]

    ln1 = b.ld(lambda e: e.dma_start(out=nwv[:], in_=caps["nw"][:]), slot=5)
    ln2 = b.ld(lambda e: e.dma_start(out=nbv[:], in_=caps["nb"][:]), slot=4)
    vv = b.ve(lambda e: e.nop(), waits=[ln1, ln2])
    pv = None
    for hh in range(2):
        pv = b.pe(lambda e, hh=hh: e.matmul(
            PB[:, 512 * hh:512 * hh + 512], consts["onesrow_f"][:],
            nwv[:, 512 * hh:512 * hh + 512], start=True, stop=True),
            waits=[("ve", vv)], inc="pe")
    vv = b.ve(lambda e: e.tensor_copy(wbc[:], PB[:]), waits=[("pe", pv)])
    for hh in range(2):
        pv = b.pe(lambda e, hh=hh: e.matmul(
            PB[:, 512 * hh:512 * hh + 512], consts["onesrow_f"][:],
            nbv[:, 512 * hh:512 * hh + 512], start=True, stop=True),
            waits=[("ve", vv)], inc="pe")
    vv = b.ve(lambda e: e.tensor_copy(bbc[:], PB[:]), waits=[("pe", pv)])

    pe_done = b.cnt["pe"]
    evac_cnt = [b.cnt["ve"], b.cnt["ve"]]
    scal_hist = [b.cnt["ve"]] * 4
    ars = {}
    for tl in range(NT + 1):
        if tl < NT:
            t, buf, nb2 = tl, tl % 4, tl % 2
            ld_t = b.ld(lambda e, t=t, buf=buf: e.dma_start(
                out=xs[buf][:], in_=x_dram[128 * t:128 * t + 128, :]),
                waits=[("ve", scal_hist[buf])] if t >= 4 else (), slot=buf)
            scal_hist[buf] = b.ve(lambda e, t=t, buf=buf, nb2=nb2: e.tensor_scalar(
                xnfs[nb2][:], xs[buf][:], stats_sb[:, 2 * t:2 * t + 1],
                stats_sb[:, 2 * t + 1:2 * t + 2], OP.subtract, OP.mult),
                waits=[ld_t])
            s1 = b.ve(lambda e, nb2=nb2: e.bn_stats(st6[:, 0:6], xnfs[nb2][:, 0:512]))
            s2 = b.ve(lambda e, nb2=nb2: e.bn_stats(st6[:, 6:12], xnfs[nb2][:, 512:1024]),
                      waits=[("ve", s1)])
            ve_mv = b.ve(lambda e, nb2=nb2: e.bn_aggr(mvs[nb2][:], st6[:]),
                         waits=[("ve", s2)])
            b.act(lambda e, nb2=nb2: e.activation(lnv[:], mvs[nb2][:, 1:2], AF.Ln,
                                                  bias=eps5[:], scale=1.0),
                  waits=[("ve", ve_mv)])
            ars[t] = b.act(lambda e, nb2=nb2: e.activation(
                rsts[nb2][:], lnv[:], AF.Exp, bias=0.0, scale=-0.5))
        if tl >= 1:
            t, nb2 = tl - 1, (tl - 1) % 2
            b.ve(lambda e, nb2=nb2: e.tensor_scalar(
                xnfs[nb2][:], xnfs[nb2][:], mvs[nb2][:, 0:1], rsts[nb2][:],
                OP.subtract, OP.mult),
                waits=[("act", ars.pop(t))])
            b.ve(lambda e, nb2=nb2: e.tensor_tensor(xnfs[nb2][:], xnfs[nb2][:],
                                                    wbc[:], OP.mult))
            ve_done = b.ve(lambda e, nb2=nb2: e.tensor_tensor(
                xnb[nb2][:], xnfs[nb2][:], bbc[:], OP.add),
                waits=[("pe", pe_done)])
            pb = t % 2
            tp = None
            for c in range(NC):
                w2 = [("ve", ve_done), ("ve", evac_cnt[pb])] if c == 0 else ()
                tp = b.pe(lambda e, c=c, nb2=nb2, pb=pb: e.transpose(
                    ptr[pb][:, 128 * c:128 * c + 128],
                    xnb[nb2][:, 128 * c:128 * c + 128], ident[:]),
                    waits=w2, inc="pe" if c == NC - 1 else None)
            ov = xn2T_all.rearrange("p (c t) -> p c t", c=NC)[:, :, 128 * t:128 * t + 128]
            evac_cnt[pb] = b.ve(lambda e, ov=ov, pb=pb: e.tensor_copy(
                ov, ptr[pb][:].rearrange("p (c t) -> p c t", c=NC)),
                waits=[("pe", tp)])
            pe_done = tp
    es.close()


def emit_l0ffn(b, consts, caps, x_dram, stats_sb, f0_dram, qT0f, kT0f, aoT0f, dbg=None):
    H = 8
    es_main = b.scope()
    xn2T_all = b.sb("xn2T_all", [128, NC * T], BF16, es_main, side="right")
    xn2T = [xn2T_all[:, c * T:(c + 1) * T] for c in range(NC)]
    emit_xn2T(b, consts, caps, x_dram, stats_sb, xn2T, xn2T_all)

    V0 = b.sb("l0f_V", [128, NT * 1024], BF16, es_main)
    cb = b.sb("l0f_cb", [128, 1], F32, es_main)
    qm = b.sb("l0f_qm", [1, 2], F32, es_main)
    b.ve(lambda e: e.memset(qm[:], 0.0))

    def qk_pass(which, wT_dram, bias_dram, out_dram, scale, qmcol):
        esP = b.scope()
        stage = b.sb(f"l0f_stage_{which}", [128, 1024], F32, esP)
        stageB = b.sb(f"l0f_stageB_{which}", [128, 1024], F32, esP)
        wT_b = _cast_load(b, esP, f"l0f_{which}w", wT_dram, 1024, 1024, stage,
                          stage2=stageB)
        bia = b.sb(f"l0f_{which}b", [128, 8], F32, esP)
        lb = b.ld(lambda e: e.dma_start(out=bia[:], in_=bias_dram[:]), slot=5)
        qsl = [b.sb(f"l0f_{which}sl{i}", [128, 512], BF16, esP) for i in range(4)]
        sqs = b.sb(f"l0f_{which}sq", [128, 512], F32, esP)
        nrm1 = b.sb(f"l0f_{which}n1", [1, 512], F32, esP)
        PP = b.ps(f"l0f_PP_{which}", [128, 1024], F32, esP)
        PN = b.ps(f"l0f_PN_{which}", [128, 512], F32, esP)
        pp_free = [b.cnt["ve"], b.cnt["ve"]]
        sl_st = [(f"st{i}", 0) for i in range(4)]
        sidx = 0
        for e_ in range(4):
            mch = NC >> e_
            for do in range(NC):
                for ws in range(2):
                    w = 2 * e_ + ws
                    bb = sidx % 2
                    qb = sidx % 4
                    pe_q = None
                    for ci in range(mch):
                        pe_q = b.pe(lambda e, ci=ci, do=do, w=w, mch=mch, bb=bb:
                                    e.matmul(
                            PP[:, 512 * bb:512 * bb + 512],
                            wT_b[:, 1024 * ci + 128 * do:1024 * ci + 128 * do + 128],
                            xn2T[ci][:, 512 * w:512 * w + 512],
                            start=(ci == 0), stop=(ci == mch - 1)),
                            waits=[("ve", pp_free[bb]), lb] if ci == 0 else (),
                            inc="pe" if ci == mch - 1 else None)
                    if scale is not None:
                        vq = b.ve(lambda e, do=do, bb=bb, qb=qb: e.tensor_scalar(
                            qsl[qb][:], PP[:, 512 * bb:512 * bb + 512],
                            bia[:, do:do + 1], scale, OP.add, OP.mult),
                            waits=[("pe", pe_q), sl_st[qb]])
                    else:
                        vq = b.ve(lambda e, do=do, bb=bb, qb=qb: e.tensor_scalar(
                            qsl[qb][:], PP[:, 512 * bb:512 * bb + 512],
                            bia[:, do:do + 1], None, OP.add),
                            waits=[("pe", pe_q), sl_st[qb]])
                    pp_free[bb] = b.ve(lambda e, qb=qb: e.tensor_tensor(
                        sqs[:], qsl[qb][:], qsl[qb][:], OP.mult))
                    sl_st[qb] = b.st(lambda e, do=do, w=w, qb=qb: e.dma_start(
                        out=out_dram[128 * do:128 * do + 128, 512 * w:512 * w + 512],
                        in_=qsl[qb][:]), waits=[("ve", pp_free[bb])], slot=qb)
                    pn = b.pe(lambda e: e.matmul(PN[0:1, :], consts["onescol_f"][:],
                                                 sqs[:], start=True, stop=True),
                              waits=[("ve", pp_free[bb])], inc="pe")
                    b.ve(lambda e: e.reduce_max(nrm1[:, 0:1], PN[0:1, :], axis=AX.X),
                         waits=[("pe", pn)])
                    b.ve(lambda e, qmcol=qmcol: e.tensor_tensor(
                        qm[:, qmcol:qmcol + 1], qm[:, qmcol:qmcol + 1],
                        nrm1[:, 0:1], OP.max))
                    sidx += 1
        b.emit("ve", lambda e: e.nop(),
               waits=[(f"st{i}", b.cnt[f"st{i}"]) for i in range(4)])
        esP.close()

    qk_pass("q", caps["qwT"], caps["qb"], qT0f, 1.0 / 32, 0)
    qk_pass("k", caps["kwT"], caps["kb"], kT0f, None, 1)

    esV = b.scope()
    stageV = b.sb("l0f_stageV", [128, 1024], F32, esV)
    stageVB = b.sb("l0f_stageVB", [128, 1024], F32, esV)
    vwT_b = _cast_load(b, esV, "l0f_vw", caps["vwT"], 1024, 1024, stageV,
                       stage2=stageVB)
    vb1024 = b.sb("l0f_vb1024", [1, 1024], F32, esV)
    vbc = b.sb("l0f_vbc", [128, 1024], F32, esV)
    lb = b.ld(lambda e: e.dma_start(out=vb1024[:], in_=caps["vb"][:]), slot=5)
    vv = b.ve(lambda e: e.nop(), waits=[lb])
    PP = b.ps("l0f_PPv", [128, 1024], F32, esV)
    for hh in range(2):
        pv = b.pe(lambda e, hh=hh: e.matmul(
            PP[:, 0:512], consts["onesrow_f"][:], vb1024[:, 512 * hh:512 * hh + 512],
            start=True, stop=True), waits=[("ve", vv)], inc="pe")
        vv = b.ve(lambda e, hh=hh: e.tensor_copy(vbc[:, 512 * hh:512 * hh + 512],
                                                 PP[:, 0:512]),
                  waits=[("pe", pv)])
    pp_free = [vv, vv]
    sidx = 0
    for e_ in range(4):
        mch = NC >> e_
        for ts_ in range(8):
            t = 8 * e_ + ts_
            for hh in range(2):
                bb = sidx % 2
                pe_v = None
                for ci in range(mch):
                    pe_v = b.pe(lambda e, ci=ci, t=t, hh=hh, mch=mch, bb=bb: e.matmul(
                        PP[:, 512 * bb:512 * bb + 512],
                        xn2T[ci][:, 128 * t:128 * t + 128],
                        vwT_b[:, 1024 * ci + 512 * hh:1024 * ci + 512 * hh + 512],
                        start=(ci == 0), stop=(ci == mch - 1)),
                        waits=[("ve", pp_free[bb])] if ci == 0 else (),
                        inc="pe" if ci == mch - 1 else None)
                pp_free[bb] = b.ve(lambda e, t=t, hh=hh, bb=bb: e.tensor_tensor(
                    V0[:, 1024 * t + 512 * hh:1024 * t + 512 * hh + 512],
                    PP[:, 512 * bb:512 * bb + 512],
                    vbc[:, 512 * hh:512 * hh + 512], OP.add),
                    waits=[("pe", pe_v)])
                sidx += 1
    esV.close()

    esB = b.scope()
    PN2 = b.ps("l0f_PN2", [128, 8], F32, esB)
    ve0 = b.ve(lambda e: e.tensor_tensor(qm[:, 0:1], qm[:, 0:1], qm[:, 1:2], OP.mult))
    b.act(lambda e: e.activation(qm[:, 1:2], qm[:, 0:1], AF.Ln, bias=0.0, scale=1.0),
          waits=[("ve", ve0)])
    a1 = b.act(lambda e: e.activation(qm[:, 0:1], qm[:, 1:2], AF.Exp, bias=0.0, scale=0.5))
    ve1 = b.ve(lambda e: e.tensor_scalar(qm[:, 1:2], qm[:, 0:1], -1.01, None, OP.mult),
               waits=[("act", a1)])
    p0 = b.pe(lambda e: e.matmul(PN2[:, 0:1], consts["onesrow_f"][:], qm[:, 1:2],
                                 start=True, stop=True), waits=[("ve", ve1)], inc="pe")
    b.ve(lambda e: e.tensor_copy(cb[:], PN2[:, 0:1]), waits=[("pe", p0)])
    esB.close()
    es_xn2T_close = None

    esA = b.scope()
    PS = b.ps("l0f_PS", [128, 3072], F32, esA)
    PAV = b.ps("l0f_PAV", [128, 512], F32, esA)
    PSM = b.ps("l0f_PSM", [128, 512], F32, esA)
    PT = b.sb("l0f_PT", [128, 3072], BF16, esA)
    qwin = [b.sb(f"l0f_qw{i}", [128, 512], BF16, esA) for i in range(2)]
    kTh = [b.sb(f"l0f_kTh{i}", [128, T], BF16, esA) for i in range(2)]
    srow = b.sb("l0f_srow", [1, 512], F32, esA)
    rrow = b.sb("l0f_rrow", [1, 512], F32, esA)
    bcast = b.sb("l0f_bcast", [128, 512], F32, esA)
    pavc = b.sb("l0f_pavc", [128, 512], F32, esA)
    aosl = [b.sb(f"l0f_aosl{i}", [128, 512], BF16, esA) for i in range(2)]

    GROUPS = [(0, 3), (3, 3), (6, 3), (9, 3), (12, 3), (15, 3), (18, 3), (21, 3),
              (24, 3), (27, 3), (30, 2)]
    wins = [(h, w) for h in range(H) for w in range(NW)]
    ps_free = [b.cnt["act"], b.cnt["act"]]
    pt_free = [b.cnt["pe"], b.cnt["pe"]]
    qw_free = [b.cnt["pe"], b.cnt["pe"]]
    kh_free = [b.cnt["pe"], b.cnt["pe"]]
    pav_ready = b.cnt["ve"]
    psm_ready = b.cnt["ve"]
    ao_st = [("st2", 0), ("st3", 0)]
    lq = {}
    kld = {}

    def issue_lq(idx):
        if idx >= len(wins):
            return
        h, w = wins[idx]
        qb_ = idx % 2
        lq[idx] = b.ld(lambda e, h=h, w=w, qb_=qb_: e.dma_start(
            out=qwin[qb_][:], in_=qT0f[128 * h:128 * h + 128, 512 * w:512 * w + 512]),
            waits=[("pe", qw_free[qb_])], slot=qb_)

    def issue_kld(h):
        if h >= H:
            return
        hb = h % 2
        kld[h] = b.ld(lambda e, h=h, hb=hb: e.dma_start(
            out=kTh[hb][:], in_=kT0f[128 * h:128 * h + 128, :]),
            waits=[("pe", kh_free[hb])], slot=4 + hb)

    issue_lq(0)
    issue_kld(0)
    issue_kld(1)
    for idx, (h, w) in enumerate(wins):
        hb = h % 2
        qb_ = idx % 2
        pending = None

        def mk_av(h, kt0, gn, bb, exp_t, lastg, first):
            def emit_av():
                for i in range(gn):
                    kt = kt0 + i
                    wl = [exp_t] if i == 0 else ()
                    if i == 0 and first:
                        wl = [exp_t, ("ve", pav_ready)]
                    b.pe(lambda e, h=h, kt=kt, i=i, bb=bb, lastg=lastg, gn=gn:
                         e.matmul(
                        PAV[:], V0[:, 1024 * kt + 128 * h:1024 * kt + 128 * h + 128],
                        PT[:, 1536 * bb + 512 * i:1536 * bb + 512 * i + 512],
                        start=(kt == 0), stop=(lastg and i == gn - 1),
                        skip_group_check=True),
                        waits=wl)
                r_ = None
                for i in range(gn):
                    kt = kt0 + i
                    wl = [("ve", psm_ready)] if (i == 0 and first) else ()
                    r_ = b.pe(lambda e, kt=kt, i=i, bb=bb, lastg=lastg, gn=gn:
                              e.matmul(
                        PSM[0:1, :], consts["onescol_bf"][:],
                        PT[:, 1536 * bb + 512 * i:1536 * bb + 512 * i + 512],
                        start=(kt == 0), stop=(lastg and i == gn - 1),
                        skip_group_check=True),
                        waits=wl, inc="pe" if i == gn - 1 else None)
                pt_free[bb] = r_
            return emit_av

        for gi, (kt0, gn) in enumerate(GROUPS):
            bb = gi % 2
            pe_s = None
            for i in range(gn):
                kt = kt0 + i
                wl = []
                if i == 0:
                    wl = [("act", ps_free[bb]), lq[idx], kld[h]]
                pe_s = b.pe(lambda e, hb=hb, kt=kt, i=i, bb=bb, qb_=qb_, gn=gn:
                            e.matmul(
                    PS[:, 1536 * bb + 512 * i:1536 * bb + 512 * i + 512],
                    kTh[hb][:, 128 * kt:128 * kt + 128], qwin[qb_][:],
                    start=True, stop=True),
                    waits=wl, inc="pe" if i == gn - 1 else None)
            qw_free[qb_] = pe_s
            kh_free[hb] = pe_s
            if gi == 2:
                issue_lq(idx + 1)
            exp_t = b.act(lambda e, bb=bb, gn=gn: e.activation(
                PT[:, 1536 * bb:1536 * bb + 512 * gn],
                PS[:, 1536 * bb:1536 * bb + 512 * gn], AF.Exp, bias=cb[:],
                scale=1.0),
                waits=[("pe", pe_s), ("pe", pt_free[bb])])
            ps_free[bb] = exp_t
            lastg = (gi == len(GROUPS) - 1)
            if pending is not None:
                pending()
            pending = mk_av(h, kt0, gn, bb, ("act", exp_t), lastg, gi == 0)
        pending()
        if w == NW - 1:
            issue_kld(h + 2)
        pv = b.ve(lambda e: e.tensor_copy(pavc[:], PAV[:]),
                  waits=[("pe", pt_free[(len(GROUPS) - 1) % 2]), ao_st[idx % 2]])
        sv = b.ve(lambda e: e.tensor_copy(srow[:], PSM[0:1, :]))
        psm_ready = sv
        rv = b.ve(lambda e: e.reciprocal(rrow[:], srow[:]))
        pb_ = b.pe(lambda e: e.matmul(PAV[:], consts["onesrow_f"][:], rrow[:],
                                      start=True, stop=True),
                   waits=[("ve", rv)], inc="pe")
        pav_ready = b.ve(lambda e: e.tensor_copy(bcast[:], PAV[:]), waits=[("pe", pb_)])
        ve_f = b.ve(lambda e, qb_=qb_: e.tensor_tensor(aosl[qb_][:], pavc[:], bcast[:],
                                                       OP.mult))
        s_, v_ = b.st(lambda e, h=h, w=w, qb_=qb_: e.dma_start(
            out=aoT0f[128 * h:128 * h + 128, 512 * w:512 * w + 512], in_=aosl[qb_][:]),
            waits=[("ve", ve_f)], slot=2 + idx % 2)
        ao_st[idx % 2] = (s_, v_)
    esA.close()

    esO = b.scope()
    stage4 = b.sb("l0f_stage4", [128, 1024], F32, esO)
    stage4B = b.sb("l0f_stage4B", [128, 1024], F32, esO)
    owT_b = _cast_load(b, esO, "l0f_ow", caps["owT"], 1024, 1024, stage4,
                       stage2=stage4B)
    ob = b.sb("l0f_ob", [128, 8], F32, esO)
    b.ld(lambda e: e.dma_start(out=ob[:], in_=caps["ob"][:]), slot=5)
    aoseg = b.sb("l0f_aoseg", [128, 8 * 1024], BF16, esO)
    PO = b.ps("l0f_PO", [128, 1024], F32, esO)
    PTf = [b.ps(f"l0f_PTf{i}", [128, 128], F32, esO) for i in range(2)]
    of32 = [b.sb(f"l0f_of32_{i}", [128, 512], F32, esO) for i in range(2)]
    obuf = b.sb("l0f_obuf", [128, 4096], F32, esO)
    zbuf = b.sb("l0f_zbuf", [128, 1024], F32, esO)
    vz = b.ve(lambda e: e.memset(zbuf[:], 0.0))
    for t in range(NT):
        b.st(lambda e, t=t: e.dma_start(out=f0_dram[128 * t:128 * t + 128, :],
                                        in_=zbuf[:]), waits=[("ve", vz)], slot=t % 4)
    zdone = [(f"st{i}", b.cnt[f"st{i}"]) for i in range(4)]

    ptf_evac = [b.cnt["ve"], b.cnt["ve"]]
    for e_ in range(4):
        mch = NC >> e_
        ldd = {}
        for ci in range(mch):
            s_, v_ = b.ld(lambda e, e_=e_, ci=ci: e.dma_start(
                out=aoseg[:, 1024 * ci:1024 * ci + 1024],
                in_=aoT0f[128 * ci:128 * ci + 128, 1024 * e_:1024 * e_ + 1024]),
                waits=[("ve", b.cnt["ve"])], slot=ci % 4)
            ldd[s_] = v_
        lds = list(ldd.items())
        ve_o = b.cnt["ve"]
        st_w = []
        po_free = [b.cnt["ve"], b.cnt["ve"]]
        sidx = 0
        for ws in range(2):
            for do in range(mch):
                bb = sidx % 2
                pe_o = None
                for ci in range(mch):
                    pe_o = b.pe(lambda e, ci=ci, do=do, ws=ws, mch=mch, bb=bb: e.matmul(
                        PO[:, 512 * bb:512 * bb + 512],
                        owT_b[:, 1024 * ci + 128 * do:1024 * ci + 128 * do + 128],
                        aoseg[:, 1024 * ci + 512 * ws:1024 * ci + 512 * ws + 512],
                        start=(ci == 0), stop=(ci == mch - 1)),
                        waits=[("ve", po_free[bb])] + (lds if (ws == 0 and do == 0)
                                                      else [])
                        if ci == 0 else (),
                        inc="pe" if ci == mch - 1 else None)
                po_free[bb] = b.ve(lambda e, do=do, bb=bb: e.tensor_scalar(
                    of32[bb][:], PO[:, 512 * bb:512 * bb + 512],
                    ob[:, do:do + 1], None, OP.add),
                    waits=[("pe", pe_o), ("ld5", b.cnt["ld5"])])
                for tt in range(4):
                    pb = tt % 2
                    tp = b.pe(lambda e, tt=tt, pb=pb, bb=bb: e.transpose(
                        PTf[pb][:], of32[bb][:, 128 * tt:128 * tt + 128],
                        consts["identf"][:]),
                        waits=[("ve", po_free[bb]), ("ve", ptf_evac[pb])], inc="pe")
                    ptf_evac[pb] = b.ve(lambda e, tt=tt, do=do, pb=pb: e.tensor_copy(
                        obuf[:, 1024 * tt + 128 * do:1024 * tt + 128 * do + 128],
                        PTf[pb][:]), waits=[("pe", tp)])
                sidx += 1
            for tt in range(4):
                t_global = 8 * e_ + 4 * ws + tt
                st_w = b.st(lambda e, t_global=t_global, tt=tt, mch=mch: e.dma_start(
                    out=f0_dram[128 * t_global:128 * t_global + 128, 0:128 * mch],
                    in_=obuf[:, 1024 * tt:1024 * tt + 128 * mch]),
                    waits=[("ve", max(ptf_evac))] + (zdone if e_ == 0 and ws == 0 else []),
                    slot=tt)
            ve_o = b.emit("ve", lambda e: e.nop(),
                          waits=[(f"st{i}", b.cnt[f"st{i}"]) for i in range(4)])
    esO.close()
    es_main.close()



def emit_assembly_ln(b, consts, x_dram, a_dram, f_dram, stats_sb, add_xn,
                     x1_dram, stats1_sb, xnT1, xnT1_all):
    es = b.scope()
    xs = [b.sb(f"as_x{i}", [128, D], F32, es) for i in range(2)]
    as_ = [b.sb(f"as_a{i}", [128, D], F32, es) for i in range(2)]
    fs = [b.sb(f"as_f{i}", [128, D], F32, es) for i in range(2)]
    x1fs = [b.sb(f"as_x1f{i}", [128, D], F32, es) for i in range(2)]
    tmp = b.sb("as_tmp", [128, D], F32, es)
    st6 = b.sb("as_st6", [128, 12], F32, es)
    mvs = [b.sb(f"as_mv{i}", [128, 2], F32, es) for i in range(2)]
    lnv = b.sb("as_lnv", [128, 1], F32, es)
    xnb = [b.sb(f"as_xnb{i}", [128, D], BF16, es) for i in range(2)]
    ptr = [b.ps(f"as_pt{i}", [128, 1024], BF16, es) for i in range(2)]
    ident = consts["ident"]
    eps5 = consts["eps5"]

    for c in range(NC):
        b.ve(lambda e, c=c: e.memset(xnT1[c][:, 0:1], 0.0))

    pe_done = b.cnt["pe"]
    evac_cnt = [b.cnt["ve"], b.cnt["ve"]]
    scal_hist = [b.cnt["ve"], b.cnt["ve"]]
    ars = {}
    for tl in range(NT + 1):
        if tl < NT:
            t, buf, nb = tl, tl % 2, tl % 2
            w0 = [("ve", scal_hist[buf])] if t >= 2 else ()
            l1 = b.ld(lambda e, t=t, buf=buf: e.dma_start(
                out=xs[buf][:], in_=x_dram[128 * t:128 * t + 128, :]), waits=w0,
                slot=buf)
            l2 = b.ld(lambda e, t=t, buf=buf: e.dma_start(
                out=as_[buf][:], in_=a_dram[128 * t:128 * t + 128, :]), waits=w0,
                slot=2 + buf)
            l3 = b.ld(lambda e, t=t, buf=buf: e.dma_start(
                out=fs[buf][:], in_=f_dram[128 * t:128 * t + 128, :]), waits=w0,
                slot=4 + buf)
            x1f = x1fs[nb]
            b.ve(lambda e, buf=buf, x1f=x1f: e.tensor_tensor(
                x1f[:], as_[buf][:], fs[buf][:], OP.add),
                waits=[l2, l3, (f"st{t % 2}", b.cnt[f"st{t % 2}"])])
            scal_hist[buf] = b.ve(lambda e, buf=buf, x1f=x1f: e.tensor_tensor(
                x1f[:], x1f[:], xs[buf][:], OP.add), waits=[l1])
            if add_xn:
                scal_hist[buf] = b.ve(lambda e, t=t, buf=buf: e.tensor_scalar(
                    tmp[:], xs[buf][:], stats_sb[:, 2 * t:2 * t + 1],
                    stats_sb[:, 2 * t + 1:2 * t + 2], OP.subtract, OP.mult))
                b.ve(lambda e, x1f=x1f: e.tensor_tensor(x1f[:], x1f[:], tmp[:],
                                                        OP.add))
            vx = b.ve(lambda e: e.nop())
            b.st(lambda e, t=t, x1f=x1f: e.dma_start(
                out=x1_dram[128 * t:128 * t + 128, :], in_=x1f[:]),
                waits=[("ve", vx)], slot=t % 2)
            s1 = b.ve(lambda e, x1f=x1f: e.bn_stats(st6[:, 0:6], x1f[:, 0:512]))
            s2 = b.ve(lambda e, x1f=x1f: e.bn_stats(st6[:, 6:12], x1f[:, 512:1024]),
                      waits=[("ve", s1)])
            ve_mv = b.ve(lambda e, nb=nb: e.bn_aggr(mvs[nb][:], st6[:]),
                         waits=[("ve", s2)])
            b.ve(lambda e, t=t, nb=nb: e.tensor_copy(stats1_sb[:, 2 * t:2 * t + 1],
                                                     mvs[nb][:, 0:1]),
                 waits=[("ve", ve_mv)])
            b.act(lambda e, nb=nb: e.activation(lnv[:], mvs[nb][:, 1:2], AF.Ln,
                                                bias=eps5[:], scale=1.0),
                  waits=[("ve", ve_mv)])
            ars[t] = b.act(lambda e, t=t: e.activation(
                stats1_sb[:, 2 * t + 1:2 * t + 2], lnv[:], AF.Exp, bias=0.0,
                scale=-0.5))
        if tl >= 1:
            t, nb = tl - 1, (tl - 1) % 2
            x1f = x1fs[nb]
            ve_done = b.ve(lambda e, t=t, nb=nb, x1f=x1f: e.tensor_scalar(
                xnb[nb][:], x1f[:], stats1_sb[:, 2 * t:2 * t + 1],
                stats1_sb[:, 2 * t + 1:2 * t + 2], OP.subtract, OP.mult),
                waits=[("act", ars.pop(t)), ("pe", pe_done)])
            pb = t % 2
            tp = None
            for c in range(NC):
                w2 = [("ve", ve_done), ("ve", evac_cnt[pb])] if c == 0 else ()
                tp = b.pe(lambda e, c=c, nb=nb, pb=pb: e.transpose(
                    ptr[pb][:, 128 * c:128 * c + 128],
                    xnb[nb][:, 128 * c:128 * c + 128], ident[:]),
                    waits=w2, inc="pe" if c == NC - 1 else None)
            ov = xnT1_all.rearrange("p (c t) -> p c t", c=NC)[
                :, :, 1 + 128 * t:1 + 128 * t + 128]
            evac_cnt[pb] = b.ve(lambda e, ov=ov, pb=pb: e.tensor_copy(
                ov, ptr[pb][:].rearrange("p (c t) -> p c t", c=NC)),
                waits=[("pe", tp)])
            pe_done = tp
    es.close()


def emit_l1a(b, consts, caps, xnT1, qT1f, a1_dram, nrm_dram):
    es_main = b.scope()
    es_kv = b.scope()
    kT64 = [b.sb(f"l1_kT64_{v}", [128, T], BF16, es_kv) for v in range(2)]
    V1 = b.sb("l1_V1", [128, NT * 256], BF16, es_kv)
    cb = b.sb("l1_cb", [128, 1], F32, es_kv)
    Qn = b.sb("l1_Qn", [128, 16], F32, es_kv)
    Kn = b.sb("l1_Kn", [128, 16], F32, es_kv)
    b.ve(lambda e: e.memset(Qn[:], 0.0))
    b.ve(lambda e: e.memset(Kn[:], 0.0))

    esP = b.scope()
    stg1 = [b.sb(f"l1_stage{i}", [128, 1024], F32, esP) for i in range(4)]
    cos4 = b.sb("l1_cos4", [128, T], F32, esP)
    sin4 = b.sb("l1_sin4", [128, T], F32, esP)
    lc = b.ld(lambda e: e.dma_start(out=cos4[:], in_=caps["cos4"][:]), slot=0)
    ls = b.ld(lambda e: e.dma_start(out=sin4[:], in_=caps["sin4"][:]), slot=1)
    wq1_b = _cast_load(b, esP, "l1_wq", caps["wq1"], 1024, 1024, None,
                       stages=stg1, slots=[4, 5, 6, 7])
    wk1_b = _cast_load(b, esP, "l1_wk", caps["wk1"], 1024, 256, None,
                       stages=stg1, slots=[4, 5, 6, 7])
    wv1_b = _cast_load(b, esP, "l1_wv", caps["wv1"], 1024, 256, None,
                       stages=stg1, slots=[4, 5, 6, 7])
    P1m = []
    P2m = []
    for v in range(2):
        t1_ = b.sb(f"cs2_P1_{v}", [128, 128], BF16, esP)
        t2_ = b.sb(f"cs2_P2_{v}", [128, 128], BF16, esP)
        b.ld(lambda e, t1_=t1_, v=v: e.dma_start(out=t1_[:], in_=caps[f"P1_{v}"][:]),
             waits=[("ve", b.cnt["ve"])], slot=7)
        b.ld(lambda e, t2_=t2_, v=v: e.dma_start(out=t2_[:], in_=caps[f"P2_{v}"][:]),
             waits=[("ve", b.cnt["ve"])], slot=7)
        P1m.append(t1_)
        P2m.append(t2_)
    ldP = ("ld7", b.cnt["ld7"])
    PL = b.ps("l1_PL", [128, 512], F32, esP)
    PH_ = b.ps("l1_PH", [128, 512], F32, esP)
    PX = b.ps("l1_PX", [128, 1024], F32, esP)
    PNq = b.ps("l1_PNq", [128, 512], F32, esP)
    t1 = b.sb("l1_t1", [128, 512], F32, esP)
    t2 = b.sb("l1_t2", [128, 512], F32, esP)
    rlo = b.sb("l1_rlo", [128, 512], BF16, esP)
    rhi = b.sb("l1_rhi", [128, 512], BF16, esP)
    q64 = b.sb("l1_q64", [128, 512], BF16, esP)
    sqr = b.sb("l1_sqr", [128, 512], F32, esP)
    nr1 = b.sb("l1_nr1", [128, 1], F32, esP)

    def rope_perm_block(jp, w, lo_cols, hi_cols, wmat, qdst, Nacc, ncol, wstride):
        pe_l = pe_h = None
        for ci in range(NC):
            pe_l = b.pe(lambda e, ci=ci, w=w, lo_cols=lo_cols, ws_=wstride,
                        wmat=wmat: e.matmul(
                PL[:], wmat[:, ws_ * ci + lo_cols:ws_ * ci + lo_cols + 128],
                xnT1[ci][:, 1 + 512 * w:1 + 512 * w + 512],
                start=(ci == 0), stop=(ci == NC - 1)),
                waits=[("ve", b.cnt["ve"])] if ci == 0 else (),
                inc="pe" if ci == NC - 1 else None)
        for ci in range(NC):
            pe_h = b.pe(lambda e, ci=ci, w=w, hi_cols=hi_cols, ws_=wstride,
                        wmat=wmat: e.matmul(
                PH_[:], wmat[:, ws_ * ci + hi_cols:ws_ * ci + hi_cols + 128],
                xnT1[ci][:, 1 + 512 * w:1 + 512 * w + 512],
                start=(ci == 0), stop=(ci == NC - 1)),
                inc="pe" if ci == NC - 1 else None)
        cw = [lc, ls, ldP] if (jp == 0 and w == 0) else []
        b.ve(lambda e, w=w: e.tensor_tensor(t1[:], PL[:], cos4[:, 512 * w:512 * w + 512],
                                            OP.mult), waits=[("pe", pe_l)] + cw)
        b.ve(lambda e, w=w: e.tensor_tensor(t2[:], PH_[:], sin4[:, 512 * w:512 * w + 512],
                                            OP.mult), waits=[("pe", pe_h)])
        b.ve(lambda e: e.tensor_tensor(rlo[:], t1[:], t2[:], OP.subtract))
        b.ve(lambda e, w=w: e.tensor_tensor(t1[:], PL[:], sin4[:, 512 * w:512 * w + 512],
                                            OP.mult))
        b.ve(lambda e, w=w: e.tensor_tensor(t2[:], PH_[:], cos4[:, 512 * w:512 * w + 512],
                                            OP.mult))
        vr = b.ve(lambda e: e.tensor_tensor(rhi[:], t1[:], t2[:], OP.add))
        mzn = b.ve(lambda e: e.memset(PNq[:], 0.0), waits=[("pe", b.cnt["pe"])])
        sq_l = b.ve(lambda e: e.tensor_tensor(sqr[:], rlo[:], rlo[:], OP.mult))
        pn = None
        for i in range(4):
            pn = b.pe(lambda e, i=i: e.matmul(
                PNq[32 * i:32 * i + 1, :], consts["onescol_f"][32 * i:32 * i + 32, :],
                sqr[32 * i:32 * i + 32, :], start=(i == 0), stop=False,
                tile_position=(32 * i, 32 * i), skip_group_check=True),
                waits=[("ve", sq_l), ("ve", mzn)] if i == 0 else (),
                inc="pe" if i == 3 else None)
        sq_h = b.ve(lambda e: e.tensor_tensor(sqr[:], rhi[:], rhi[:], OP.mult),
                    waits=[("pe", pn)])
        for i in range(4):
            pn = b.pe(lambda e, i=i: e.matmul(
                PNq[32 * i:32 * i + 1, :], consts["onescol_f"][32 * i:32 * i + 32, :],
                sqr[32 * i:32 * i + 32, :], start=False, stop=(i == 3),
                tile_position=(32 * i, 32 * i), skip_group_check=True),
                waits=[("ve", sq_h)] if i == 0 else (),
                inc="pe" if i == 3 else None)
        b.ve(lambda e: e.reduce_max(nr1[:], PNq[:], axis=AX.X), waits=[("pe", pn)])
        vq = b.ve(lambda e, ncol=ncol: e.tensor_tensor(
            Nacc[:, ncol:ncol + 1], Nacc[:, ncol:ncol + 1], nr1[:], OP.max))
        for v in range(2):
            pp = b.pe(lambda e, v=v: e.matmul(
                PX[:, 512 * v:512 * v + 512], P1m[v][:], rlo[:],
                start=True, stop=False), waits=[("ve", vq)] if v == 0 else (),
                inc=None)
            pp = b.pe(lambda e, v=v: e.matmul(
                PX[:, 512 * v:512 * v + 512], P2m[v][:], rhi[:],
                start=False, stop=True), inc="pe")
            qdst(v, pp)

    st_q = {"n": 0}

    def qdst_q(jp, w):
        def f(v, pp):
            u = 2 * jp + v
            vv = b.ve(lambda e, v=v: e.tensor_copy(q64[:], PX[:, 512 * v:512 * v + 512]),
                      waits=[("pe", pp)])
            b.st(lambda e, u=u, w=w: e.dma_start(
                out=qT1f[128 * u:128 * u + 128, 512 * w:512 * w + 512], in_=q64[:]),
                waits=[("ve", vv)], slot=st_q["n"] % 2)
            st_q["n"] += 1
            b.emit("ve", lambda e: e.nop(),
                   waits=[("st0", b.cnt["st0"]), ("st1", b.cnt["st1"])])
        return f

    def qdst_k(w):
        def f(v, pp):
            b.ve(lambda e, v=v, w=w: e.tensor_copy(
                kT64[v][:, 512 * w:512 * w + 512], PX[:, 512 * v:512 * v + 512]),
                waits=[("pe", pp)])
        return f

    for jp in range(4):
        for w in range(NW):
            rope_perm_block(jp, w, 128 * jp, 512 + 128 * jp, wq1_b,
                            qdst_q(jp, w), Qn, jp, 1024)
    for w in range(NW):
        rope_perm_block(0, w, 0, 128, wk1_b, qdst_k(w), Kn, 0, 256)
    vv = b.cnt["ve"]
    for t in range(NT):
        pe_v = None
        for ci in range(NC):
            pe_v = b.pe(lambda e, ci=ci, t=t: e.matmul(
                PL[:, 0:256], xnT1[ci][:, 1 + 128 * t:1 + 128 * t + 128],
                wv1_b[:, 256 * ci:256 * ci + 256],
                start=(ci == 0), stop=(ci == NC - 1)),
                waits=[("ve", vv)] if ci == 0 else (),
                inc="pe" if ci == NC - 1 else None)
        vv = b.ve(lambda e, t=t: e.tensor_copy(V1[:, 256 * t:256 * t + 256],
                                               PL[:, 0:256]), waits=[("pe", pe_v)])
    _build_shift_bias(b, esP, consts, Qn, Kn, nrm_dram, cb, "l1a", extra_scale=0.125)
    esP.close()

    aoT1 = [b.sb(f"l1_aoT{t}", [128, T], BF16, es_main, side="right") for t in range(8)]
    esA = b.scope()
    PS = b.ps("l1_PS", [128, 2048], F32, esA)
    PAV = b.ps("l1_PAV", [128, 512], F32, esA)
    PSMn = b.ps("l1_PSM", [128, 512], F32, esA)
    PB2 = b.ps("l1_PB2", [128, 512], F32, esA)
    PT = b.sb("l1_PT", [128, 2048], BF16, esA)
    qw64 = [b.sb(f"l1_qw64_{i}", [128, 512], BF16, esA) for i in range(2)]
    sums_sb = b.sb("l1_sums", [128, 512], F32, esA)
    b.ve(lambda e: e.memset(sums_sb[:], 0.0))
    pavc = b.sb("l1_pavc", [128, 512], F32, esA)
    bcast_sb = b.sb("l1_bc", [128, 512], F32, esA)
    recipb = b.sb("l1_recip", [128, 512], F32, esA)
    EL1 = []
    for nm in ("Ep1", "Ep2"):
        et = b.sb("cs2_" + nm, [128, 128], F32, esA)
        b.ld(lambda e, et=et, nm=nm: e.dma_start(out=et[:], in_=caps[nm][:]),
             waits=[("ve", b.cnt["ve"])], slot=7)
        EL1.append(et)
    b.emit("ve", lambda e: e.nop(), waits=[("ld7", b.cnt["ld7"])])

    wins = [(jp, v, w) for jp in range(4) for v in range(2) for w in range(NW)]
    ps_free = [b.cnt["act"], b.cnt["act"]]
    pt_free = [b.cnt["pe"], b.cnt["pe"]]
    qw_free = [b.cnt["pe"], b.cnt["pe"]]
    pav_ready = b.cnt["ve"]
    psm_ready = b.cnt["ve"]
    lq = {}

    def issue_lq(idx):
        if idx >= len(wins):
            return
        jp, v, w = wins[idx]
        qb_ = idx % 2
        u = 2 * jp + v
        lq[idx] = b.ld(lambda e, u=u, w=w, qb_=qb_: e.dma_start(
            out=qw64[qb_][:], in_=qT1f[128 * u:128 * u + 128, 512 * w:512 * w + 512]),
            waits=[("pe", qw_free[qb_])], slot=qb_)

    issue_lq(0)
    for idx, (jp, v, w) in enumerate(wins):
        qb_ = idx % 2
        mz = b.ve(lambda e: e.memset(PSMn[:], 0.0), waits=[("pe", min(pt_free)),
                                                           ("ve", psm_ready)])
        mz = b.ve(lambda e: e.memset(PAV[:], 0.0), waits=[("ve", pav_ready)])
        pending = None

        def mk_av(v, kt, bb, exp_t, last, mzw):
            def emit_av():
                r_ = None
                for ii in range(2):
                    g = 2 * v + ii
                    wl = list(mzw) if ii == 0 else []
                    if ii == 0:
                        wl.append(exp_t)
                    b.pe(lambda e, kt=kt, g=g, ii=ii, bb=bb, last=last: e.matmul(
                        PAV[64 * ii:64 * ii + 64, :],
                        V1[:, 256 * kt + 64 * g:256 * kt + 64 * g + 64],
                        PT[:, 1024 * bb + 512 * ii:1024 * bb + 512 * ii + 512],
                        start=(kt == 0 and ii == 0), stop=last,
                        tile_position=(0, 64 * ii), skip_group_check=True),
                        waits=wl)
                for ii in range(2):
                    g = 2 * v + ii
                    r_ = b.pe(lambda e, kt=kt, g=g, ii=ii, bb=bb, last=last: e.matmul(
                        PSMn[32 * g:32 * g + 1, :], consts["onescol_bf"][:],
                        PT[:, 1024 * bb + 512 * ii:1024 * bb + 512 * ii + 512],
                        start=(kt == 0 and ii == 0), stop=last,
                        tile_position=(0, 32 * g), skip_group_check=True),
                        inc="pe" if ii == 1 else None)
                pt_free[bb] = r_
            return emit_av

        for kt in range(NT):
            bb = kt % 2
            pe_s = None
            for ii in range(2):
                wl = [("act", ps_free[bb]), lq[idx]] if ii == 0 else []
                if ii == 0 and kt == 0:
                    wl.append(("ve", pav_ready))
                pe_s = b.pe(lambda e, kt=kt, v=v, ii=ii, bb=bb, qb_=qb_: e.matmul(
                    PS[:, 1024 * bb + 512 * ii:1024 * bb + 512 * ii + 512],
                    kT64[v][64 * ii:64 * ii + 64, 128 * kt:128 * kt + 128],
                    qw64[qb_][64 * ii:64 * ii + 64, :],
                    start=True, stop=True, tile_position=(64 * ii, 0)),
                    waits=wl, inc="pe" if ii == 1 else None)
            qw_free[qb_] = pe_s
            if kt == 2:
                issue_lq(idx + 1)
            exp_t = b.act(lambda e, bb=bb: e.activation(
                PT[:, 1024 * bb:1024 * bb + 1024],
                PS[:, 1024 * bb:1024 * bb + 1024], AF.Exp, bias=cb[:], scale=0.125),
                waits=[("pe", pe_s), ("pe", pt_free[bb])])
            ps_free[bb] = exp_t
            if pending is not None:
                pending()
            pending = mk_av(v, kt, bb, ("act", exp_t), kt == NT - 1,
                            [("ve", mz)] if kt == 0 else [])
        pending()
        pv = b.ve(lambda e: e.tensor_copy(pavc[:], PAV[:]),
                  waits=[("pe", pt_free[(NT - 1) % 2])])
        sv = None
        for ii in range(2):
            g = 2 * v + ii
            sv = b.ve(lambda e, g=g: e.tensor_copy(
                sums_sb[32 * g:32 * g + 1, :], PSMn[32 * g:32 * g + 1, :]))
        pav_ready = pv
        psm_ready = sv
        pb_ = b.pe(lambda e, v=v: e.matmul(PB2[:], EL1[v][:], sums_sb[:],
                                           start=True, stop=True),
                   waits=[("ve", sv)], inc="pe")
        b.ve(lambda e: e.tensor_copy(bcast_sb[:], PB2[:]), waits=[("pe", pb_)])
        b.ve(lambda e: e.reciprocal(recipb[:], bcast_sb[:]))
        b.ve(lambda e, jp=jp, v=v, w=w: e.tensor_tensor(
            aoT1[2 * jp + v][0:64, 512 * w:512 * w + 512], pavc[0:64, :],
            recipb[0:64, :], OP.mult))
        b.ve(lambda e, jp=jp, v=v, w=w: e.tensor_tensor(
            aoT1[2 * jp + v][64:128, 512 * w:512 * w + 512], pavc[64:128, :],
            recipb[64:128, :], OP.mult))
    esA.close()
    es_kv.close()

    esO = b.scope()
    stgF = [b.sb(f"l1_stage3{i}", [128, 1024], F32, esO) for i in range(4)]
    wo1_b = _cast_load(b, esO, "l1_wo", caps["wo1"], 1024, 1024, None,
                       stages=stgF, slots=[4, 5, 6, 7])
    PO = b.ps("l1_PO", [128, 1024], F32, esO)
    PTf = [b.ps(f"l1_PTf{i}", [128, 128], F32, esO) for i in range(2)]
    of32 = [b.sb(f"l1_of32_{i}", [128, 512], F32, esO) for i in range(2)]
    obuf = b.sb("l1_obuf", [128, 4096], F32, esO)

    ptf_evac = [b.cnt["ve"], b.cnt["ve"]]
    po_free = [b.cnt["ve"], b.cnt["ve"]]
    for w in range(NW):
        for do in range(NC):
            bb = do % 2
            pe_o = None
            for t in range(8):
                pe_o = b.pe(lambda e, w=w, do=do, t=t, bb=bb: e.matmul(
                    PO[:, 512 * bb:512 * bb + 512],
                    wo1_b[:, 1024 * t + 128 * do:1024 * t + 128 * do + 128],
                    aoT1[t][:, 512 * w:512 * w + 512],
                    start=(t == 0), stop=(t == 7)),
                    waits=[("ve", po_free[bb])] if t == 0 else (),
                    inc="pe" if t == 7 else None)
            po_free[bb] = b.ve(lambda e, bb=bb: e.tensor_copy(
                of32[bb][:], PO[:, 512 * bb:512 * bb + 512]), waits=[("pe", pe_o)])
            for tt in range(4):
                pb = tt % 2
                tp = b.pe(lambda e, tt=tt, pb=pb, bb=bb: e.transpose(
                    PTf[pb][:], of32[bb][:, 128 * tt:128 * tt + 128],
                    consts["identf"][:]),
                    waits=[("ve", po_free[bb]), ("ve", ptf_evac[pb])], inc="pe")
                ptf_evac[pb] = b.ve(lambda e, tt=tt, do=do, pb=pb: e.tensor_copy(
                    obuf[:, 1024 * tt + 128 * do:1024 * tt + 128 * do + 128],
                    PTf[pb][:]), waits=[("pe", tp)])
        for tt in range(4):
            b.st(lambda e, w=w, tt=tt: e.dma_start(
                out=a1_dram[512 * w + 128 * tt:512 * w + 128 * tt + 128, :],
                in_=obuf[:, 1024 * tt:1024 * tt + 1024]),
                waits=[("ve", max(ptf_evac))], slot=tt)
        nv = b.emit("ve", lambda e: e.nop(),
                    waits=[(f"st{i}", b.cnt[f"st{i}"]) for i in range(4)])
        ptf_evac = [nv, nv]
    esO.close()
    es_main.close()
    return aoT1


def emit_l1ffn_final(b, consts, caps, xnT1, x1_dram, a1_dram, out_dram):
    es = b.scope()
    mur = b.sb("g_mur", [128, 8], F32, es)
    muk = b.sb("g_muk", [128, 8], F32, es)
    l1_ = b.ld(lambda e: e.dma_start(out=mur[:], in_=caps["mur"][:]),
               waits=[("ve", b.cnt["ve"])], slot=0)
    l2_ = b.ld(lambda e: e.dma_start(out=muk[:], in_=caps["muk"][:]),
               waits=[("ve", b.cnt["ve"])], slot=1)
    dbuf = b.sb("g_d", [128, 512], F32, es)
    xrb = b.sb("g_xrb", [128, 4096], BF16, es)
    xkb = b.sb("g_xkb", [128, 4096], BF16, es)
    tsb = b.sb("g_t", [128, 4096], F32, es)
    kab = b.sb("g_ka", [128, 4096], BF16, es)
    kaf = b.sb("g_kaf", [128, 512], F32, es)
    of32 = b.sb("g_of32", [128, 512], F32, es)
    obuf = b.sb("g_obuf", [128, 4096], F32, es)
    x1s = [b.sb(f"g_x1{i}", [128, D], F32, es) for i in range(2)]
    a1s = [b.sb(f"g_a1{i}", [128, D], F32, es) for i in range(2)]
    outbs = [b.sb(f"g_outb{i}", [128, D], F32, es) for i in range(2)]
    PR = b.ps("g_PR", [128, 512], F32, es)
    PK = b.ps("g_PK", [128, 512], F32, es)
    PV = b.ps("g_PV", [128, 512], F32, es)
    PTf = [b.ps(f"g_PTf{i}", [128, 128], F32, es) for i in range(2)]

    castc = [b.cnt["ve"]] * 4
    wstages = [x1s[0], x1s[1], a1s[0], a1s[1]]

    def wload(name, cap, scale=None):
        out = b.sb(name + "_b", [128, 8192], BF16, es)
        for c in range(8):
            sbf = c % 4
            ld = b.ld(lambda e, c=c, sbf=sbf, cap=cap: e.dma_start(
                out=wstages[sbf][:], in_=cap[128 * c:128 * c + 128, :]),
                waits=[("ve", castc[sbf])], slot=4 + sbf)
            if scale is None:
                castc[sbf] = b.ve(lambda e, c=c, sbf=sbf, out=out: e.tensor_copy(
                    out[:, 1024 * c:1024 * c + 1024], wstages[sbf][:]), waits=[ld])
            else:
                castc[sbf] = b.ve(lambda e, c=c, sbf=sbf, out=out: e.tensor_scalar(
                    out[:, 1024 * c:1024 * c + 1024], wstages[sbf][:], scale,
                    None, OP.mult), waits=[ld])
        return out

    wrT_b = wload("g_wr", caps["wrT"])
    wkT_b = wload("g_wk", caps["wkT1"])
    wvT_b = wload("g_wv", caps["wvT1"], scale=0.5)
    mu_w = b.ve(lambda e: e.nop(), waits=[l1_, l2_])
    ptf_evac = [b.cnt["ve"], b.cnt["ve"]]
    lastx = {0: mu_w, 1: mu_w}
    lasta = {0: mu_w, 1: mu_w}
    outg = {0: ("st0", 0), 1: ("st1", 0)}
    stg = [0, 0, 0, 0]
    pload = {}
    vmix = 0
    kafmax = 0
    kafmult = 0
    sttc = 0

    def mix_chunk(w, c, pe_guard):
        wl = []
        if c == 0:
            wl = [("pe", pe_guard)]
            if w == 0:
                wl.append(("ve", mu_w))
        b.ve(lambda e, c=c, w=w: e.tensor_tensor(
            dbuf[:], xnT1[c][:, 1 + 512 * w:1 + 512 * w + 512],
            xnT1[c][:, 512 * w:512 * w + 512], OP.subtract),
            waits=wl)
        b.ve(lambda e, c=c, w=w: e.scalar_tensor_tensor(
            xrb[:, 512 * c:512 * c + 512], dbuf[:], mur[:, c:c + 1],
            xnT1[c][:, 512 * w:512 * w + 512], OP.mult, OP.add))
        return b.ve(lambda e, c=c, w=w: e.scalar_tensor_tensor(
            xkb[:, 512 * c:512 * c + 512], dbuf[:], muk[:, c:c + 1],
            xnT1[c][:, 512 * w:512 * w + 512], OP.mult, OP.add))

    def prefetch(wa, tt):
        t = 4 * wa + tt
        buf = tt % 2
        lx = b.ld(lambda e, t=t, buf=buf: e.dma_start(
            out=x1s[buf][:], in_=x1_dram[128 * t:128 * t + 128, :]),
            waits=[("ve", lastx[buf])], slot=buf)
        la = b.ld(lambda e, t=t, buf=buf: e.dma_start(
            out=a1s[buf][:], in_=a1_dram[128 * t:128 * t + 128, :]),
            waits=[("ve", lasta[buf])], slot=2 + buf)
        pload[(wa, tt)] = (lx, la)

    def emit_assemble(wa):
        for tt in range(4):
            t = 4 * wa + tt
            buf = tt % 2
            lx, la = pload.pop((wa, tt))
            a1c = b.ve(lambda e, tt=tt, buf=buf: e.tensor_tensor(
                outbs[buf][:], obuf[:, 1024 * tt:1024 * tt + 1024], x1s[buf][:],
                OP.add), waits=[lx, outg[buf]])
            lastx[buf] = a1c
            a2c = b.ve(lambda e, buf=buf: e.tensor_tensor(
                outbs[buf][:], outbs[buf][:], a1s[buf][:], OP.add), waits=[la])
            lasta[buf] = a2c
            sret = b.st(lambda e, t=t, buf=buf: e.dma_start(
                out=out_dram[128 * t:128 * t + 128, :], in_=outbs[buf][:]),
                waits=[("ve", a2c), (f"st{tt}", stg[tt])], slot=tt)
            stg[tt] = sret[1]
            outg[buf] = sret
            if tt < 2:
                prefetch(wa, tt + 2)

    for w in range(NW):
        if w == 0:
            for c in range(NC):
                vmix = mix_chunk(0, c, b.cnt["pe"])
        else:
            emit_assemble(w - 1)
        for do in range(NC):
            pe_r = None
            for ci in range(NC):
                pe_r = b.pe(lambda e, ci=ci, do=do: e.matmul(
                    PR[:], wrT_b[:, 1024 * ci + 128 * do:1024 * ci + 128 * do + 128],
                    xrb[:, 512 * ci:512 * ci + 512],
                    start=(ci == 0), stop=(ci == NC - 1)),
                    waits=[("ve", vmix), ("act", b.cnt["act"])] if ci == 0 else (),
                    inc="pe" if ci == NC - 1 else None)
            b.act(lambda e, do=do: e.activation(tsb[:, 512 * do:512 * do + 512], PR[:],
                                                AF.Tanh, bias=0.0, scale=0.5),
                  waits=[("pe", pe_r)])
            kw = vmix if do == 0 else kafmax
            pe_k = None
            for ci in range(NC):
                pe_k = b.pe(lambda e, ci=ci, do=do: e.matmul(
                    PK[:], wkT_b[:, 1024 * ci + 128 * do:1024 * ci + 128 * do + 128],
                    xkb[:, 512 * ci:512 * ci + 512],
                    start=(ci == 0), stop=(ci == NC - 1)),
                    waits=[("ve", kw)] if ci == 0 else (),
                    inc="pe" if ci == NC - 1 else None)
            kafmax = b.ve(lambda e: e.tensor_scalar(kaf[:], PK[:], 0.0, None, OP.max),
                          waits=[("pe", pe_k)])
            kafmult = b.ve(lambda e, do=do: e.tensor_tensor(
                kab[:, 512 * do:512 * do + 512], kaf[:], kaf[:], OP.mult))
        aw = b.cnt["act"]
        pe_guard_mix = b.cnt["pe"]
        for do in range(NC):
            pvw = kafmult if do == 0 else sttc
            pe_v = None
            for ci in range(NC):
                pe_v = b.pe(lambda e, ci=ci, do=do: e.matmul(
                    PV[:], wvT_b[:, 1024 * ci + 128 * do:1024 * ci + 128 * do + 128],
                    kab[:, 512 * ci:512 * ci + 512],
                    start=(ci == 0), stop=(ci == NC - 1)),
                    waits=[("ve", pvw)] if ci == 0 else (),
                    inc="pe" if ci == NC - 1 else None)
            sttc = b.ve(lambda e, do=do: e.scalar_tensor_tensor(
                of32[:], tsb[:, 512 * do:512 * do + 512], 1.0, PV[:],
                OP.add, OP.mult), waits=[("pe", pe_v), ("act", aw)])
            for tt in range(4):
                pb = tt % 2
                tp = b.pe(lambda e, tt=tt, pb=pb: e.transpose(
                    PTf[pb][:], of32[:, 128 * tt:128 * tt + 128], consts["identf"][:]),
                    waits=[("ve", sttc), ("ve", ptf_evac[pb])], inc="pe")
                ptf_evac[pb] = b.ve(lambda e, tt=tt, do=do, pb=pb: e.tensor_copy(
                    obuf[:, 1024 * tt + 128 * do:1024 * tt + 128 * do + 128],
                    PTf[pb][:]), waits=[("pe", tp)])
            if w + 1 < NW:
                vmix = mix_chunk(w + 1, do, pe_guard_mix)
            if do in (4, 6):
                prefetch(w, (do - 4) // 2)
    emit_assemble(NW - 1)
    es.close()



def _consts_host():
    c = {}
    c["ident"] = np.eye(128, dtype=BF)
    c["identf"] = np.eye(128, dtype=np.float32)
    ones = np.ones((128, 1), dtype=BF)
    c["onescol_bf"] = ones
    c["onescol_f"] = np.ones((128, 1), dtype=np.float32)
    c["onesrow_f"] = np.ones((1, 128), dtype=np.float32)
    for d in range(4):
        p = np.arange(128)[:, None]
        f = np.arange(512)[None, :]
        c[f"mask{d}"] = ((128 * d + p) <= f).astype(BF)
        c[f"mask2_{d}"] = np.concatenate([c[f"mask{d}"], c[f"mask{d}"]], axis=1)
    epair = np.zeros((128, 128), dtype=np.float32)
    epair[32, 0:32] = 1.0
    epair[96, 64:96] = 1.0
    c["Epair"] = epair
    e4 = np.zeros((128, 128), dtype=np.float32)
    for m in range(128):
        e4[32 * (m // 32), m] = 1.0
    c["E4s"] = e4
    er0 = np.zeros((128, 128), dtype=np.float32)
    er0[0, :] = 1.0
    c["Erow0"] = er0
    ep1 = np.zeros((128, 128), dtype=np.float32)
    ep1[0, 0:64] = 1.0
    ep1[32, 64:128] = 1.0
    c["Ep1"] = ep1
    ep2 = np.zeros((128, 128), dtype=np.float32)
    ep2[64, 0:64] = 1.0
    ep2[96, 64:128] = 1.0
    c["Ep2"] = ep2
    for v in range(2):
        p1 = np.zeros((128, 128), dtype=BF)
        p2 = np.zeros((128, 128), dtype=BF)
        for d in range(32):
            p1[64 * v + d, d] = 1
            p2[64 * v + d, 32 + d] = 1
            p1[64 * v + 32 + d, 64 + d] = 1
            p2[64 * v + 32 + d, 96 + d] = 1
        c[f"P1_{v}"] = p1
        c[f"P2_{v}"] = p2
    half = 32
    inv = 1.0 / (10000.0 ** (np.arange(half, dtype=np.float64) * 2.0 / 64.0))
    pos = np.arange(T, dtype=np.float64)
    fr = pos[:, None] * inv[None, :]
    cos = np.cos(fr).astype(np.float32).T
    sin = np.sin(fr).astype(np.float32).T
    c["cos4"] = np.tile(cos, (4, 1))
    c["sin4"] = np.tile(sin, (4, 1))
    return c


def _declare_consts(nc, host):
    aps = {}
    for k, v in host.items():
        dt = BF16 if v.dtype == BF else F32
        aps[k] = nc.dram_tensor("c_" + k, list(v.shape), dt, kind="ExternalInput").ap()
    return aps


def load_const(b, caps, name, shape, dt):
    t = b.sb("cs_" + name, shape, dt)
    ld = b.ld(lambda e: e.dma_start(out=t[:], in_=caps[name][:]), slot=7)
    return t, ld


def build_program():
    nc = bass.Bass("TRN2", target_bir_lowering=False)
    x = nc.dram_tensor("x", [T, D], F32, kind="ExternalInput").ap()
    host_consts = _consts_host()
    caps = _declare_consts(nc, host_consts)
    caps["wq"] = nc.dram_tensor("wq", [D, D], F32, kind="ExternalInput").ap()
    caps["wkv"] = nc.dram_tensor("wkv", [D, 128], F32, kind="ExternalInput").ap()
    caps["wk"] = nc.dram_tensor("wk", [128, 512], F32, kind="ExternalInput").ap()
    caps["wv"] = nc.dram_tensor("wv", [128, 512], F32, kind="ExternalInput").ap()
    caps["wosel"] = nc.dram_tensor("wosel", [D, D], F32, kind="ExternalInput").ap()
    caps["qgv"] = nc.dram_tensor("qgv", [1, 64], F32, kind="ExternalInput").ap()
    caps["kgv"] = nc.dram_tensor("kgv", [1, 32], F32, kind="ExternalInput").ap()
    for nm in ("qwT", "kwT", "vwT", "owT"):
        caps[nm] = nc.dram_tensor(nm, [D, D], F32, kind="ExternalInput").ap()
    for nm in ("qb", "kb", "ob"):
        caps[nm] = nc.dram_tensor(nm, [128, 8], F32, kind="ExternalInput").ap()
    caps["vb"] = nc.dram_tensor("vb", [1, D], F32, kind="ExternalInput").ap()
    caps["nw"] = nc.dram_tensor("nw", [1, D], F32, kind="ExternalInput").ap()
    caps["nb"] = nc.dram_tensor("nb", [1, D], F32, kind="ExternalInput").ap()
    caps["wq1"] = nc.dram_tensor("wq1", [D, D], F32, kind="ExternalInput").ap()
    caps["wk1"] = nc.dram_tensor("wk1", [D, 256], F32, kind="ExternalInput").ap()
    caps["wv1"] = nc.dram_tensor("wv1", [D, 256], F32, kind="ExternalInput").ap()
    caps["wo1"] = nc.dram_tensor("wo1", [D, D], F32, kind="ExternalInput").ap()
    for nm in ("wrT", "wkT1", "wvT1"):
        caps[nm] = nc.dram_tensor(nm, [D, D], F32, kind="ExternalInput").ap()
    caps["mur"] = nc.dram_tensor("mur", [128, 8], F32, kind="ExternalInput").ap()
    caps["muk"] = nc.dram_tensor("muk", [128, 8], F32, kind="ExternalInput").ap()
    nrm_dram = nc.dram_tensor("nrm", [1, 4096], F32, kind="Internal").ap()
    qT0f = nc.dram_tensor("qT0f", [D, T], BF16, kind="Internal").ap()
    kT0f = nc.dram_tensor("kT0f", [D, T], BF16, kind="Internal").ap()
    qT1f = nc.dram_tensor("qT1f", [D, T], BF16, kind="Internal").ap()
    aoT0f = nc.dram_tensor("aoT0f", [D, T], BF16, kind="Internal").ap()
    upto = _DBG["upto"] or "Z"
    kd = "ExternalOutput" if _DBG["dump"] else "Internal"
    a0_dram = nc.dram_tensor("a0", [T, D], F32, kind=kd).ap()
    f0_dram = nc.dram_tensor("f0", [T, D], F32, kind=kd).ap()
    x1_dram = nc.dram_tensor("x1", [T, D], F32, kind=kd).ap()
    a1_dram = nc.dram_tensor("a1", [T, D], F32, kind=kd).ap()
    out_dram = nc.dram_tensor("out", [T, D], F32, kind="ExternalOutput").ap()

    dbg = {}
    if _DBG["dump"]:
        dbg["xnT"] = nc.dram_tensor("dbg_xnT", [NC * 128, T], BF16, kind="ExternalOutput").ap()
        dbg["stats"] = nc.dram_tensor("dbg_stats", [128, 64], F32, kind="ExternalOutput").ap()

    b = Bld(nc)
    with b.ctx:
        consts = {}
        for nm in ("ident", "identf", "onescol_bf", "onescol_f", "onesrow_f"):
            consts[nm], _ = load_const(b, caps, nm, list(_chost_shape(host_consts[nm])),
                                       BF16 if host_consts[nm].dtype == BF else F32)
        consts["eps5"] = b.fconst(1e-5)
        consts["eps6"] = b.fconst(1e-6)
        b.barrier()

        stats0 = b.sb("stats0", [128, 64], F32)
        es_xnT = b.scope()
        xnT_all = b.sb("xnT_all", [128, NC * T], BF16, es_xnT, side="right")
        xnT = [xnT_all[:, c * T:(c + 1) * T] for c in range(NC)]
        emit_ln_xnT(b, consts, x, stats0, xnT, ext=False, xnT_all=xnT_all)

        if _DBG["dump"]:
            for c in range(NC):
                b.st(lambda e, c=c: e.dma_start(out=dbg["xnT"][128 * c:128 * c + 128, :],
                                                in_=xnT[c][:]),
                     waits=[("ve", b.cnt["ve"])], slot=c % 4)
            b.st(lambda e: e.dma_start(out=dbg["stats"][:], in_=stats0[:]),
                 waits=[("ve", b.cnt["ve"])], slot=3)
            b.barrier()

        if upto.startswith("B") or upto >= "C":
            emit_l0a(b, consts, caps, xnT, es_xnT, a0_dram, nrm_dram)
        else:
            es_xnT.close()

        if upto >= "C" and not upto.startswith("B"):
            emit_l0ffn(b, consts, caps, x, stats0, f0_dram, qT0f, kT0f, aoT0f)

        if upto >= "D" and not upto.startswith("B"):
            stats1 = b.sb("stats1", [128, 64], F32)
            es_xnT1 = b.scope()
            xnT1_all = b.sb("xnT1_all", [128, NC * (T + 1)], BF16, es_xnT1,
                            side="right")
            xnT1 = [xnT1_all[:, c * (T + 1):(c + 1) * (T + 1)] for c in range(NC)]
            emit_assembly_ln(b, consts, x, a0_dram, f0_dram, stats0, True,
                             x1_dram, stats1, xnT1, xnT1_all)
        if upto >= "F" and not upto.startswith("B"):
            emit_l1a(b, consts, caps, xnT1, qT1f, a1_dram, nrm_dram)
        if upto >= "G" and not upto.startswith("B"):
            emit_l1ffn_final(b, consts, caps, xnT1, x1_dram, a1_dram, out_dram)
        if upto >= "D" and not upto.startswith("B"):
            es_xnT1.close()

        b.emit("gp", lambda e: e.nop(),
               waits=[(f"st{i}", b.cnt[f"st{i}"]) for i in range(4)])
        b.validate_counts()
        with nc.Block() as blk:
            b.replay(blk)
    return nc, host_consts


def _chost_shape(v):
    return v.shape


def _host_inputs(inputs, core):
    m = {}
    assert inputs["L0_ffn_router_prob"].shape[2] == 4
    m["x"] = np.ascontiguousarray(inputs["L0_attn_x"][core], dtype=np.float32)
    m["wq"] = inputs["L0_attn_Wq"]
    m["wkv"] = inputs["L0_attn_Wkv"]
    m["wk"] = inputs["L0_attn_Wk"]
    m["wv"] = inputs["L0_attn_Wv"]
    wo = inputs["L0_attn_Wo"]
    sel = np.zeros((D, D), dtype=np.float32)
    for j in range(4):
        for p in range(2):
            s_ = 2 * j + p
            h0, h1 = 4 * j + 2 * p, 4 * j + 2 * p + 1
            sel[128 * s_:128 * s_ + 32, :] = wo[64 * h0:64 * h0 + 32, :]
            sel[128 * s_ + 64:128 * s_ + 96, :] = wo[64 * h1:64 * h1 + 32, :]
    m["wosel"] = sel
    m["qgv"] = np.ascontiguousarray(inputs["L0_attn_q_g"].reshape(1, 64), dtype=np.float32)
    m["kgv"] = np.ascontiguousarray(inputs["L0_attn_k_g"].reshape(1, 32), dtype=np.float32)
    m["qwT"] = np.ascontiguousarray(inputs["L0_ffn_q_w"].T)
    m["kwT"] = np.ascontiguousarray(inputs["L0_ffn_k_w"].T)
    m["vwT"] = np.ascontiguousarray(inputs["L0_ffn_v_w"].T)
    m["owT"] = np.ascontiguousarray(inputs["L0_ffn_o_w"].T)
    for nm, key in (("qb", "L0_ffn_q_b"), ("kb", "L0_ffn_k_b"), ("ob", "L0_ffn_o_b")):
        m[nm] = np.ascontiguousarray(inputs[key].reshape(8, 128).T, dtype=np.float32)
    m["vb"] = np.ascontiguousarray(inputs["L0_ffn_v_b"].reshape(1, D), dtype=np.float32)
    m["nw"] = np.ascontiguousarray(inputs["L0_ffn_norm_w"].reshape(1, D), dtype=np.float32)
    m["nb"] = np.ascontiguousarray(inputs["L0_ffn_norm_b"].reshape(1, D), dtype=np.float32)
    perm_q = []
    for half in range(2):
        for jp in range(4):
            for i in range(4):
                h = 4 * i + jp
                perm_q.extend(range(64 * h + 32 * half, 64 * h + 32 * half + 32))
    m["wq1"] = np.ascontiguousarray(inputs["L1_attn_w_q"][:, perm_q])
    perm_k = []
    for half in range(2):
        for g in range(4):
            perm_k.extend(range(64 * g + 32 * half, 64 * g + 32 * half + 32))
    m["wk1"] = np.ascontiguousarray(inputs["L1_attn_w_k"][:, perm_k])
    m["wv1"] = inputs["L1_attn_w_v"]
    perm_o = []
    for u in range(8):
        jp, v = u // 2, u % 2
        for h in (8 * v + jp, 8 * v + 4 + jp):
            perm_o.extend(range(64 * h, 64 * h + 64))
    m["wo1"] = np.ascontiguousarray(inputs["L1_attn_w_o"][perm_o, :])
    m["wrT"] = np.ascontiguousarray(inputs["L1_ffn_W_r"].T)
    m["wkT1"] = np.ascontiguousarray(inputs["L1_ffn_W_k"].T)
    m["wvT1"] = np.ascontiguousarray(inputs["L1_ffn_W_v"].T)
    m["mur"] = np.ascontiguousarray(inputs["L1_ffn_mu_r"].reshape(8, 128).T,
                                    dtype=np.float32)
    m["muk"] = np.ascontiguousarray(inputs["L1_ffn_mu_k"].reshape(8, 128).T,
                                    dtype=np.float32)
    return m


def kernel(**inputs):
    nc, host_consts = build_program()
    in_maps = []
    for core in range(8):
        m = _host_inputs(inputs, core)
        for k, v in host_consts.items():
            m["c_" + k] = v
        in_maps.append(m)
    res = run_bass_kernel_spmd(nc, in_maps, core_ids=list(range(8)))
    out = np.stack([r["out"] for r in res.results], axis=0)
    return out


# version 20
# speedup vs baseline: 1.0036x; 1.0009x over previous
import numpy as np
import ml_dtypes
from contextlib import ExitStack

import concourse.bass as bass
import concourse.mybir as mybir
from concourse.alu_op_type import AluOpType as OP
from concourse.bass_utils import run_bass_kernel_spmd

F32 = mybir.dt.float32
BF16 = mybir.dt.bfloat16
AF = mybir.ActivationFunctionType
AX = mybir.AxisListType
BF = ml_dtypes.bfloat16

T = 4096
D = 1024
NT = T // 128
NW = T // 512
NC = D // 128

_DBG = {"upto": None, "dump": False}

ENGINES = ("pe", "act", "ve", "sync", "gp")
_ENG_ATTR = {"pe": "tensor", "act": "scalar", "ve": "vector", "sync": "sync", "gp": "gpsimd"}


class Bld:

    def __init__(self, nc):
        self.nc = nc
        self.q = {e: [] for e in ENGINES}
        self.ctx = ExitStack()
        self.sem = {}
        names = ["pe", "act", "ve", "gp"]
        names += [f"ld{i}" for i in range(8)] + [f"st{i}" for i in range(4)]
        for name in names:
            self.sem[name] = self.ctx.enter_context(nc.semaphore("sem_" + name))
        self.cnt = {k: 0 for k in self.sem}
        self.maxwait = {k: 0 for k in self.sem}

    AUTOSYNC = ("act", "ve", "gp")

    def emit(self, eng, fn, waits=(), inc=None, n=1, dma=False):
        waits = list(waits)
        if eng in self.AUTOSYNC and dma:
            waits.append((eng, self.cnt[eng]))
        elif eng in self.AUTOSYNC:
            sname = eng
            waits.append((sname, self.cnt[sname]))
            assert inc in (None, sname)
            inc, n = sname, 1
        elif eng == "pe" and inc == "pe":
            waits.append(("pe", self.cnt["pe"]))
        for s, v in waits:
            self.maxwait[s] = max(self.maxwait[s], v)
        if inc is not None:
            self.cnt[inc] += n
            target = self.cnt[inc]
        else:
            target = None
        incsem = self.sem[inc] if inc is not None else None
        w = [(self.sem[s], v) for s, v in waits]

        def run(e):
            for sh, v in w:
                e.wait_ge(sh, v)
            r = fn(e)
            if incsem is not None:
                r.then_inc(incsem, n)

        self.q[eng].append(run)
        return target

    def pe(self, fn, waits=(), inc=None):
        return self.emit("pe", fn, waits, inc)

    def act(self, fn, waits=(), inc=None):
        return self.emit("act", fn, waits, inc)

    def ve(self, fn, waits=(), inc=None):
        return self.emit("ve", fn, waits, inc)

    def ld(self, fn, waits=(), slot=0):
        s = f"ld{slot % 8}"
        return s, self.emit("sync", fn, waits, s, 16)

    def st(self, fn, waits=(), slot=0):
        s = f"st{slot % 4}"
        return s, self.emit("gp", fn, waits, s, 16, dma=True)

    def gp(self, fn, waits=(), inc=None):
        return self.emit("gp", fn, waits, inc if inc else None)

    def barrier(self):
        snap = {}
        for e, s in (("pe", "pe"), ("act", "act"), ("ve", "ve"), ("gp", "gp")):
            snap[s] = self.emit(e, lambda en: en.nop(), inc=s if e == "pe" else None)
        for i in range(8):
            snap[f"ld{i}"] = self.cnt[f"ld{i}"]
        for i in range(4):
            snap[f"st{i}"] = self.cnt[f"st{i}"]
        for e in ("pe", "act", "ve", "gp", "sync"):
            waits = [(s, snap[s]) for s in snap]

            def mk(waits=waits):
                def run(en):
                    for sname, v in waits:
                        self.maxwait[sname] = max(self.maxwait[sname], v)
                        en.wait_ge(self.sem[sname], v)
                return run
            self.q[e].append(mk())

    def replay(self, blk):
        for e in ENGINES:
            lst = self.q[e]
            if not lst:
                continue
            getattr(blk, _ENG_ATTR[e])(lambda eng, lst=lst: [f(eng) for f in lst])

    def validate_counts(self):
        for s in self.sem:
            assert self.maxwait[s] <= self.cnt[s], (
                f"wait on {s} for {self.maxwait[s]} but final count {self.cnt[s]}")

    def sb(self, name, shape, dt, es=None, side=None):
        kw = {"side": side} if side else {}
        return (es or self.ctx).enter_context(self.nc.sbuf_tensor(name, shape, dt, **kw))

    def ps(self, name, shape, dt, es=None):
        return (es or self.ctx).enter_context(self.nc.psum_tensor(name, shape, dt))

    def scope(self):
        b = self

        class _S(ExitStack):
            def __exit__(self, *a):
                r = ExitStack.__exit__(self, *a)
                b.barrier()
                return r
        return _S()

    def fconst(self, value):
        key = ("fc", value)
        if not hasattr(self, "_fc"):
            self._fc = {}
        if key not in self._fc:
            t = self.sb(f"fc_{len(self._fc)}", [128, 1], F32)
            self.ve(lambda e: e.memset(t[:], float(value)), inc="ve")
            self._fc[key] = t
        return self._fc[key]



def emit_ln_xnT(b, consts, x_dram, stats_sb, xnT, ext=False, stats_dram=None,
                xnT_all=None):
    nc = b.nc
    off = 1 if ext else 0
    es = b.scope()
    tag = x_dram.tensor.name
    xs = [b.sb(f"ln_x{i}_{tag}", [128, D], F32, es) for i in range(4)]
    st6 = b.sb(f"ln_st6_{tag}", [128, 12], F32, es)
    mvs = [b.sb(f"ln_mv{i}_{tag}", [128, 2], F32, es) for i in range(2)]
    lnv = b.sb(f"ln_lnv_{tag}", [128, 1], F32, es)
    xnb = [b.sb(f"ln_xnb{i}_{tag}", [128, D], BF16, es) for i in range(2)]
    ptr = [b.ps(f"ln_pt{i}_{tag}", [128, 1024], BF16, es) for i in range(2)]
    ident = consts["ident"]

    if ext:
        for c in range(NC):
            b.ve(lambda e, c=c: e.memset(xnT[c][:, 0:1], 0.0), inc="ve")

    pe_done = b.cnt["pe"]
    evac_cnt = [b.cnt["ve"], b.cnt["ve"]]
    scal_hist = [b.cnt["ve"]] * 4
    eps5 = consts["eps5"]
    ars = {}
    for tl in range(NT + 1):
        if tl < NT:
            t, buf, nb = tl, tl % 4, tl % 2
            ld_t = b.ld(lambda e, t=t, buf=buf: e.dma_start(
                out=xs[buf][:], in_=x_dram[128 * t:128 * t + 128, :]),
                waits=[("ve", scal_hist[buf])] if t >= 4 else (), slot=buf)
            s1 = b.ve(lambda e, buf=buf: e.bn_stats(st6[:, 0:6], xs[buf][:, 0:512]),
                      waits=[ld_t], inc="ve")
            s2 = b.ve(lambda e, buf=buf: e.bn_stats(st6[:, 6:12], xs[buf][:, 512:1024]),
                      waits=[("ve", s1)], inc="ve")
            ve_mv = b.ve(lambda e, nb=nb: e.bn_aggr(mvs[nb][:], st6[:]),
                         waits=[("ve", s2)], inc="ve")
            b.ve(lambda e, t=t, nb=nb: e.tensor_copy(stats_sb[:, 2 * t:2 * t + 1],
                                                     mvs[nb][:, 0:1]),
                 waits=[("ve", ve_mv)])
            b.act(lambda e, nb=nb: e.activation(lnv[:], mvs[nb][:, 1:2], AF.Ln,
                                                bias=eps5[:], scale=1.0),
                  waits=[("ve", ve_mv)])
            ars[t] = b.act(lambda e, t=t: e.activation(
                stats_sb[:, 2 * t + 1:2 * t + 2], lnv[:], AF.Exp, bias=0.0, scale=-0.5),
                inc="act")
        if tl >= 1:
            t, buf, nb = tl - 1, (tl - 1) % 4, (tl - 1) % 2
            ve_done = b.ve(lambda e, t=t, buf=buf, nb=nb: e.tensor_scalar(
                xnb[nb][:], xs[buf][:], stats_sb[:, 2 * t:2 * t + 1],
                stats_sb[:, 2 * t + 1:2 * t + 2], OP.subtract, OP.mult),
                waits=[("act", ars.pop(t)), ("pe", pe_done)], inc="ve")
            scal_hist[buf] = ve_done
            pb = t % 2
            tp = None
            for c in range(NC):
                w2 = [("ve", ve_done), ("ve", evac_cnt[pb])] if c == 0 else ()
                tp = b.pe(lambda e, c=c, nb=nb, pb=pb: e.transpose(
                    ptr[pb][:, 128 * c:128 * c + 128],
                    xnb[nb][:, 128 * c:128 * c + 128], ident[:]),
                    waits=w2, inc="pe" if c == NC - 1 else None)
            if xnT_all is not None:
                ov = xnT_all.rearrange("p (c t) -> p c t", c=NC)[
                    :, :, off + 128 * t: off + 128 * t + 128]
                evac_cnt[pb] = b.ve(lambda e, ov=ov, pb=pb: e.tensor_copy(
                    ov, ptr[pb][:].rearrange("p (c t) -> p c t", c=NC)),
                    waits=[("pe", tp)])
            else:
                for c in range(NC):
                    evac_cnt[pb] = b.ve(lambda e, c=c, t=t, pb=pb: e.tensor_copy(
                        xnT[c][:, off + 128 * t: off + 128 * t + 128],
                        ptr[pb][:, 128 * c:128 * c + 128]),
                        waits=[("pe", tp)] if c == 0 else ())
            pe_done = tp
    if stats_dram is not None:
        b.st(lambda e: e.dma_start(out=stats_dram[:], in_=stats_sb[:]),
             waits=[("act", b.cnt["act"]), ("ve", b.cnt["ve"])], slot=3)
    es.close()



def _cast_load(b, es, name, dram_ap, rows, cols, stage, slot=6, scale=None,
               stage2=None, slot2=7, stages=None, slots=None):
    n = rows // 128
    out = b.sb(name + "_b", [128, n * cols], BF16, es)
    if stages is None:
        stages = [stage, stage2] if stage2 is not None else [stage]
        slots = [slot, slot2] if stage2 is not None else [slot]
    castc = [b.cnt["ve"]] * len(stages)
    for c in range(n):
        sb_ = c % len(stages)
        stg = stages[sb_]
        ld = b.ld(lambda e, c=c, stg=stg: e.dma_start(
            out=stg[:, 0:cols], in_=dram_ap[128 * c:128 * c + 128, :]),
            waits=[("ve", castc[sb_])], slot=slots[sb_])
        if scale is None:
            castc[sb_] = b.ve(lambda e, c=c, stg=stg: e.tensor_copy(
                out[:, cols * c:cols * c + cols], stg[:, 0:cols]),
                waits=[ld], inc="ve")
        else:
            castc[sb_] = b.ve(lambda e, c=c, stg=stg: e.tensor_scalar(
                out[:, cols * c:cols * c + cols], stg[:, 0:cols], scale, None,
                OP.mult),
                waits=[ld], inc="ve")
    return out


def _build_shift_bias(b, es, consts, Qm2, Km2, nrm_dram, cb, tag, extra_scale=1.0):
    nq = Qm2.shape[1]
    nk = Km2.shape[1]
    v = nrm_dram.rearrange("a (p c) -> (a p) c", p=128)
    st_q = b.st(lambda e: e.dma_start(out=v[:, 0:nq], in_=Qm2[:]),
                waits=[("ve", b.cnt["ve"])], slot=2)
    st_k = b.st(lambda e: e.dma_start(out=v[:, 16:16 + nk], in_=Km2[:]),
                waits=[("ve", b.cnt["ve"])], slot=1)
    flat = b.sb("nrmflat_" + tag, [1, 4096], F32, es)
    ld = b.ld(lambda e: e.dma_start(out=flat[:], in_=nrm_dram[:]),
              waits=[st_q, st_k], slot=6)
    mq = b.sb("nrm_mq_" + tag, [1, 4], F32, es)
    mrow = b.sb("nrm_mrow_" + tag, [1, 256], F32, es)
    fv = flat[:].rearrange("a (p c) -> a p c", c=32)
    b.ve(lambda e: e.reduce_max(mrow[:, 0:128], fv[:, :, 0:nq], axis=AX.X), waits=[ld])
    b.ve(lambda e: e.reduce_max(mq[:, 0:1], mrow[:, 0:128], axis=AX.X))
    b.ve(lambda e: e.reduce_max(mrow[:, 128:256], fv[:, :, 16:16 + nk], axis=AX.X))
    b.ve(lambda e: e.reduce_max(mq[:, 1:2], mrow[:, 128:256], axis=AX.X))
    ve0 = b.ve(lambda e: e.tensor_tensor(mq[:, 2:3], mq[:, 0:1], mq[:, 1:2], OP.mult),
               inc="ve")
    b.act(lambda e: e.activation(mq[:, 3:4], mq[:, 2:3], AF.Ln, bias=0.0, scale=1.0),
          waits=[("ve", ve0)])
    a1 = b.act(lambda e: e.activation(mq[:, 2:3], mq[:, 3:4], AF.Exp, bias=0.0, scale=0.5),
               inc="act")
    ve1 = b.ve(lambda e: e.tensor_scalar(mq[:, 3:4], mq[:, 2:3],
                                         -1.01 * extra_scale, None, OP.mult),
               waits=[("act", a1)], inc="ve")
    pbias = b.ps("pb_" + tag, [128, 8], F32, es)
    p0 = b.pe(lambda e: e.matmul(pbias[:, 0:1], consts["onesrow_f"][:], mq[:, 3:4],
                                 start=True, stop=True), waits=[("ve", ve1)], inc="pe")
    b.ve(lambda e: e.tensor_copy(cb[:], pbias[:, 0:1]), waits=[("pe", p0)], inc="ve")


def emit_l0a(b, consts, caps, xnT, es_xnT, a0_dram, nrm_dram, dbg=None):
    nc = b.nc
    ident = consts["ident"]
    H = 16

    es_main = b.scope()
    es_qkv = b.scope()
    esQ = b.scope()
    qTs_all = b.sb("l0a_qT_all", [128, 4 * T], BF16, es_qkv)
    kTs_all = b.sb("l0a_kT_all", [128, 4 * T], BF16, es_qkv)
    qTs = [qTs_all[:, j * T:(j + 1) * T] for j in range(4)]
    kTs = [kTs_all[:, j * T:(j + 1) * T] for j in range(4)]
    V = b.sb("l0a_V", [128, NT * 528], BF16, es_qkv)
    Qm2 = b.sb("l0a_qm2", [128, 16], F32, es_qkv)
    Km2 = b.sb("l0a_km2", [128, 16], F32, es_qkv)
    kvT = b.sb("l0a_kvT", [128, T], BF16, es_qkv)
    cb = b.sb("l0a_cb", [128, 1], F32, es_qkv)
    b.ve(lambda e: e.memset(Qm2[:], 0.0), inc="ve")
    b.ve(lambda e: e.memset(Km2[:], 0.0), inc="ve")
    b.ve(lambda e: e.memset(V[:], 1.0), inc="ve")

    stage = b.sb("l0a_stage", [128, 512], F32, esQ)
    stageB = b.sb("l0a_stageB", [128, 512], F32, esQ)
    PQ = b.ps("l0a_PQ", [128, 1024], F32, esQ)
    PTx = [b.ps(f"l0a_PTx{i}", [128, 256], BF16, esQ) for i in range(2)]
    qf32s = [b.sb(f"l0a_qf32{i}", [128, 512], F32, esQ) for i in range(2)]
    sqs = [b.sb(f"l0a_sq{i}", [128, 512], F32, esQ) for i in range(2)]
    mss = [b.sb(f"l0a_ms{i}", [128, 8], F32, esQ) for i in range(2)]
    lns = b.sb("l0a_lns", [128, 8], F32, esQ)
    rinvs = [b.sb(f"l0a_rinv{i}", [128, 8], F32, esQ) for i in range(2)]
    qtb = b.sb("l0a_qtb", [128, 256], BF16, esQ)
    eps6 = consts["eps6"]

    ptx_evac = [b.cnt["ve"], b.cnt["ve"]]
    for half in range(2):
        wq_b = _cast_load(b, esQ, f"l0a_wq{half}",
                          caps["wq"][:, 512 * half:512 * half + 512], 1024, 512, stage,
                          stage2=stageB)
        pq_free = [b.cnt["ve"], b.cnt["ve"]]
        ars = {}
        for tl in range(NT + 1):
            if tl < NT:
                t, bb, nb = tl, tl % 2, tl % 2
                pe_q = None
                for c in range(NC):
                    w_ = [("ve", pq_free[bb])] if c == 0 else ()
                    pe_q = b.pe(lambda e, t=t, c=c, bb=bb, wq_b=wq_b: e.matmul(
                        PQ[:, 512 * bb:512 * bb + 512],
                        xnT[c][:, 128 * t:128 * t + 128],
                        wq_b[:, 512 * c:512 * c + 512],
                        start=(c == 0), stop=(c == NC - 1)),
                        waits=w_, inc="pe" if c == NC - 1 else None)
                pq_free[bb] = b.ve(lambda e, bb=bb, nb=nb: e.tensor_copy(
                    qf32s[nb][:], PQ[:, 512 * bb:512 * bb + 512]),
                    waits=[("pe", pe_q)])
                b.ve(lambda e, nb=nb: e.tensor_tensor(sqs[nb][:], qf32s[nb][:],
                                                      qf32s[nb][:], OP.mult))
                ve_ms = b.ve(lambda e, nb=nb: e.tensor_reduce(
                    mss[nb][:], sqs[nb][:].rearrange("p (h d) -> p h d", d=64),
                    AX.X, OP.add))
                b.act(lambda e, nb=nb: e.activation(lns[:], mss[nb][:], AF.Ln,
                                                    bias=eps6[:], scale=1.0 / 64),
                      waits=[("ve", ve_ms)])
                ars[t] = b.act(lambda e, nb=nb: e.activation(
                    rinvs[nb][:], lns[:], AF.Exp, bias=0.0, scale=-0.5))
            if tl >= 1:
                t, nb = tl - 1, (tl - 1) % 2
                for hh in range(8):
                    b.ve(lambda e, hh=hh, nb=nb: e.tensor_scalar(
                        qtb[:, 32 * hh:32 * hh + 32],
                        qf32s[nb][:, 64 * hh:64 * hh + 32],
                        rinvs[nb][:, hh:hh + 1], None, OP.mult),
                        waits=[("act", ars.pop(t))] if hh == 0 else ())
                b.ve(lambda e, nb=nb: e.tensor_tensor(sqs[nb][:, 0:256], qtb[:],
                                                      qtb[:], OP.mult))
                b.ve(lambda e, nb=nb: e.tensor_reduce(
                    mss[nb][:], sqs[nb][:, 0:256].rearrange("p (h d) -> p h d", d=32),
                    AX.X, OP.add))
                ve_tail = b.ve(lambda e, half=half, nb=nb: e.tensor_tensor(
                    Qm2[:, 8 * half:8 * half + 8], Qm2[:, 8 * half:8 * half + 8],
                    mss[nb][:], OP.max))
                pb = t % 2
                tp = None
                for jj in range(2):
                    for a in range(4):
                        hh = 4 * jj + a
                        w_ = [("ve", ve_tail), ("ve", ptx_evac[pb])] \
                            if (jj == 0 and a == 0) else ()
                        tp = b.pe(lambda e, hh=hh, a=a, jj=jj, pb=pb: e.transpose(
                            PTx[pb][32 * a:32 * a + 32, 128 * jj:128 * jj + 128],
                            qtb[:, 32 * hh:32 * hh + 32],
                            ident[:], tile_position=(0, 32 * a)),
                            waits=w_, inc="pe" if (jj == 1 and a == 3) else None)
                ov = qTs_all.rearrange("p (j t) -> p j t", j=4)[
                    :, 2 * half:2 * half + 2, 128 * t:128 * t + 128]
                ptx_evac[pb] = b.ve(lambda e, ov=ov, pb=pb: e.tensor_copy(
                    ov, PTx[pb][:].rearrange("p (j t) -> p j t", j=2)),
                    waits=[("pe", tp)])

    wkv_b = _cast_load(b, esQ, "l0a_wkv", caps["wkv"], 1024, 128, stage,
                       stage2=stageB)
    ve_prev = [b.cnt["ve"], b.cnt["ve"]]
    for w in range(NW):
        bb = w % 2
        pe_kv = None
        for c in range(NC):
            pe_kv = b.pe(lambda e, w=w, c=c, bb=bb: e.matmul(
                PQ[:, 512 * bb:512 * bb + 512], wkv_b[:, 128 * c:128 * c + 128],
                xnT[c][:, 512 * w:512 * w + 512],
                start=(c == 0), stop=(c == NC - 1)),
                waits=[("ve", ve_prev[bb])] if c == 0 else (),
                inc="pe" if c == NC - 1 else None)
        ve_prev[bb] = b.ve(lambda e, w=w, bb=bb: e.tensor_copy(
            kvT[:, 512 * w:512 * w + 512], PQ[:, 512 * bb:512 * bb + 512]),
            waits=[("pe", pe_kv)], inc="ve")
    esQ.close()
    es_xnT.close()
    aoTs = [b.sb(f"l0a_aoT{s}", [128, T], BF16, es_main, side="right")
            for s in range(8)]
    for s in range(8):
        b.ve(lambda e, s=s: e.memset(aoTs[s][:], 0.0))

    esK = b.scope()
    stage2 = b.sb("l0a_stage2", [128, 512], F32, esK)
    wk_b = _cast_load(b, esK, "l0a_wk", caps["wk"], 128, 512, stage2)
    wv_b = _cast_load(b, esK, "l0a_wv", caps["wv"], 128, 512, stage2)
    qg = b.sb("l0a_qg", [1, 64], F32, esK)
    kg = b.sb("l0a_kg", [1, 32], F32, esK)
    g512 = b.sb("l0a_g512", [1, 512], F32, esK)
    gbc = b.sb("l0a_gbc", [128, 512], F32, esK)
    ldg = b.ld(lambda e: e.dma_start(out=qg[:], in_=caps["qgv"][:]), slot=5)
    ldk = b.ld(lambda e: e.dma_start(out=kg[:], in_=caps["kgv"][:]), slot=4)
    b.ve(lambda e: e.tensor_tensor(g512[:, 0:32], qg[:, 0:32], kg[:], OP.mult),
         waits=[ldg, ldk])
    b.ve(lambda e: e.tensor_scalar(g512[:, 0:32], g512[:, 0:32], 0.125, None, OP.mult))
    for h in range(1, H):
        b.ve(lambda e, h=h: e.tensor_copy(g512[:, 32 * h:32 * h + 32], g512[:, 0:32]))
    veg = b.ve(lambda e: e.nop(), inc="ve")
    PK = b.ps("l0a_PK", [128, 512], F32, esK)
    PVs = [b.ps(f"l0a_PV{i}", [128, 512], F32, esK) for i in range(2)]
    PTk = [b.ps(f"l0a_PTk{i}", [128, 512], BF16, esK) for i in range(2)]
    pes_v = {}
    pg = b.pe(lambda e: e.matmul(PK[:], consts["onesrow_f"][:], g512[:],
                                 start=True, stop=True), waits=[("ve", veg)], inc="pe")
    veg = b.ve(lambda e: e.tensor_copy(gbc[:], PK[:]), waits=[("pe", pg)], inc="ve")

    lnsk = b.sb("l0a_lnsk", [128, 16], F32, esK)
    kf32s = [b.sb(f"l0a_kf32{i}", [128, 512], F32, esK) for i in range(2)]
    ktmp = b.sb("l0a_ktmp", [128, 512], F32, esK)
    ktb = b.sb("l0a_ktb", [128, 512], BF16, esK)
    sqk = b.sb("l0a_sqk", [128, 512], F32, esK)
    msks = [b.sb(f"l0a_msk{i}", [128, 16], F32, esK) for i in range(2)]
    rinvks = [b.sb(f"l0a_rinvk{i}", [128, 16], F32, esK) for i in range(2)]

    ptx_evac = [b.cnt["ve"], b.cnt["ve"]]
    kcopy = [veg, veg]
    vevac = [veg, veg]
    ars = {}
    for tl in range(NT + 1):
        if tl < NT:
            t, nb = tl, tl % 2
            pe_k = b.pe(lambda e, t=t: e.matmul(PK[:], kvT[:, 128 * t:128 * t + 128],
                                                wk_b[:], start=True, stop=True),
                        waits=[("ve", kcopy[(t + 1) % 2])], inc="pe")
            pe_v = b.pe(lambda e, t=t, nb=nb: e.matmul(
                PVs[nb][:], kvT[:, 128 * t:128 * t + 128],
                wv_b[:], start=True, stop=True),
                waits=[("ve", vevac[nb])], inc="pe")
            pes_v[t] = pe_v
            kcopy[nb] = b.ve(lambda e, nb=nb: e.tensor_copy(kf32s[nb][:], PK[:]),
                             waits=[("pe", pe_k)])
            b.ve(lambda e, nb=nb: e.tensor_tensor(sqk[:], kf32s[nb][:], kf32s[nb][:],
                                                  OP.mult))
            ve_ms = b.ve(lambda e, nb=nb: e.tensor_reduce(
                msks[nb][:], sqk[:].rearrange("p (h d) -> p h d", d=32), AX.X, OP.add),
                inc="ve")
            b.act(lambda e, nb=nb: e.activation(lnsk[:], msks[nb][:], AF.Ln,
                                                bias=eps6[:], scale=1.0 / 32),
                  waits=[("ve", ve_ms)])
            ars[t] = b.act(lambda e, nb=nb: e.activation(
                rinvks[nb][:], lnsk[:], AF.Exp, bias=0.0, scale=-0.5), inc="act")
        if tl >= 1:
            t, nb = tl - 1, (tl - 1) % 2
            for h in range(H):
                b.ve(lambda e, h=h, nb=nb: e.tensor_scalar(
                    ktmp[:, 32 * h:32 * h + 32], kf32s[nb][:, 32 * h:32 * h + 32],
                    rinvks[nb][:, h:h + 1], None, OP.mult),
                    waits=[("act", ars.pop(t))] if h == 0 else ())
            b.ve(lambda e: e.tensor_tensor(ktb[:], ktmp[:], gbc[:], OP.mult))
            b.ve(lambda e: e.tensor_tensor(sqk[:], ktb[:], ktb[:], OP.mult))
            b.ve(lambda e, nb=nb: e.tensor_reduce(
                msks[nb][:], sqk[:].rearrange("p (h d) -> p h d", d=32), AX.X, OP.add))
            km_t = b.ve(lambda e, nb=nb: e.tensor_tensor(Km2[:], Km2[:], msks[nb][:],
                                                         OP.max), inc="ve")
            vevac[nb] = b.ve(lambda e, t=t, nb=nb: e.tensor_copy(
                V[:, 528 * t:528 * t + 528].rearrange("p (h c) -> p h c", c=33)[:, :, 0:32],
                PVs[nb][:].rearrange("p (h c) -> p h c", c=32)),
                waits=[("pe", pes_v.pop(t))], inc="ve")
            pb = t % 2
            tp = None
            for j in range(4):
                for a in range(4):
                    h = 4 * j + a
                    w = [("ve", km_t), ("ve", ptx_evac[pb])] if (j == 0 and a == 0) else ()
                    tp = b.pe(lambda e, h=h, a=a, j=j, pb=pb: e.transpose(
                        PTk[pb][32 * a:32 * a + 32, 128 * j:128 * j + 128],
                        ktb[:, 32 * h:32 * h + 32],
                        ident[:], tile_position=(0, 32 * a)),
                        waits=w, inc="pe" if (j == 3 and a == 3) else None)
            ov = kTs_all.rearrange("p (j t) -> p j t", j=4)[:, :, 128 * t:128 * t + 128]
            ptx_evac[pb] = b.ve(lambda e, ov=ov, pb=pb: e.tensor_copy(
                ov, PTk[pb][:].rearrange("p (j t) -> p j t", j=4)),
                waits=[("pe", tp)])
    _build_shift_bias(b, esK, consts, Qm2, Km2, nrm_dram, cb, "l0a")
    esK.close()
    if _DBG.get("upto") == "B1":
        es_main.close()
        return

    esA = b.scope()
    PS = b.ps("l0a_PS", [128, 2048], F32, esA)
    PAV = [b.ps(f"l0a_PAV{p}", [128, 512], F32, esA) for p in range(2)]
    PB2 = b.ps("l0a_PB2", [128, 512], F32, esA)
    PT = b.sb("l0a_PTexp", [128, 2048], BF16, esA)
    pavc = b.sb("l0a_pavc", [128, 512], F32, esA)
    bcast_sb = b.sb("l0a_bc", [128, 512], F32, esA)
    recipb = b.sb("l0a_recip", [128, 512], F32, esA)
    Epair = b.sb("cs2_Epair", [128, 128], F32, esA)
    lE = b.ld(lambda e: e.dma_start(out=Epair[:], in_=caps["Epair"][:]),
              waits=[("ve", b.cnt["ve"])], slot=7)
    masks2 = []
    for d in range(4):
        mt = b.sb(f"cs2_mask2_{d}", [128, 1024], BF16, esA)
        b.ld(lambda e, mt=mt, d=d: e.dma_start(out=mt[:], in_=caps[f"mask2_{d}"][:]),
             waits=[lE] if d == 0 else (), slot=7)
        masks2.append(mt)
    b.emit("ve", lambda e: e.nop(), waits=[("ld7", b.cnt["ld7"])])

    ps_free = [b.cnt["act"], b.cnt["act"]]
    pt_free = [b.cnt["pe"], b.cnt["pe"]]
    pav_free = [b.cnt["ve"], b.cnt["ve"]]
    pb2_free = b.cnt["ve"]
    norm_done = b.cnt["ve"]
    for j in range(4):
        for w in range(NW):
            for p in range(2):
                ktn = 4 * (w + 1)
                mz = b.ve(lambda e, p=p: e.memset(PAV[p][:], 0.0),
                          waits=[("ve", pav_free[p])])
                pending = None

                def mk_av(j, w, p, kt, bb, gate, ktn, mzw):
                    def emit_av():
                        for ai in range(2):
                            a = 2 * p + ai
                            h = 4 * j + a
                            wl = [gate] + mzw if ai == 0 else ()
                            r_ = b.pe(lambda e, kt=kt, h=h, ai=ai, bb=bb, p=p, ktn=ktn:
                                      e.matmul(
                                PAV[p][64 * ai:64 * ai + 33, :],
                                V[:, 528 * kt + 33 * h:528 * kt + 33 * h + 33],
                                PT[:, 1024 * bb + 512 * ai:1024 * bb + 512 * ai + 512],
                                start=(kt == 0 and ai == 0), stop=(kt == ktn - 1),
                                tile_position=(0, 64 * ai), skip_group_check=True),
                                waits=wl, inc="pe" if ai == 1 else None)
                            if ai == 1:
                                pt_free[bb] = r_
                    return emit_av

                for kt in range(ktn):
                    bb = kt % 2
                    diag = kt >= 4 * w
                    pe_s = None
                    for ai in range(2):
                        a = 2 * p + ai
                        wl = [("act", ps_free[bb])] if ai == 0 else ()
                        pe_s = b.pe(lambda e, j=j, w=w, kt=kt, a=a, ai=ai, bb=bb:
                                    e.matmul(
                            PS[:, 1024 * bb + 512 * ai:1024 * bb + 512 * ai + 512],
                            kTs[j][32 * a:32 * a + 32, 128 * kt:128 * kt + 128],
                            qTs[j][32 * a:32 * a + 32, 512 * w:512 * w + 512],
                            start=True, stop=True, tile_position=(32 * a, 0)),
                            waits=wl, inc="pe" if ai == 1 else None)
                    exp_t = b.act(lambda e, bb=bb: e.activation(
                        PT[:, 1024 * bb:1024 * bb + 1024],
                        PS[:, 1024 * bb:1024 * bb + 1024], AF.Exp, bias=cb[:],
                        scale=1.0),
                        waits=[("pe", pe_s), ("pe", pt_free[bb])])
                    ps_free[bb] = exp_t
                    gate = ("act", exp_t)
                    if diag:
                        d = kt - 4 * w
                        mk = b.ve(lambda e, d=d, bb=bb: e.tensor_tensor(
                            PT[:, 1024 * bb:1024 * bb + 1024],
                            PT[:, 1024 * bb:1024 * bb + 1024], masks2[d][:], OP.mult),
                            waits=[("act", exp_t)])
                        gate = ("ve", mk)
                    if pending is not None:
                        pending()
                    pending = mk_av(j, w, p, kt, bb, gate, ktn,
                                    [("ve", mz)] if kt == 0 else [])
                pending()
                pv = b.ve(lambda e, p=p: e.tensor_copy(pavc[:], PAV[p][:]),
                          waits=[("pe", pt_free[(ktn - 1) % 2])])
                pav_free[p] = pv
                pb_ = b.pe(lambda e: e.matmul(PB2[:], Epair[:], pavc[:],
                                              start=True, stop=True),
                           waits=[("ve", pv), ("ve", pb2_free)], inc="pe")
                b.ve(lambda e: e.tensor_copy(bcast_sb[:], PB2[:]), waits=[("pe", pb_)])
                pb2_free = b.ve(lambda e: e.reciprocal(recipb[:], bcast_sb[:]))
                b.ve(lambda e, j=j, w=w, p=p: e.tensor_tensor(
                    aoTs[2 * j + p][0:32, 512 * w:512 * w + 512], pavc[0:32, :],
                    recipb[0:32, :], OP.mult))
                b.ve(lambda e, j=j, w=w, p=p: e.tensor_tensor(
                    aoTs[2 * j + p][64:96, 512 * w:512 * w + 512], pavc[64:96, :],
                    recipb[64:96, :], OP.mult))
    esA.close()
    es_qkv.close()

    esO = b.scope()
    stg3 = [b.sb(f"l0a_stage3{i}", [128, 1024], F32, esO) for i in range(4)]
    wo_b = _cast_load(b, esO, "l0a_wo", caps["wosel"], 1024, 1024, None,
                      stages=stg3, slots=[4, 5, 6, 7])
    PO = b.ps("l0a_PO", [128, 1024], F32, esO)
    PTf = [b.ps(f"l0a_PTf{i}", [128, 128], F32, esO) for i in range(2)]
    of32 = [b.sb(f"l0a_of32_{i}", [128, 512], F32, esO) for i in range(2)]
    obuf = b.sb("l0a_obuf", [128, 4096], F32, esO)

    ptf_evac = [b.cnt["ve"], b.cnt["ve"]]
    po_free = [b.cnt["ve"], b.cnt["ve"]]
    ve_o = b.cnt["ve"]
    st_o = ("st0", b.cnt["st0"])
    for w in range(NW):
        for do in range(NC):
            bb = do % 2
            pe_o = None
            for s in range(8):
                pe_o = b.pe(lambda e, w=w, do=do, s=s, bb=bb: e.matmul(
                    PO[:, 512 * bb:512 * bb + 512],
                    wo_b[:, 1024 * s + 128 * do:1024 * s + 128 * do + 128],
                    aoTs[s][:, 512 * w:512 * w + 512],
                    start=(s == 0), stop=(s == 7)),
                    waits=[("ve", po_free[bb])] if s == 0 else (),
                    inc="pe" if s == 7 else None)
            po_free[bb] = b.ve(lambda e, bb=bb: e.tensor_copy(
                of32[bb][:], PO[:, 512 * bb:512 * bb + 512]), waits=[("pe", pe_o)])
            for tt in range(4):
                pb = tt % 2
                tp = b.pe(lambda e, tt=tt, pb=pb, bb=bb: e.transpose(
                    PTf[pb][:], of32[bb][:, 128 * tt:128 * tt + 128],
                    consts["identf"][:]),
                    waits=[("ve", po_free[bb]), ("ve", ptf_evac[pb])], inc="pe")
                ptf_evac[pb] = b.ve(lambda e, tt=tt, do=do, pb=pb: e.tensor_copy(
                    obuf[:, 1024 * tt + 128 * do:1024 * tt + 128 * do + 128], PTf[pb][:]),
                    waits=[("pe", tp)])
        for tt in range(4):
            st_o = b.st(lambda e, w=w, tt=tt: e.dma_start(
                out=a0_dram[512 * w + 128 * tt:512 * w + 128 * tt + 128, :],
                in_=obuf[:, 1024 * tt:1024 * tt + 1024]),
                waits=[("ve", max(ptf_evac))], slot=tt)
        if w < NW - 1:
            nv = b.emit("ve", lambda e: e.nop(),
                        waits=[(f"st{i}", b.cnt[f"st{i}"]) for i in range(4)])
            ptf_evac = [nv, nv]
    esO.close()
    es_main.close()



def emit_xn2T(b, consts, caps, x_dram, stats_sb, xn2T, xn2T_all):
    es = b.scope()
    xs = [b.sb(f"x2_{i}", [128, D], F32, es) for i in range(4)]
    xnfs = [b.sb(f"x2_xnf{i}", [128, D], F32, es) for i in range(2)]
    st6 = b.sb("x2_st6", [128, 12], F32, es)
    mvs = [b.sb(f"x2_mv{i}", [128, 2], F32, es) for i in range(2)]
    lnv = b.sb("x2_lnv", [128, 1], F32, es)
    rsts = [b.sb(f"x2_rst2{i}", [128, 1], F32, es) for i in range(2)]
    xnb = [b.sb(f"x2_xnb{i}", [128, D], BF16, es) for i in range(2)]
    ptr = [b.ps(f"x2_pt{i}", [128, 1024], BF16, es) for i in range(2)]
    wbc = b.sb("x2_wbc", [128, D], F32, es)
    bbc = b.sb("x2_bbc", [128, D], F32, es)
    nwv = b.sb("x2_nw", [1, D], F32, es)
    nbv = b.sb("x2_nb", [1, D], F32, es)
    PB = b.ps("x2_PB", [128, 1024], F32, es)
    ident = consts["ident"]
    eps5 = consts["eps5"]

    ln1 = b.ld(lambda e: e.dma_start(out=nwv[:], in_=caps["nw"][:]), slot=5)
    ln2 = b.ld(lambda e: e.dma_start(out=nbv[:], in_=caps["nb"][:]), slot=4)
    vv = b.ve(lambda e: e.nop(), waits=[ln1, ln2])
    pv = None
    for hh in range(2):
        pv = b.pe(lambda e, hh=hh: e.matmul(
            PB[:, 512 * hh:512 * hh + 512], consts["onesrow_f"][:],
            nwv[:, 512 * hh:512 * hh + 512], start=True, stop=True),
            waits=[("ve", vv)], inc="pe")
    vv = b.ve(lambda e: e.tensor_copy(wbc[:], PB[:]), waits=[("pe", pv)])
    for hh in range(2):
        pv = b.pe(lambda e, hh=hh: e.matmul(
            PB[:, 512 * hh:512 * hh + 512], consts["onesrow_f"][:],
            nbv[:, 512 * hh:512 * hh + 512], start=True, stop=True),
            waits=[("ve", vv)], inc="pe")
    vv = b.ve(lambda e: e.tensor_copy(bbc[:], PB[:]), waits=[("pe", pv)])

    pe_done = b.cnt["pe"]
    evac_cnt = [b.cnt["ve"], b.cnt["ve"]]
    scal_hist = [b.cnt["ve"]] * 4
    ars = {}
    for tl in range(NT + 1):
        if tl < NT:
            t, buf, nb2 = tl, tl % 4, tl % 2
            ld_t = b.ld(lambda e, t=t, buf=buf: e.dma_start(
                out=xs[buf][:], in_=x_dram[128 * t:128 * t + 128, :]),
                waits=[("ve", scal_hist[buf])] if t >= 4 else (), slot=buf)
            scal_hist[buf] = b.ve(lambda e, t=t, buf=buf, nb2=nb2: e.tensor_scalar(
                xnfs[nb2][:], xs[buf][:], stats_sb[:, 2 * t:2 * t + 1],
                stats_sb[:, 2 * t + 1:2 * t + 2], OP.subtract, OP.mult),
                waits=[ld_t])
            s1 = b.ve(lambda e, nb2=nb2: e.bn_stats(st6[:, 0:6], xnfs[nb2][:, 0:512]))
            s2 = b.ve(lambda e, nb2=nb2: e.bn_stats(st6[:, 6:12], xnfs[nb2][:, 512:1024]),
                      waits=[("ve", s1)])
            ve_mv = b.ve(lambda e, nb2=nb2: e.bn_aggr(mvs[nb2][:], st6[:]),
                         waits=[("ve", s2)])
            b.act(lambda e, nb2=nb2: e.activation(lnv[:], mvs[nb2][:, 1:2], AF.Ln,
                                                  bias=eps5[:], scale=1.0),
                  waits=[("ve", ve_mv)])
            ars[t] = b.act(lambda e, nb2=nb2: e.activation(
                rsts[nb2][:], lnv[:], AF.Exp, bias=0.0, scale=-0.5))
        if tl >= 1:
            t, nb2 = tl - 1, (tl - 1) % 2
            b.ve(lambda e, nb2=nb2: e.tensor_scalar(
                xnfs[nb2][:], xnfs[nb2][:], mvs[nb2][:, 0:1], rsts[nb2][:],
                OP.subtract, OP.mult),
                waits=[("act", ars.pop(t))])
            b.ve(lambda e, nb2=nb2: e.tensor_tensor(xnfs[nb2][:], xnfs[nb2][:],
                                                    wbc[:], OP.mult))
            ve_done = b.ve(lambda e, nb2=nb2: e.tensor_tensor(
                xnb[nb2][:], xnfs[nb2][:], bbc[:], OP.add),
                waits=[("pe", pe_done)])
            pb = t % 2
            tp = None
            for c in range(NC):
                w2 = [("ve", ve_done), ("ve", evac_cnt[pb])] if c == 0 else ()
                tp = b.pe(lambda e, c=c, nb2=nb2, pb=pb: e.transpose(
                    ptr[pb][:, 128 * c:128 * c + 128],
                    xnb[nb2][:, 128 * c:128 * c + 128], ident[:]),
                    waits=w2, inc="pe" if c == NC - 1 else None)
            ov = xn2T_all.rearrange("p (c t) -> p c t", c=NC)[:, :, 128 * t:128 * t + 128]
            evac_cnt[pb] = b.ve(lambda e, ov=ov, pb=pb: e.tensor_copy(
                ov, ptr[pb][:].rearrange("p (c t) -> p c t", c=NC)),
                waits=[("pe", tp)])
            pe_done = tp
    es.close()


def emit_l0ffn(b, consts, caps, x_dram, stats_sb, f0_dram, qT0f, kT0f, aoT0f, dbg=None):
    H = 8
    es_main = b.scope()
    xn2T_all = b.sb("xn2T_all", [128, NC * T], BF16, es_main, side="right")
    xn2T = [xn2T_all[:, c * T:(c + 1) * T] for c in range(NC)]
    emit_xn2T(b, consts, caps, x_dram, stats_sb, xn2T, xn2T_all)

    V0 = b.sb("l0f_V", [128, NT * 1024], BF16, es_main)
    cb = b.sb("l0f_cb", [128, 1], F32, es_main)
    qm = b.sb("l0f_qm", [1, 2], F32, es_main)
    b.ve(lambda e: e.memset(qm[:], 0.0))

    def qk_pass(which, wT_dram, bias_dram, out_dram, scale, qmcol):
        esP = b.scope()
        stage = b.sb(f"l0f_stage_{which}", [128, 1024], F32, esP)
        stageB = b.sb(f"l0f_stageB_{which}", [128, 1024], F32, esP)
        wT_b = _cast_load(b, esP, f"l0f_{which}w", wT_dram, 1024, 1024, stage,
                          stage2=stageB)
        bia = b.sb(f"l0f_{which}b", [128, 8], F32, esP)
        lb = b.ld(lambda e: e.dma_start(out=bia[:], in_=bias_dram[:]), slot=5)
        qsl = [b.sb(f"l0f_{which}sl{i}", [128, 512], BF16, esP) for i in range(4)]
        sqs = b.sb(f"l0f_{which}sq", [128, 512], F32, esP)
        nrm1 = b.sb(f"l0f_{which}n1", [1, 512], F32, esP)
        PP = b.ps(f"l0f_PP_{which}", [128, 1024], F32, esP)
        PN = b.ps(f"l0f_PN_{which}", [128, 512], F32, esP)
        pp_free = [b.cnt["ve"], b.cnt["ve"]]
        sl_st = [(f"st{i}", 0) for i in range(4)]
        sidx = 0
        for e_ in range(4):
            mch = NC >> e_
            for do in range(NC):
                for ws in range(2):
                    w = 2 * e_ + ws
                    bb = sidx % 2
                    qb = sidx % 4
                    pe_q = None
                    for ci in range(mch):
                        pe_q = b.pe(lambda e, ci=ci, do=do, w=w, mch=mch, bb=bb:
                                    e.matmul(
                            PP[:, 512 * bb:512 * bb + 512],
                            wT_b[:, 1024 * ci + 128 * do:1024 * ci + 128 * do + 128],
                            xn2T[ci][:, 512 * w:512 * w + 512],
                            start=(ci == 0), stop=(ci == mch - 1)),
                            waits=[("ve", pp_free[bb]), lb] if ci == 0 else (),
                            inc="pe" if ci == mch - 1 else None)
                    if scale is not None:
                        vq = b.ve(lambda e, do=do, bb=bb, qb=qb: e.tensor_scalar(
                            qsl[qb][:], PP[:, 512 * bb:512 * bb + 512],
                            bia[:, do:do + 1], scale, OP.add, OP.mult),
                            waits=[("pe", pe_q), sl_st[qb]])
                    else:
                        vq = b.ve(lambda e, do=do, bb=bb, qb=qb: e.tensor_scalar(
                            qsl[qb][:], PP[:, 512 * bb:512 * bb + 512],
                            bia[:, do:do + 1], None, OP.add),
                            waits=[("pe", pe_q), sl_st[qb]])
                    pp_free[bb] = b.ve(lambda e, qb=qb: e.tensor_tensor(
                        sqs[:], qsl[qb][:], qsl[qb][:], OP.mult))
                    sl_st[qb] = b.st(lambda e, do=do, w=w, qb=qb: e.dma_start(
                        out=out_dram[128 * do:128 * do + 128, 512 * w:512 * w + 512],
                        in_=qsl[qb][:]), waits=[("ve", pp_free[bb])], slot=qb)
                    pn = b.pe(lambda e: e.matmul(PN[0:1, :], consts["onescol_f"][:],
                                                 sqs[:], start=True, stop=True),
                              waits=[("ve", pp_free[bb])], inc="pe")
                    b.ve(lambda e: e.reduce_max(nrm1[:, 0:1], PN[0:1, :], axis=AX.X),
                         waits=[("pe", pn)])
                    b.ve(lambda e, qmcol=qmcol: e.tensor_tensor(
                        qm[:, qmcol:qmcol + 1], qm[:, qmcol:qmcol + 1],
                        nrm1[:, 0:1], OP.max))
                    sidx += 1
        b.emit("ve", lambda e: e.nop(),
               waits=[(f"st{i}", b.cnt[f"st{i}"]) for i in range(4)])
        esP.close()

    qk_pass("q", caps["qwT"], caps["qb"], qT0f, 1.0 / 32, 0)
    qk_pass("k", caps["kwT"], caps["kb"], kT0f, None, 1)

    esV = b.scope()
    stageV = b.sb("l0f_stageV", [128, 1024], F32, esV)
    stageVB = b.sb("l0f_stageVB", [128, 1024], F32, esV)
    vwT_b = _cast_load(b, esV, "l0f_vw", caps["vwT"], 1024, 1024, stageV,
                       stage2=stageVB)
    vb1024 = b.sb("l0f_vb1024", [1, 1024], F32, esV)
    vbc = b.sb("l0f_vbc", [128, 1024], F32, esV)
    lb = b.ld(lambda e: e.dma_start(out=vb1024[:], in_=caps["vb"][:]), slot=5)
    vv = b.ve(lambda e: e.nop(), waits=[lb])
    PP = b.ps("l0f_PPv", [128, 1024], F32, esV)
    for hh in range(2):
        pv = b.pe(lambda e, hh=hh: e.matmul(
            PP[:, 0:512], consts["onesrow_f"][:], vb1024[:, 512 * hh:512 * hh + 512],
            start=True, stop=True), waits=[("ve", vv)], inc="pe")
        vv = b.ve(lambda e, hh=hh: e.tensor_copy(vbc[:, 512 * hh:512 * hh + 512],
                                                 PP[:, 0:512]),
                  waits=[("pe", pv)])
    pp_free = [vv, vv]
    sidx = 0
    for e_ in range(4):
        mch = NC >> e_
        for ts_ in range(8):
            t = 8 * e_ + ts_
            for hh in range(2):
                bb = sidx % 2
                pe_v = None
                for ci in range(mch):
                    pe_v = b.pe(lambda e, ci=ci, t=t, hh=hh, mch=mch, bb=bb: e.matmul(
                        PP[:, 512 * bb:512 * bb + 512],
                        xn2T[ci][:, 128 * t:128 * t + 128],
                        vwT_b[:, 1024 * ci + 512 * hh:1024 * ci + 512 * hh + 512],
                        start=(ci == 0), stop=(ci == mch - 1)),
                        waits=[("ve", pp_free[bb])] if ci == 0 else (),
                        inc="pe" if ci == mch - 1 else None)
                pp_free[bb] = b.ve(lambda e, t=t, hh=hh, bb=bb: e.tensor_tensor(
                    V0[:, 1024 * t + 512 * hh:1024 * t + 512 * hh + 512],
                    PP[:, 512 * bb:512 * bb + 512],
                    vbc[:, 512 * hh:512 * hh + 512], OP.add),
                    waits=[("pe", pe_v)])
                sidx += 1
    esV.close()

    esB = b.scope()
    PN2 = b.ps("l0f_PN2", [128, 8], F32, esB)
    ve0 = b.ve(lambda e: e.tensor_tensor(qm[:, 0:1], qm[:, 0:1], qm[:, 1:2], OP.mult))
    b.act(lambda e: e.activation(qm[:, 1:2], qm[:, 0:1], AF.Ln, bias=0.0, scale=1.0),
          waits=[("ve", ve0)])
    a1 = b.act(lambda e: e.activation(qm[:, 0:1], qm[:, 1:2], AF.Exp, bias=0.0, scale=0.5))
    ve1 = b.ve(lambda e: e.tensor_scalar(qm[:, 1:2], qm[:, 0:1], -1.01, None, OP.mult),
               waits=[("act", a1)])
    p0 = b.pe(lambda e: e.matmul(PN2[:, 0:1], consts["onesrow_f"][:], qm[:, 1:2],
                                 start=True, stop=True), waits=[("ve", ve1)], inc="pe")
    b.ve(lambda e: e.tensor_copy(cb[:], PN2[:, 0:1]), waits=[("pe", p0)])
    esB.close()
    es_xn2T_close = None

    esA = b.scope()
    PS = b.ps("l0f_PS", [128, 3072], F32, esA)
    PAV = b.ps("l0f_PAV", [128, 512], F32, esA)
    PSM = b.ps("l0f_PSM", [128, 512], F32, esA)
    PT = b.sb("l0f_PT", [128, 3072], BF16, esA)
    qwin = [b.sb(f"l0f_qw{i}", [128, 512], BF16, esA) for i in range(2)]
    kTh = [b.sb(f"l0f_kTh{i}", [128, T], BF16, esA) for i in range(2)]
    srow = b.sb("l0f_srow", [1, 512], F32, esA)
    rrow = b.sb("l0f_rrow", [1, 512], F32, esA)
    bcast = b.sb("l0f_bcast", [128, 512], F32, esA)
    pavc = b.sb("l0f_pavc", [128, 512], F32, esA)
    aosl = [b.sb(f"l0f_aosl{i}", [128, 512], BF16, esA) for i in range(2)]

    GROUPS = [(0, 3), (3, 3), (6, 3), (9, 3), (12, 3), (15, 3), (18, 3), (21, 3),
              (24, 3), (27, 3), (30, 2)]
    wins = [(h, w) for h in range(H) for w in range(NW)]
    ps_free = [b.cnt["act"], b.cnt["act"]]
    pt_free = [b.cnt["pe"], b.cnt["pe"]]
    qw_free = [b.cnt["pe"], b.cnt["pe"]]
    kh_free = [b.cnt["pe"], b.cnt["pe"]]
    pav_ready = b.cnt["ve"]
    psm_ready = b.cnt["ve"]
    ao_st = [("st2", 0), ("st3", 0)]
    lq = {}
    kld = {}

    def issue_lq(idx):
        if idx >= len(wins):
            return
        h, w = wins[idx]
        qb_ = idx % 2
        lq[idx] = b.ld(lambda e, h=h, w=w, qb_=qb_: e.dma_start(
            out=qwin[qb_][:], in_=qT0f[128 * h:128 * h + 128, 512 * w:512 * w + 512]),
            waits=[("pe", qw_free[qb_])], slot=qb_)

    def issue_kld(h):
        if h >= H:
            return
        hb = h % 2
        kld[h] = b.ld(lambda e, h=h, hb=hb: e.dma_start(
            out=kTh[hb][:], in_=kT0f[128 * h:128 * h + 128, :]),
            waits=[("pe", kh_free[hb])], slot=4 + hb)

    issue_lq(0)
    issue_kld(0)
    issue_kld(1)
    for idx, (h, w) in enumerate(wins):
        hb = h % 2
        qb_ = idx % 2
        pending = None

        def mk_av(h, kt0, gn, bb, exp_t, lastg, first):
            def emit_av():
                for i in range(gn):
                    kt = kt0 + i
                    wl = [exp_t] if i == 0 else ()
                    if i == 0 and first:
                        wl = [exp_t, ("ve", pav_ready)]
                    b.pe(lambda e, h=h, kt=kt, i=i, bb=bb, lastg=lastg, gn=gn:
                         e.matmul(
                        PAV[:], V0[:, 1024 * kt + 128 * h:1024 * kt + 128 * h + 128],
                        PT[:, 1536 * bb + 512 * i:1536 * bb + 512 * i + 512],
                        start=(kt == 0), stop=(lastg and i == gn - 1),
                        skip_group_check=True),
                        waits=wl)
                r_ = None
                for i in range(gn):
                    kt = kt0 + i
                    wl = [("ve", psm_ready)] if (i == 0 and first) else ()
                    r_ = b.pe(lambda e, kt=kt, i=i, bb=bb, lastg=lastg, gn=gn:
                              e.matmul(
                        PSM[0:1, :], consts["onescol_bf"][:],
                        PT[:, 1536 * bb + 512 * i:1536 * bb + 512 * i + 512],
                        start=(kt == 0), stop=(lastg and i == gn - 1),
                        skip_group_check=True),
                        waits=wl, inc="pe" if i == gn - 1 else None)
                pt_free[bb] = r_
            return emit_av

        for gi, (kt0, gn) in enumerate(GROUPS):
            bb = gi % 2
            pe_s = None
            for i in range(gn):
                kt = kt0 + i
                wl = []
                if i == 0:
                    wl = [("act", ps_free[bb]), lq[idx], kld[h]]
                pe_s = b.pe(lambda e, hb=hb, kt=kt, i=i, bb=bb, qb_=qb_, gn=gn:
                            e.matmul(
                    PS[:, 1536 * bb + 512 * i:1536 * bb + 512 * i + 512],
                    kTh[hb][:, 128 * kt:128 * kt + 128], qwin[qb_][:],
                    start=True, stop=True),
                    waits=wl, inc="pe" if i == gn - 1 else None)
            qw_free[qb_] = pe_s
            kh_free[hb] = pe_s
            if gi == 2:
                issue_lq(idx + 1)
            exp_t = b.act(lambda e, bb=bb, gn=gn: e.activation(
                PT[:, 1536 * bb:1536 * bb + 512 * gn],
                PS[:, 1536 * bb:1536 * bb + 512 * gn], AF.Exp, bias=cb[:],
                scale=1.0),
                waits=[("pe", pe_s), ("pe", pt_free[bb])])
            ps_free[bb] = exp_t
            lastg = (gi == len(GROUPS) - 1)
            if pending is not None:
                pending()
            pending = mk_av(h, kt0, gn, bb, ("act", exp_t), lastg, gi == 0)
        pending()
        if w == NW - 1:
            issue_kld(h + 2)
        pv = b.ve(lambda e: e.tensor_copy(pavc[:], PAV[:]),
                  waits=[("pe", pt_free[(len(GROUPS) - 1) % 2]), ao_st[idx % 2]])
        sv = b.ve(lambda e: e.tensor_copy(srow[:], PSM[0:1, :]))
        psm_ready = sv
        rv = b.ve(lambda e: e.reciprocal(rrow[:], srow[:]))
        pb_ = b.pe(lambda e: e.matmul(PAV[:], consts["onesrow_f"][:], rrow[:],
                                      start=True, stop=True),
                   waits=[("ve", rv)], inc="pe")
        pav_ready = b.ve(lambda e: e.tensor_copy(bcast[:], PAV[:]), waits=[("pe", pb_)])
        ve_f = b.ve(lambda e, qb_=qb_: e.tensor_tensor(aosl[qb_][:], pavc[:], bcast[:],
                                                       OP.mult))
        s_, v_ = b.st(lambda e, h=h, w=w, qb_=qb_: e.dma_start(
            out=aoT0f[128 * h:128 * h + 128, 512 * w:512 * w + 512], in_=aosl[qb_][:]),
            waits=[("ve", ve_f)], slot=2 + idx % 2)
        ao_st[idx % 2] = (s_, v_)
    esA.close()

    esO = b.scope()
    stage4 = b.sb("l0f_stage4", [128, 1024], F32, esO)
    stage4B = b.sb("l0f_stage4B", [128, 1024], F32, esO)
    owT_b = _cast_load(b, esO, "l0f_ow", caps["owT"], 1024, 1024, stage4,
                       stage2=stage4B)
    ob = b.sb("l0f_ob", [128, 8], F32, esO)
    b.ld(lambda e: e.dma_start(out=ob[:], in_=caps["ob"][:]), slot=5)
    aoseg = b.sb("l0f_aoseg", [128, 8 * 1024], BF16, esO)
    PO = b.ps("l0f_PO", [128, 1024], F32, esO)
    PTf = [b.ps(f"l0f_PTf{i}", [128, 128], F32, esO) for i in range(2)]
    of32 = [b.sb(f"l0f_of32_{i}", [128, 512], F32, esO) for i in range(2)]
    obuf = b.sb("l0f_obuf", [128, 4096], F32, esO)
    zbuf = b.sb("l0f_zbuf", [128, 1024], F32, esO)
    vz = b.ve(lambda e: e.memset(zbuf[:], 0.0))
    for t in range(NT):
        b.st(lambda e, t=t: e.dma_start(out=f0_dram[128 * t:128 * t + 128, :],
                                        in_=zbuf[:]), waits=[("ve", vz)], slot=t % 4)
    zdone = [(f"st{i}", b.cnt[f"st{i}"]) for i in range(4)]

    ptf_evac = [b.cnt["ve"], b.cnt["ve"]]
    for e_ in range(4):
        mch = NC >> e_
        ldd = {}
        for ci in range(mch):
            s_, v_ = b.ld(lambda e, e_=e_, ci=ci: e.dma_start(
                out=aoseg[:, 1024 * ci:1024 * ci + 1024],
                in_=aoT0f[128 * ci:128 * ci + 128, 1024 * e_:1024 * e_ + 1024]),
                waits=[("ve", b.cnt["ve"])], slot=ci % 4)
            ldd[s_] = v_
        lds = list(ldd.items())
        ve_o = b.cnt["ve"]
        st_w = []
        po_free = [b.cnt["ve"], b.cnt["ve"]]
        sidx = 0
        for ws in range(2):
            for do in range(mch):
                bb = sidx % 2
                pe_o = None
                for ci in range(mch):
                    pe_o = b.pe(lambda e, ci=ci, do=do, ws=ws, mch=mch, bb=bb: e.matmul(
                        PO[:, 512 * bb:512 * bb + 512],
                        owT_b[:, 1024 * ci + 128 * do:1024 * ci + 128 * do + 128],
                        aoseg[:, 1024 * ci + 512 * ws:1024 * ci + 512 * ws + 512],
                        start=(ci == 0), stop=(ci == mch - 1)),
                        waits=[("ve", po_free[bb])] + (lds if (ws == 0 and do == 0)
                                                      else [])
                        if ci == 0 else (),
                        inc="pe" if ci == mch - 1 else None)
                po_free[bb] = b.ve(lambda e, do=do, bb=bb: e.tensor_scalar(
                    of32[bb][:], PO[:, 512 * bb:512 * bb + 512],
                    ob[:, do:do + 1], None, OP.add),
                    waits=[("pe", pe_o), ("ld5", b.cnt["ld5"])])
                for tt in range(4):
                    pb = tt % 2
                    tp = b.pe(lambda e, tt=tt, pb=pb, bb=bb: e.transpose(
                        PTf[pb][:], of32[bb][:, 128 * tt:128 * tt + 128],
                        consts["identf"][:]),
                        waits=[("ve", po_free[bb]), ("ve", ptf_evac[pb])], inc="pe")
                    ptf_evac[pb] = b.ve(lambda e, tt=tt, do=do, pb=pb: e.tensor_copy(
                        obuf[:, 1024 * tt + 128 * do:1024 * tt + 128 * do + 128],
                        PTf[pb][:]), waits=[("pe", tp)])
                sidx += 1
            for tt in range(4):
                t_global = 8 * e_ + 4 * ws + tt
                st_w = b.st(lambda e, t_global=t_global, tt=tt, mch=mch: e.dma_start(
                    out=f0_dram[128 * t_global:128 * t_global + 128, 0:128 * mch],
                    in_=obuf[:, 1024 * tt:1024 * tt + 128 * mch]),
                    waits=[("ve", max(ptf_evac))] + (zdone if e_ == 0 and ws == 0 else []),
                    slot=tt)
            ve_o = b.emit("ve", lambda e: e.nop(),
                          waits=[(f"st{i}", b.cnt[f"st{i}"]) for i in range(4)])
    esO.close()
    es_main.close()



def emit_assembly_ln(b, consts, x_dram, a_dram, f_dram, stats_sb, add_xn,
                     x1_dram, stats1_sb, xnT1, xnT1_all):
    es = b.scope()
    xs = [b.sb(f"as_x{i}", [128, D], F32, es) for i in range(2)]
    as_ = [b.sb(f"as_a{i}", [128, D], F32, es) for i in range(2)]
    fs = [b.sb(f"as_f{i}", [128, D], F32, es) for i in range(2)]
    x1fs = [b.sb(f"as_x1f{i}", [128, D], F32, es) for i in range(2)]
    tmp = b.sb("as_tmp", [128, D], F32, es)
    st6 = b.sb("as_st6", [128, 12], F32, es)
    mvs = [b.sb(f"as_mv{i}", [128, 2], F32, es) for i in range(2)]
    lnv = b.sb("as_lnv", [128, 1], F32, es)
    xnb = [b.sb(f"as_xnb{i}", [128, D], BF16, es) for i in range(2)]
    ptr = [b.ps(f"as_pt{i}", [128, 1024], BF16, es) for i in range(2)]
    ident = consts["ident"]
    eps5 = consts["eps5"]

    for c in range(NC):
        b.ve(lambda e, c=c: e.memset(xnT1[c][:, 0:1], 0.0))

    pe_done = b.cnt["pe"]
    evac_cnt = [b.cnt["ve"], b.cnt["ve"]]
    scal_hist = [b.cnt["ve"], b.cnt["ve"]]
    ars = {}
    for tl in range(NT + 1):
        if tl < NT:
            t, buf, nb = tl, tl % 2, tl % 2
            w0 = [("ve", scal_hist[buf])] if t >= 2 else ()
            l1 = b.ld(lambda e, t=t, buf=buf: e.dma_start(
                out=xs[buf][:], in_=x_dram[128 * t:128 * t + 128, :]), waits=w0,
                slot=buf)
            l2 = b.ld(lambda e, t=t, buf=buf: e.dma_start(
                out=as_[buf][:], in_=a_dram[128 * t:128 * t + 128, :]), waits=w0,
                slot=2 + buf)
            l3 = b.ld(lambda e, t=t, buf=buf: e.dma_start(
                out=fs[buf][:], in_=f_dram[128 * t:128 * t + 128, :]), waits=w0,
                slot=4 + buf)
            x1f = x1fs[nb]
            b.ve(lambda e, buf=buf, x1f=x1f: e.tensor_tensor(
                x1f[:], as_[buf][:], fs[buf][:], OP.add),
                waits=[l2, l3, (f"st{t % 2}", b.cnt[f"st{t % 2}"])])
            scal_hist[buf] = b.ve(lambda e, buf=buf, x1f=x1f: e.tensor_tensor(
                x1f[:], x1f[:], xs[buf][:], OP.add), waits=[l1])
            if add_xn:
                scal_hist[buf] = b.ve(lambda e, t=t, buf=buf: e.tensor_scalar(
                    tmp[:], xs[buf][:], stats_sb[:, 2 * t:2 * t + 1],
                    stats_sb[:, 2 * t + 1:2 * t + 2], OP.subtract, OP.mult))
                b.ve(lambda e, x1f=x1f: e.tensor_tensor(x1f[:], x1f[:], tmp[:],
                                                        OP.add))
            vx = b.ve(lambda e: e.nop())
            b.st(lambda e, t=t, x1f=x1f: e.dma_start(
                out=x1_dram[128 * t:128 * t + 128, :], in_=x1f[:]),
                waits=[("ve", vx)], slot=t % 2)
            s1 = b.ve(lambda e, x1f=x1f: e.bn_stats(st6[:, 0:6], x1f[:, 0:512]))
            s2 = b.ve(lambda e, x1f=x1f: e.bn_stats(st6[:, 6:12], x1f[:, 512:1024]),
                      waits=[("ve", s1)])
            ve_mv = b.ve(lambda e, nb=nb: e.bn_aggr(mvs[nb][:], st6[:]),
                         waits=[("ve", s2)])
            b.ve(lambda e, t=t, nb=nb: e.tensor_copy(stats1_sb[:, 2 * t:2 * t + 1],
                                                     mvs[nb][:, 0:1]),
                 waits=[("ve", ve_mv)])
            b.act(lambda e, nb=nb: e.activation(lnv[:], mvs[nb][:, 1:2], AF.Ln,
                                                bias=eps5[:], scale=1.0),
                  waits=[("ve", ve_mv)])
            ars[t] = b.act(lambda e, t=t: e.activation(
                stats1_sb[:, 2 * t + 1:2 * t + 2], lnv[:], AF.Exp, bias=0.0,
                scale=-0.5))
        if tl >= 1:
            t, nb = tl - 1, (tl - 1) % 2
            x1f = x1fs[nb]
            ve_done = b.ve(lambda e, t=t, nb=nb, x1f=x1f: e.tensor_scalar(
                xnb[nb][:], x1f[:], stats1_sb[:, 2 * t:2 * t + 1],
                stats1_sb[:, 2 * t + 1:2 * t + 2], OP.subtract, OP.mult),
                waits=[("act", ars.pop(t)), ("pe", pe_done)])
            pb = t % 2
            tp = None
            for c in range(NC):
                w2 = [("ve", ve_done), ("ve", evac_cnt[pb])] if c == 0 else ()
                tp = b.pe(lambda e, c=c, nb=nb, pb=pb: e.transpose(
                    ptr[pb][:, 128 * c:128 * c + 128],
                    xnb[nb][:, 128 * c:128 * c + 128], ident[:]),
                    waits=w2, inc="pe" if c == NC - 1 else None)
            ov = xnT1_all.rearrange("p (c t) -> p c t", c=NC)[
                :, :, 1 + 128 * t:1 + 128 * t + 128]
            evac_cnt[pb] = b.ve(lambda e, ov=ov, pb=pb: e.tensor_copy(
                ov, ptr[pb][:].rearrange("p (c t) -> p c t", c=NC)),
                waits=[("pe", tp)])
            pe_done = tp
    es.close()


def emit_l1a(b, consts, caps, xnT1, qT1f, a1_dram, nrm_dram):
    es_main = b.scope()
    es_kv = b.scope()
    kT64 = [b.sb(f"l1_kT64_{v}", [128, T], BF16, es_kv) for v in range(2)]
    V1 = b.sb("l1_V1", [128, NT * 256], BF16, es_kv)
    cb = b.sb("l1_cb", [128, 1], F32, es_kv)
    Qn = b.sb("l1_Qn", [128, 16], F32, es_kv)
    Kn = b.sb("l1_Kn", [128, 16], F32, es_kv)
    b.ve(lambda e: e.memset(Qn[:], 0.0))
    b.ve(lambda e: e.memset(Kn[:], 0.0))

    esP = b.scope()
    stg1 = [b.sb(f"l1_stage{i}", [128, 1024], F32, esP) for i in range(4)]
    cos4 = b.sb("l1_cos4", [128, T], F32, esP)
    sin4 = b.sb("l1_sin4", [128, T], F32, esP)
    lc = b.ld(lambda e: e.dma_start(out=cos4[:], in_=caps["cos4"][:]), slot=0)
    ls = b.ld(lambda e: e.dma_start(out=sin4[:], in_=caps["sin4"][:]), slot=1)
    wq1_b = _cast_load(b, esP, "l1_wq", caps["wq1"], 1024, 1024, None,
                       stages=stg1, slots=[4, 5, 6, 7])
    wk1_b = _cast_load(b, esP, "l1_wk", caps["wk1"], 1024, 256, None,
                       stages=stg1, slots=[4, 5, 6, 7])
    wv1_b = _cast_load(b, esP, "l1_wv", caps["wv1"], 1024, 256, None,
                       stages=stg1, slots=[4, 5, 6, 7])
    P1m = []
    P2m = []
    for v in range(2):
        t1_ = b.sb(f"cs2_P1_{v}", [128, 128], BF16, esP)
        t2_ = b.sb(f"cs2_P2_{v}", [128, 128], BF16, esP)
        b.ld(lambda e, t1_=t1_, v=v: e.dma_start(out=t1_[:], in_=caps[f"P1_{v}"][:]),
             waits=[("ve", b.cnt["ve"])], slot=7)
        b.ld(lambda e, t2_=t2_, v=v: e.dma_start(out=t2_[:], in_=caps[f"P2_{v}"][:]),
             waits=[("ve", b.cnt["ve"])], slot=7)
        P1m.append(t1_)
        P2m.append(t2_)
    ldP = ("ld7", b.cnt["ld7"])
    PL = b.ps("l1_PL", [128, 512], F32, esP)
    PH_ = b.ps("l1_PH", [128, 512], F32, esP)
    PX = b.ps("l1_PX", [128, 1024], F32, esP)
    PNq = b.ps("l1_PNq", [128, 512], F32, esP)
    t1 = b.sb("l1_t1", [128, 512], F32, esP)
    t2 = b.sb("l1_t2", [128, 512], F32, esP)
    rlo = b.sb("l1_rlo", [128, 512], BF16, esP)
    rhi = b.sb("l1_rhi", [128, 512], BF16, esP)
    q64 = b.sb("l1_q64", [128, 512], BF16, esP)
    sqr = b.sb("l1_sqr", [128, 512], F32, esP)
    nr1 = b.sb("l1_nr1", [128, 1], F32, esP)

    def rope_perm_block(jp, w, lo_cols, hi_cols, wmat, qdst, Nacc, ncol, wstride):
        pe_l = pe_h = None
        for ci in range(NC):
            pe_l = b.pe(lambda e, ci=ci, w=w, lo_cols=lo_cols, ws_=wstride,
                        wmat=wmat: e.matmul(
                PL[:], wmat[:, ws_ * ci + lo_cols:ws_ * ci + lo_cols + 128],
                xnT1[ci][:, 1 + 512 * w:1 + 512 * w + 512],
                start=(ci == 0), stop=(ci == NC - 1)),
                waits=[("ve", b.cnt["ve"])] if ci == 0 else (),
                inc="pe" if ci == NC - 1 else None)
        for ci in range(NC):
            pe_h = b.pe(lambda e, ci=ci, w=w, hi_cols=hi_cols, ws_=wstride,
                        wmat=wmat: e.matmul(
                PH_[:], wmat[:, ws_ * ci + hi_cols:ws_ * ci + hi_cols + 128],
                xnT1[ci][:, 1 + 512 * w:1 + 512 * w + 512],
                start=(ci == 0), stop=(ci == NC - 1)),
                inc="pe" if ci == NC - 1 else None)
        cw = [lc, ls, ldP] if (jp == 0 and w == 0) else []
        b.ve(lambda e, w=w: e.tensor_tensor(t1[:], PL[:], cos4[:, 512 * w:512 * w + 512],
                                            OP.mult), waits=[("pe", pe_l)] + cw)
        b.ve(lambda e, w=w: e.tensor_tensor(t2[:], PH_[:], sin4[:, 512 * w:512 * w + 512],
                                            OP.mult), waits=[("pe", pe_h)])
        b.ve(lambda e: e.tensor_tensor(rlo[:], t1[:], t2[:], OP.subtract))
        b.ve(lambda e, w=w: e.tensor_tensor(t1[:], PL[:], sin4[:, 512 * w:512 * w + 512],
                                            OP.mult))
        b.ve(lambda e, w=w: e.tensor_tensor(t2[:], PH_[:], cos4[:, 512 * w:512 * w + 512],
                                            OP.mult))
        vr = b.ve(lambda e: e.tensor_tensor(rhi[:], t1[:], t2[:], OP.add))
        mzn = b.ve(lambda e: e.memset(PNq[:], 0.0), waits=[("pe", b.cnt["pe"])])
        sq_l = b.ve(lambda e: e.tensor_tensor(sqr[:], rlo[:], rlo[:], OP.mult))
        pn = None
        for i in range(4):
            pn = b.pe(lambda e, i=i: e.matmul(
                PNq[32 * i:32 * i + 1, :], consts["onescol_f"][32 * i:32 * i + 32, :],
                sqr[32 * i:32 * i + 32, :], start=(i == 0), stop=False,
                tile_position=(32 * i, 32 * i), skip_group_check=True),
                waits=[("ve", sq_l), ("ve", mzn)] if i == 0 else (),
                inc="pe" if i == 3 else None)
        sq_h = b.ve(lambda e: e.tensor_tensor(sqr[:], rhi[:], rhi[:], OP.mult),
                    waits=[("pe", pn)])
        for i in range(4):
            pn = b.pe(lambda e, i=i: e.matmul(
                PNq[32 * i:32 * i + 1, :], consts["onescol_f"][32 * i:32 * i + 32, :],
                sqr[32 * i:32 * i + 32, :], start=False, stop=(i == 3),
                tile_position=(32 * i, 32 * i), skip_group_check=True),
                waits=[("ve", sq_h)] if i == 0 else (),
                inc="pe" if i == 3 else None)
        b.ve(lambda e: e.reduce_max(nr1[:], PNq[:], axis=AX.X), waits=[("pe", pn)])
        vq = b.ve(lambda e, ncol=ncol: e.tensor_tensor(
            Nacc[:, ncol:ncol + 1], Nacc[:, ncol:ncol + 1], nr1[:], OP.max))
        for v in range(2):
            pp = b.pe(lambda e, v=v: e.matmul(
                PX[:, 512 * v:512 * v + 512], P1m[v][:], rlo[:],
                start=True, stop=False), waits=[("ve", vq)] if v == 0 else (),
                inc=None)
            pp = b.pe(lambda e, v=v: e.matmul(
                PX[:, 512 * v:512 * v + 512], P2m[v][:], rhi[:],
                start=False, stop=True), inc="pe")
            qdst(v, pp)

    st_q = {"n": 0}

    def qdst_q(jp, w):
        def f(v, pp):
            u = 2 * jp + v
            vv = b.ve(lambda e, v=v: e.tensor_copy(q64[:], PX[:, 512 * v:512 * v + 512]),
                      waits=[("pe", pp)])
            b.st(lambda e, u=u, w=w: e.dma_start(
                out=qT1f[128 * u:128 * u + 128, 512 * w:512 * w + 512], in_=q64[:]),
                waits=[("ve", vv)], slot=st_q["n"] % 2)
            st_q["n"] += 1
            b.emit("ve", lambda e: e.nop(),
                   waits=[("st0", b.cnt["st0"]), ("st1", b.cnt["st1"])])
        return f

    def qdst_k(w):
        def f(v, pp):
            b.ve(lambda e, v=v, w=w: e.tensor_copy(
                kT64[v][:, 512 * w:512 * w + 512], PX[:, 512 * v:512 * v + 512]),
                waits=[("pe", pp)])
        return f

    for jp in range(4):
        for w in range(NW):
            rope_perm_block(jp, w, 128 * jp, 512 + 128 * jp, wq1_b,
                            qdst_q(jp, w), Qn, jp, 1024)
    for w in range(NW):
        rope_perm_block(0, w, 0, 128, wk1_b, qdst_k(w), Kn, 0, 256)
    vv = b.cnt["ve"]
    for t in range(NT):
        pe_v = None
        for ci in range(NC):
            pe_v = b.pe(lambda e, ci=ci, t=t: e.matmul(
                PL[:, 0:256], xnT1[ci][:, 1 + 128 * t:1 + 128 * t + 128],
                wv1_b[:, 256 * ci:256 * ci + 256],
                start=(ci == 0), stop=(ci == NC - 1)),
                waits=[("ve", vv)] if ci == 0 else (),
                inc="pe" if ci == NC - 1 else None)
        vv = b.ve(lambda e, t=t: e.tensor_copy(V1[:, 256 * t:256 * t + 256],
                                               PL[:, 0:256]), waits=[("pe", pe_v)])
    _build_shift_bias(b, esP, consts, Qn, Kn, nrm_dram, cb, "l1a", extra_scale=0.125)
    esP.close()

    aoT1 = [b.sb(f"l1_aoT{t}", [128, T], BF16, es_main, side="right") for t in range(8)]
    esA = b.scope()
    PS = b.ps("l1_PS", [128, 2048], F32, esA)
    PAV = b.ps("l1_PAV", [128, 512], F32, esA)
    PSMn = b.ps("l1_PSM", [128, 512], F32, esA)
    PB2 = b.ps("l1_PB2", [128, 512], F32, esA)
    PT = b.sb("l1_PT", [128, 2048], BF16, esA)
    qw64 = [b.sb(f"l1_qw64_{i}", [128, 512], BF16, esA) for i in range(2)]
    sums_sb = b.sb("l1_sums", [128, 512], F32, esA)
    b.ve(lambda e: e.memset(sums_sb[:], 0.0))
    pavc = b.sb("l1_pavc", [128, 512], F32, esA)
    bcast_sb = b.sb("l1_bc", [128, 512], F32, esA)
    recipb = b.sb("l1_recip", [128, 512], F32, esA)
    EL1 = []
    for nm in ("Ep1", "Ep2"):
        et = b.sb("cs2_" + nm, [128, 128], F32, esA)
        b.ld(lambda e, et=et, nm=nm: e.dma_start(out=et[:], in_=caps[nm][:]),
             waits=[("ve", b.cnt["ve"])], slot=7)
        EL1.append(et)
    b.emit("ve", lambda e: e.nop(), waits=[("ld7", b.cnt["ld7"])])

    wins = [(jp, v, w) for jp in range(4) for v in range(2) for w in range(NW)]
    ps_free = [b.cnt["act"], b.cnt["act"]]
    pt_free = [b.cnt["pe"], b.cnt["pe"]]
    qw_free = [b.cnt["pe"], b.cnt["pe"]]
    pav_ready = b.cnt["ve"]
    psm_ready = b.cnt["ve"]
    lq = {}

    def issue_lq(idx):
        if idx >= len(wins):
            return
        jp, v, w = wins[idx]
        qb_ = idx % 2
        u = 2 * jp + v
        lq[idx] = b.ld(lambda e, u=u, w=w, qb_=qb_: e.dma_start(
            out=qw64[qb_][:], in_=qT1f[128 * u:128 * u + 128, 512 * w:512 * w + 512]),
            waits=[("pe", qw_free[qb_])], slot=qb_)

    issue_lq(0)
    for idx, (jp, v, w) in enumerate(wins):
        qb_ = idx % 2
        mz = b.ve(lambda e: e.memset(PSMn[:], 0.0), waits=[("pe", min(pt_free)),
                                                           ("ve", psm_ready)])
        mz = b.ve(lambda e: e.memset(PAV[:], 0.0), waits=[("ve", pav_ready)])
        pending = None

        def mk_av(v, kt, bb, exp_t, last, mzw):
            def emit_av():
                r_ = None
                for ii in range(2):
                    g = 2 * v + ii
                    wl = list(mzw) if ii == 0 else []
                    if ii == 0:
                        wl.append(exp_t)
                    b.pe(lambda e, kt=kt, g=g, ii=ii, bb=bb, last=last: e.matmul(
                        PAV[64 * ii:64 * ii + 64, :],
                        V1[:, 256 * kt + 64 * g:256 * kt + 64 * g + 64],
                        PT[:, 1024 * bb + 512 * ii:1024 * bb + 512 * ii + 512],
                        start=(kt == 0 and ii == 0), stop=last,
                        tile_position=(0, 64 * ii), skip_group_check=True),
                        waits=wl)
                for ii in range(2):
                    g = 2 * v + ii
                    r_ = b.pe(lambda e, kt=kt, g=g, ii=ii, bb=bb, last=last: e.matmul(
                        PSMn[32 * g:32 * g + 1, :], consts["onescol_bf"][:],
                        PT[:, 1024 * bb + 512 * ii:1024 * bb + 512 * ii + 512],
                        start=(kt == 0 and ii == 0), stop=last,
                        tile_position=(0, 32 * g), skip_group_check=True),
                        inc="pe" if ii == 1 else None)
                pt_free[bb] = r_
            return emit_av

        for kt in range(NT):
            bb = kt % 2
            pe_s = None
            for ii in range(2):
                wl = [("act", ps_free[bb]), lq[idx]] if ii == 0 else []
                if ii == 0 and kt == 0:
                    wl.append(("ve", pav_ready))
                pe_s = b.pe(lambda e, kt=kt, v=v, ii=ii, bb=bb, qb_=qb_: e.matmul(
                    PS[:, 1024 * bb + 512 * ii:1024 * bb + 512 * ii + 512],
                    kT64[v][64 * ii:64 * ii + 64, 128 * kt:128 * kt + 128],
                    qw64[qb_][64 * ii:64 * ii + 64, :],
                    start=True, stop=True, tile_position=(64 * ii, 0)),
                    waits=wl, inc="pe" if ii == 1 else None)
            qw_free[qb_] = pe_s
            if kt == 2:
                issue_lq(idx + 1)
            exp_t = b.act(lambda e, bb=bb: e.activation(
                PT[:, 1024 * bb:1024 * bb + 1024],
                PS[:, 1024 * bb:1024 * bb + 1024], AF.Exp, bias=cb[:], scale=0.125),
                waits=[("pe", pe_s), ("pe", pt_free[bb])])
            ps_free[bb] = exp_t
            if pending is not None:
                pending()
            pending = mk_av(v, kt, bb, ("act", exp_t), kt == NT - 1,
                            [("ve", mz)] if kt == 0 else [])
        pending()
        pv = b.ve(lambda e: e.tensor_copy(pavc[:], PAV[:]),
                  waits=[("pe", pt_free[(NT - 1) % 2])])
        sv = None
        for ii in range(2):
            g = 2 * v + ii
            sv = b.ve(lambda e, g=g: e.tensor_copy(
                sums_sb[32 * g:32 * g + 1, :], PSMn[32 * g:32 * g + 1, :]))
        pav_ready = pv
        psm_ready = sv
        pb_ = b.pe(lambda e, v=v: e.matmul(PB2[:], EL1[v][:], sums_sb[:],
                                           start=True, stop=True),
                   waits=[("ve", sv)], inc="pe")
        b.ve(lambda e: e.tensor_copy(bcast_sb[:], PB2[:]), waits=[("pe", pb_)])
        b.ve(lambda e: e.reciprocal(recipb[:], bcast_sb[:]))
        b.ve(lambda e, jp=jp, v=v, w=w: e.tensor_tensor(
            aoT1[2 * jp + v][0:64, 512 * w:512 * w + 512], pavc[0:64, :],
            recipb[0:64, :], OP.mult))
        b.ve(lambda e, jp=jp, v=v, w=w: e.tensor_tensor(
            aoT1[2 * jp + v][64:128, 512 * w:512 * w + 512], pavc[64:128, :],
            recipb[64:128, :], OP.mult))
    esA.close()
    es_kv.close()

    esO = b.scope()
    stgF = [b.sb(f"l1_stage3{i}", [128, 1024], F32, esO) for i in range(4)]
    wo1_b = _cast_load(b, esO, "l1_wo", caps["wo1"], 1024, 1024, None,
                       stages=stgF, slots=[4, 5, 6, 7])
    PO = b.ps("l1_PO", [128, 1024], F32, esO)
    PTf = [b.ps(f"l1_PTf{i}", [128, 128], F32, esO) for i in range(2)]
    of32 = [b.sb(f"l1_of32_{i}", [128, 512], F32, esO) for i in range(2)]
    obuf = b.sb("l1_obuf", [128, 4096], F32, esO)

    ptf_evac = [b.cnt["ve"], b.cnt["ve"]]
    po_free = [b.cnt["ve"], b.cnt["ve"]]
    for w in range(NW):
        for do in range(NC):
            bb = do % 2
            pe_o = None
            for t in range(8):
                pe_o = b.pe(lambda e, w=w, do=do, t=t, bb=bb: e.matmul(
                    PO[:, 512 * bb:512 * bb + 512],
                    wo1_b[:, 1024 * t + 128 * do:1024 * t + 128 * do + 128],
                    aoT1[t][:, 512 * w:512 * w + 512],
                    start=(t == 0), stop=(t == 7)),
                    waits=[("ve", po_free[bb])] if t == 0 else (),
                    inc="pe" if t == 7 else None)
            po_free[bb] = b.ve(lambda e, bb=bb: e.tensor_copy(
                of32[bb][:], PO[:, 512 * bb:512 * bb + 512]), waits=[("pe", pe_o)])
            for tt in range(4):
                pb = tt % 2
                tp = b.pe(lambda e, tt=tt, pb=pb, bb=bb: e.transpose(
                    PTf[pb][:], of32[bb][:, 128 * tt:128 * tt + 128],
                    consts["identf"][:]),
                    waits=[("ve", po_free[bb]), ("ve", ptf_evac[pb])], inc="pe")
                ptf_evac[pb] = b.ve(lambda e, tt=tt, do=do, pb=pb: e.tensor_copy(
                    obuf[:, 1024 * tt + 128 * do:1024 * tt + 128 * do + 128],
                    PTf[pb][:]), waits=[("pe", tp)])
        for tt in range(4):
            b.st(lambda e, w=w, tt=tt: e.dma_start(
                out=a1_dram[512 * w + 128 * tt:512 * w + 128 * tt + 128, :],
                in_=obuf[:, 1024 * tt:1024 * tt + 1024]),
                waits=[("ve", max(ptf_evac))], slot=tt)
        nv = b.emit("ve", lambda e: e.nop(),
                    waits=[(f"st{i}", b.cnt[f"st{i}"]) for i in range(4)])
        ptf_evac = [nv, nv]
    esO.close()
    es_main.close()
    return aoT1


def emit_l1ffn_final(b, consts, caps, xnT1, x1_dram, a1_dram, out_dram):
    es = b.scope()
    mur = b.sb("g_mur", [128, 8], F32, es)
    muk = b.sb("g_muk", [128, 8], F32, es)
    l1_ = b.ld(lambda e: e.dma_start(out=mur[:], in_=caps["mur"][:]),
               waits=[("ve", b.cnt["ve"])], slot=0)
    l2_ = b.ld(lambda e: e.dma_start(out=muk[:], in_=caps["muk"][:]),
               waits=[("ve", b.cnt["ve"])], slot=1)
    dbuf = b.sb("g_d", [128, 512], F32, es)
    xrb = b.sb("g_xrb", [128, 4096], BF16, es)
    xkb = b.sb("g_xkb", [128, 4096], BF16, es)
    tsb = b.sb("g_t", [128, 4096], F32, es)
    kab = b.sb("g_ka", [128, 4096], BF16, es)
    kaf = b.sb("g_kaf", [128, 512], F32, es)
    of32 = b.sb("g_of32", [128, 512], F32, es)
    obuf = b.sb("g_obuf", [128, 4096], F32, es)
    x1s = [b.sb(f"g_x1{i}", [128, D], F32, es) for i in range(2)]
    a1s = [b.sb(f"g_a1{i}", [128, D], F32, es) for i in range(2)]
    outbs = [b.sb(f"g_outb{i}", [128, D], F32, es) for i in range(2)]
    PR = b.ps("g_PR", [128, 512], F32, es)
    PK = b.ps("g_PK", [128, 512], F32, es)
    PV = b.ps("g_PV", [128, 512], F32, es)
    PTf = [b.ps(f"g_PTf{i}", [128, 128], F32, es) for i in range(2)]

    castc = [b.cnt["ve"]] * 4
    wstages = [x1s[0], x1s[1], a1s[0], a1s[1]]

    def wload(name, cap, scale=None):
        out = b.sb(name + "_b", [128, 8192], BF16, es)
        for c in range(8):
            sbf = c % 4
            ld = b.ld(lambda e, c=c, sbf=sbf, cap=cap: e.dma_start(
                out=wstages[sbf][:], in_=cap[128 * c:128 * c + 128, :]),
                waits=[("ve", castc[sbf])], slot=4 + sbf)
            if scale is None:
                castc[sbf] = b.ve(lambda e, c=c, sbf=sbf, out=out: e.tensor_copy(
                    out[:, 1024 * c:1024 * c + 1024], wstages[sbf][:]), waits=[ld])
            else:
                castc[sbf] = b.ve(lambda e, c=c, sbf=sbf, out=out: e.tensor_scalar(
                    out[:, 1024 * c:1024 * c + 1024], wstages[sbf][:], scale,
                    None, OP.mult), waits=[ld])
        return out

    mu_w = b.ve(lambda e: e.nop(), waits=[l1_, l2_])
    ptf_evac = [b.cnt["ve"], b.cnt["ve"]]
    outg = {0: ("st0", 0), 1: ("st1", 0)}
    stg = [0, 0, 0, 0]
    pload = {}
    vmix = 0
    kafmax = 0
    kafmult = 0
    sttc = 0

    def mix_chunk(w, c, pe_guard):
        wl = []
        if c == 0:
            wl = [("pe", pe_guard)]
            if w == 0:
                wl.append(("ve", mu_w))
        b.ve(lambda e, c=c, w=w: e.tensor_tensor(
            dbuf[:], xnT1[c][:, 1 + 512 * w:1 + 512 * w + 512],
            xnT1[c][:, 512 * w:512 * w + 512], OP.subtract),
            waits=wl)
        b.ve(lambda e, c=c, w=w: e.scalar_tensor_tensor(
            xrb[:, 512 * c:512 * c + 512], dbuf[:], mur[:, c:c + 1],
            xnT1[c][:, 512 * w:512 * w + 512], OP.mult, OP.add))
        return b.ve(lambda e, c=c, w=w: e.scalar_tensor_tensor(
            xkb[:, 512 * c:512 * c + 512], dbuf[:], muk[:, c:c + 1],
            xnT1[c][:, 512 * w:512 * w + 512], OP.mult, OP.add))

    for c in range(NC):
        vmix = mix_chunk(0, c, b.cnt["pe"])
    wrT_b = wload("g_wr", caps["wrT"])
    wrT_fin = max(castc)
    wkT_b = wload("g_wk", caps["wkT1"])
    wkT_fin = max(castc)
    wvT_b = wload("g_wv", caps["wvT1"], scale=0.5)
    lastx = {0: castc[0], 1: castc[1]}
    lasta = {0: castc[2], 1: castc[3]}

    def prefetch(wa, tt):
        t = 4 * wa + tt
        buf = tt % 2
        lx = b.ld(lambda e, t=t, buf=buf: e.dma_start(
            out=x1s[buf][:], in_=x1_dram[128 * t:128 * t + 128, :]),
            waits=[("ve", lastx[buf])], slot=buf)
        la = b.ld(lambda e, t=t, buf=buf: e.dma_start(
            out=a1s[buf][:], in_=a1_dram[128 * t:128 * t + 128, :]),
            waits=[("ve", lasta[buf])], slot=2 + buf)
        pload[(wa, tt)] = (lx, la)

    def emit_assemble(wa):
        for tt in range(4):
            t = 4 * wa + tt
            buf = tt % 2
            lx, la = pload.pop((wa, tt))
            a1c = b.ve(lambda e, tt=tt, buf=buf: e.tensor_tensor(
                outbs[buf][:], obuf[:, 1024 * tt:1024 * tt + 1024], x1s[buf][:],
                OP.add), waits=[lx, outg[buf]])
            lastx[buf] = a1c
            a2c = b.ve(lambda e, buf=buf: e.tensor_tensor(
                outbs[buf][:], outbs[buf][:], a1s[buf][:], OP.add), waits=[la])
            lasta[buf] = a2c
            sret = b.st(lambda e, t=t, buf=buf: e.dma_start(
                out=out_dram[128 * t:128 * t + 128, :], in_=outbs[buf][:]),
                waits=[("ve", a2c), (f"st{tt}", stg[tt])], slot=tt)
            stg[tt] = sret[1]
            outg[buf] = sret
            if tt < 2:
                prefetch(wa, tt + 2)

    for w in range(NW):
        if w == 0:
            rmix, kmix = wrT_fin, wkT_fin
        else:
            emit_assemble(w - 1)
            rmix = kmix = vmix
        for do in range(NC):
            pe_r = None
            for ci in range(NC):
                pe_r = b.pe(lambda e, ci=ci, do=do: e.matmul(
                    PR[:], wrT_b[:, 1024 * ci + 128 * do:1024 * ci + 128 * do + 128],
                    xrb[:, 512 * ci:512 * ci + 512],
                    start=(ci == 0), stop=(ci == NC - 1)),
                    waits=[("ve", rmix), ("act", b.cnt["act"])] if ci == 0 else (),
                    inc="pe" if ci == NC - 1 else None)
            b.act(lambda e, do=do: e.activation(tsb[:, 512 * do:512 * do + 512], PR[:],
                                                AF.Tanh, bias=0.0, scale=0.5),
                  waits=[("pe", pe_r)])
            kw = kmix if do == 0 else kafmax
            pe_k = None
            for ci in range(NC):
                pe_k = b.pe(lambda e, ci=ci, do=do: e.matmul(
                    PK[:], wkT_b[:, 1024 * ci + 128 * do:1024 * ci + 128 * do + 128],
                    xkb[:, 512 * ci:512 * ci + 512],
                    start=(ci == 0), stop=(ci == NC - 1)),
                    waits=[("ve", kw)] if ci == 0 else (),
                    inc="pe" if ci == NC - 1 else None)
            kafmax = b.ve(lambda e: e.tensor_scalar(kaf[:], PK[:], 0.0, None, OP.max),
                          waits=[("pe", pe_k)])
            kafmult = b.ve(lambda e, do=do: e.tensor_tensor(
                kab[:, 512 * do:512 * do + 512], kaf[:], kaf[:], OP.mult))
        aw = b.cnt["act"]
        pe_guard_mix = b.cnt["pe"]
        for do in range(NC):
            pvw = kafmult if do == 0 else sttc
            pe_v = None
            for ci in range(NC):
                pe_v = b.pe(lambda e, ci=ci, do=do: e.matmul(
                    PV[:], wvT_b[:, 1024 * ci + 128 * do:1024 * ci + 128 * do + 128],
                    kab[:, 512 * ci:512 * ci + 512],
                    start=(ci == 0), stop=(ci == NC - 1)),
                    waits=[("ve", pvw)] if ci == 0 else (),
                    inc="pe" if ci == NC - 1 else None)
            sttc = b.ve(lambda e, do=do: e.scalar_tensor_tensor(
                of32[:], tsb[:, 512 * do:512 * do + 512], 1.0, PV[:],
                OP.add, OP.mult), waits=[("pe", pe_v), ("act", aw)])
            for tt in range(4):
                pb = tt % 2
                tp = b.pe(lambda e, tt=tt, pb=pb: e.transpose(
                    PTf[pb][:], of32[:, 128 * tt:128 * tt + 128], consts["identf"][:]),
                    waits=[("ve", sttc), ("ve", ptf_evac[pb])], inc="pe")
                ptf_evac[pb] = b.ve(lambda e, tt=tt, do=do, pb=pb: e.tensor_copy(
                    obuf[:, 1024 * tt + 128 * do:1024 * tt + 128 * do + 128],
                    PTf[pb][:]), waits=[("pe", tp)])
            if w + 1 < NW:
                vmix = mix_chunk(w + 1, do, pe_guard_mix)
            if do in (4, 6):
                prefetch(w, (do - 4) // 2)
    emit_assemble(NW - 1)
    es.close()



def _consts_host():
    c = {}
    c["ident"] = np.eye(128, dtype=BF)
    c["identf"] = np.eye(128, dtype=np.float32)
    ones = np.ones((128, 1), dtype=BF)
    c["onescol_bf"] = ones
    c["onescol_f"] = np.ones((128, 1), dtype=np.float32)
    c["onesrow_f"] = np.ones((1, 128), dtype=np.float32)
    for d in range(4):
        p = np.arange(128)[:, None]
        f = np.arange(512)[None, :]
        c[f"mask{d}"] = ((128 * d + p) <= f).astype(BF)
        c[f"mask2_{d}"] = np.concatenate([c[f"mask{d}"], c[f"mask{d}"]], axis=1)
    epair = np.zeros((128, 128), dtype=np.float32)
    epair[32, 0:32] = 1.0
    epair[96, 64:96] = 1.0
    c["Epair"] = epair
    e4 = np.zeros((128, 128), dtype=np.float32)
    for m in range(128):
        e4[32 * (m // 32), m] = 1.0
    c["E4s"] = e4
    er0 = np.zeros((128, 128), dtype=np.float32)
    er0[0, :] = 1.0
    c["Erow0"] = er0
    ep1 = np.zeros((128, 128), dtype=np.float32)
    ep1[0, 0:64] = 1.0
    ep1[32, 64:128] = 1.0
    c["Ep1"] = ep1
    ep2 = np.zeros((128, 128), dtype=np.float32)
    ep2[64, 0:64] = 1.0
    ep2[96, 64:128] = 1.0
    c["Ep2"] = ep2
    for v in range(2):
        p1 = np.zeros((128, 128), dtype=BF)
        p2 = np.zeros((128, 128), dtype=BF)
        for d in range(32):
            p1[64 * v + d, d] = 1
            p2[64 * v + d, 32 + d] = 1
            p1[64 * v + 32 + d, 64 + d] = 1
            p2[64 * v + 32 + d, 96 + d] = 1
        c[f"P1_{v}"] = p1
        c[f"P2_{v}"] = p2
    half = 32
    inv = 1.0 / (10000.0 ** (np.arange(half, dtype=np.float64) * 2.0 / 64.0))
    pos = np.arange(T, dtype=np.float64)
    fr = pos[:, None] * inv[None, :]
    cos = np.cos(fr).astype(np.float32).T
    sin = np.sin(fr).astype(np.float32).T
    c["cos4"] = np.tile(cos, (4, 1))
    c["sin4"] = np.tile(sin, (4, 1))
    return c


def _declare_consts(nc, host):
    aps = {}
    for k, v in host.items():
        dt = BF16 if v.dtype == BF else F32
        aps[k] = nc.dram_tensor("c_" + k, list(v.shape), dt, kind="ExternalInput").ap()
    return aps


def load_const(b, caps, name, shape, dt):
    t = b.sb("cs_" + name, shape, dt)
    ld = b.ld(lambda e: e.dma_start(out=t[:], in_=caps[name][:]), slot=7)
    return t, ld


def build_program():
    nc = bass.Bass("TRN2", target_bir_lowering=False)
    x = nc.dram_tensor("x", [T, D], F32, kind="ExternalInput").ap()
    host_consts = _consts_host()
    caps = _declare_consts(nc, host_consts)
    caps["wq"] = nc.dram_tensor("wq", [D, D], F32, kind="ExternalInput").ap()
    caps["wkv"] = nc.dram_tensor("wkv", [D, 128], F32, kind="ExternalInput").ap()
    caps["wk"] = nc.dram_tensor("wk", [128, 512], F32, kind="ExternalInput").ap()
    caps["wv"] = nc.dram_tensor("wv", [128, 512], F32, kind="ExternalInput").ap()
    caps["wosel"] = nc.dram_tensor("wosel", [D, D], F32, kind="ExternalInput").ap()
    caps["qgv"] = nc.dram_tensor("qgv", [1, 64], F32, kind="ExternalInput").ap()
    caps["kgv"] = nc.dram_tensor("kgv", [1, 32], F32, kind="ExternalInput").ap()
    for nm in ("qwT", "kwT", "vwT", "owT"):
        caps[nm] = nc.dram_tensor(nm, [D, D], F32, kind="ExternalInput").ap()
    for nm in ("qb", "kb", "ob"):
        caps[nm] = nc.dram_tensor(nm, [128, 8], F32, kind="ExternalInput").ap()
    caps["vb"] = nc.dram_tensor("vb", [1, D], F32, kind="ExternalInput").ap()
    caps["nw"] = nc.dram_tensor("nw", [1, D], F32, kind="ExternalInput").ap()
    caps["nb"] = nc.dram_tensor("nb", [1, D], F32, kind="ExternalInput").ap()
    caps["wq1"] = nc.dram_tensor("wq1", [D, D], F32, kind="ExternalInput").ap()
    caps["wk1"] = nc.dram_tensor("wk1", [D, 256], F32, kind="ExternalInput").ap()
    caps["wv1"] = nc.dram_tensor("wv1", [D, 256], F32, kind="ExternalInput").ap()
    caps["wo1"] = nc.dram_tensor("wo1", [D, D], F32, kind="ExternalInput").ap()
    for nm in ("wrT", "wkT1", "wvT1"):
        caps[nm] = nc.dram_tensor(nm, [D, D], F32, kind="ExternalInput").ap()
    caps["mur"] = nc.dram_tensor("mur", [128, 8], F32, kind="ExternalInput").ap()
    caps["muk"] = nc.dram_tensor("muk", [128, 8], F32, kind="ExternalInput").ap()
    nrm_dram = nc.dram_tensor("nrm", [1, 4096], F32, kind="Internal").ap()
    qT0f = nc.dram_tensor("qT0f", [D, T], BF16, kind="Internal").ap()
    kT0f = nc.dram_tensor("kT0f", [D, T], BF16, kind="Internal").ap()
    qT1f = nc.dram_tensor("qT1f", [D, T], BF16, kind="Internal").ap()
    aoT0f = nc.dram_tensor("aoT0f", [D, T], BF16, kind="Internal").ap()
    upto = _DBG["upto"] or "Z"
    kd = "ExternalOutput" if _DBG["dump"] else "Internal"
    a0_dram = nc.dram_tensor("a0", [T, D], F32, kind=kd).ap()
    f0_dram = nc.dram_tensor("f0", [T, D], F32, kind=kd).ap()
    x1_dram = nc.dram_tensor("x1", [T, D], F32, kind=kd).ap()
    a1_dram = nc.dram_tensor("a1", [T, D], F32, kind=kd).ap()
    out_dram = nc.dram_tensor("out", [T, D], F32, kind="ExternalOutput").ap()

    dbg = {}
    if _DBG["dump"]:
        dbg["xnT"] = nc.dram_tensor("dbg_xnT", [NC * 128, T], BF16, kind="ExternalOutput").ap()
        dbg["stats"] = nc.dram_tensor("dbg_stats", [128, 64], F32, kind="ExternalOutput").ap()

    b = Bld(nc)
    with b.ctx:
        consts = {}
        for nm in ("ident", "identf", "onescol_bf", "onescol_f", "onesrow_f"):
            consts[nm], _ = load_const(b, caps, nm, list(_chost_shape(host_consts[nm])),
                                       BF16 if host_consts[nm].dtype == BF else F32)
        consts["eps5"] = b.fconst(1e-5)
        consts["eps6"] = b.fconst(1e-6)
        b.barrier()

        stats0 = b.sb("stats0", [128, 64], F32)
        es_xnT = b.scope()
        xnT_all = b.sb("xnT_all", [128, NC * T], BF16, es_xnT, side="right")
        xnT = [xnT_all[:, c * T:(c + 1) * T] for c in range(NC)]
        emit_ln_xnT(b, consts, x, stats0, xnT, ext=False, xnT_all=xnT_all)

        if _DBG["dump"]:
            for c in range(NC):
                b.st(lambda e, c=c: e.dma_start(out=dbg["xnT"][128 * c:128 * c + 128, :],
                                                in_=xnT[c][:]),
                     waits=[("ve", b.cnt["ve"])], slot=c % 4)
            b.st(lambda e: e.dma_start(out=dbg["stats"][:], in_=stats0[:]),
                 waits=[("ve", b.cnt["ve"])], slot=3)
            b.barrier()

        if upto.startswith("B") or upto >= "C":
            emit_l0a(b, consts, caps, xnT, es_xnT, a0_dram, nrm_dram)
        else:
            es_xnT.close()

        if upto >= "C" and not upto.startswith("B"):
            emit_l0ffn(b, consts, caps, x, stats0, f0_dram, qT0f, kT0f, aoT0f)

        if upto >= "D" and not upto.startswith("B"):
            stats1 = b.sb("stats1", [128, 64], F32)
            es_xnT1 = b.scope()
            xnT1_all = b.sb("xnT1_all", [128, NC * (T + 1)], BF16, es_xnT1,
                            side="right")
            xnT1 = [xnT1_all[:, c * (T + 1):(c + 1) * (T + 1)] for c in range(NC)]
            emit_assembly_ln(b, consts, x, a0_dram, f0_dram, stats0, True,
                             x1_dram, stats1, xnT1, xnT1_all)
        if upto >= "F" and not upto.startswith("B"):
            emit_l1a(b, consts, caps, xnT1, qT1f, a1_dram, nrm_dram)
        if upto >= "G" and not upto.startswith("B"):
            emit_l1ffn_final(b, consts, caps, xnT1, x1_dram, a1_dram, out_dram)
        if upto >= "D" and not upto.startswith("B"):
            es_xnT1.close()

        b.emit("gp", lambda e: e.nop(),
               waits=[(f"st{i}", b.cnt[f"st{i}"]) for i in range(4)])
        b.validate_counts()
        with nc.Block() as blk:
            b.replay(blk)
    return nc, host_consts


def _chost_shape(v):
    return v.shape


def _host_inputs(inputs, core):
    m = {}
    assert inputs["L0_ffn_router_prob"].shape[2] == 4
    m["x"] = np.ascontiguousarray(inputs["L0_attn_x"][core], dtype=np.float32)
    m["wq"] = inputs["L0_attn_Wq"]
    m["wkv"] = inputs["L0_attn_Wkv"]
    m["wk"] = inputs["L0_attn_Wk"]
    m["wv"] = inputs["L0_attn_Wv"]
    wo = inputs["L0_attn_Wo"]
    sel = np.zeros((D, D), dtype=np.float32)
    for j in range(4):
        for p in range(2):
            s_ = 2 * j + p
            h0, h1 = 4 * j + 2 * p, 4 * j + 2 * p + 1
            sel[128 * s_:128 * s_ + 32, :] = wo[64 * h0:64 * h0 + 32, :]
            sel[128 * s_ + 64:128 * s_ + 96, :] = wo[64 * h1:64 * h1 + 32, :]
    m["wosel"] = sel
    m["qgv"] = np.ascontiguousarray(inputs["L0_attn_q_g"].reshape(1, 64), dtype=np.float32)
    m["kgv"] = np.ascontiguousarray(inputs["L0_attn_k_g"].reshape(1, 32), dtype=np.float32)
    m["qwT"] = np.ascontiguousarray(inputs["L0_ffn_q_w"].T)
    m["kwT"] = np.ascontiguousarray(inputs["L0_ffn_k_w"].T)
    m["vwT"] = np.ascontiguousarray(inputs["L0_ffn_v_w"].T)
    m["owT"] = np.ascontiguousarray(inputs["L0_ffn_o_w"].T)
    for nm, key in (("qb", "L0_ffn_q_b"), ("kb", "L0_ffn_k_b"), ("ob", "L0_ffn_o_b")):
        m[nm] = np.ascontiguousarray(inputs[key].reshape(8, 128).T, dtype=np.float32)
    m["vb"] = np.ascontiguousarray(inputs["L0_ffn_v_b"].reshape(1, D), dtype=np.float32)
    m["nw"] = np.ascontiguousarray(inputs["L0_ffn_norm_w"].reshape(1, D), dtype=np.float32)
    m["nb"] = np.ascontiguousarray(inputs["L0_ffn_norm_b"].reshape(1, D), dtype=np.float32)
    perm_q = []
    for half in range(2):
        for jp in range(4):
            for i in range(4):
                h = 4 * i + jp
                perm_q.extend(range(64 * h + 32 * half, 64 * h + 32 * half + 32))
    m["wq1"] = np.ascontiguousarray(inputs["L1_attn_w_q"][:, perm_q])
    perm_k = []
    for half in range(2):
        for g in range(4):
            perm_k.extend(range(64 * g + 32 * half, 64 * g + 32 * half + 32))
    m["wk1"] = np.ascontiguousarray(inputs["L1_attn_w_k"][:, perm_k])
    m["wv1"] = inputs["L1_attn_w_v"]
    perm_o = []
    for u in range(8):
        jp, v = u // 2, u % 2
        for h in (8 * v + jp, 8 * v + 4 + jp):
            perm_o.extend(range(64 * h, 64 * h + 64))
    m["wo1"] = np.ascontiguousarray(inputs["L1_attn_w_o"][perm_o, :])
    m["wrT"] = np.ascontiguousarray(inputs["L1_ffn_W_r"].T)
    m["wkT1"] = np.ascontiguousarray(inputs["L1_ffn_W_k"].T)
    m["wvT1"] = np.ascontiguousarray(inputs["L1_ffn_W_v"].T)
    m["mur"] = np.ascontiguousarray(inputs["L1_ffn_mu_r"].reshape(8, 128).T,
                                    dtype=np.float32)
    m["muk"] = np.ascontiguousarray(inputs["L1_ffn_mu_k"].reshape(8, 128).T,
                                    dtype=np.float32)
    return m


def kernel(**inputs):
    nc, host_consts = build_program()
    in_maps = []
    for core in range(8):
        m = _host_inputs(inputs, core)
        for k, v in host_consts.items():
            m["c_" + k] = v
        in_maps.append(m)
    res = run_bass_kernel_spmd(nc, in_maps, core_ids=list(range(8)))
    out = np.stack([r["out"] for r in res.results], axis=0)
    return out
